# version 18
# baseline (speedup 1.0000x reference)
import numpy as np
import ml_dtypes
from contextlib import ExitStack

import concourse.bass as bass
import concourse.mybir as mybir
from concourse.bass_utils import run_bass_kernel_spmd

F32 = mybir.dt.float32
BF16 = mybir.dt.bfloat16
AF = mybir.ActivationFunctionType
ALU = mybir.AluOpType
AX = mybir.AxisListType

S = 2048
D = 1024
NH = 16
HD = 64
DH = 4096
NCORES = 8
H_SLAB_PASSES = (7, 7, 7, 7, 4)


class Sem:
    def __init__(self, es, nc, name):
        self.h = es.enter_context(nc.semaphore(name))
        self.n = 0

    def inc(self, instr, k=1):
        instr.then_inc(self.h, k)
        self.n += k
        return self.n


def build_program(n_seq=4, phases=("prep", "ln", "qkv", "attn", "proj", "w12", "h1", "h2", "w3"),
                  debug=()):
    T = n_seq * S
    TT = T // 128
    nc = bass.Bass("TRN2", target_bir_lowering=False, debug=False)

    x_h = nc.dram_tensor("x", [n_seq, S, D], F32, kind="ExternalInput").ap()
    wqkv_h = nc.dram_tensor("wqkv_t", [24, D, 128], F32, kind="ExternalInput").ap()
    bqQ_h = nc.dram_tensor("bq_q", [128, 8], F32, kind="ExternalInput").ap()
    bqK_h = nc.dram_tensor("bq_k", [128, 8], F32, kind="ExternalInput").ap()
    bqV_h = nc.dram_tensor("bq_v", [1, D], F32, kind="ExternalInput").ap()
    wout_h = nc.dram_tensor("wout", [D, D], F32, kind="ExternalInput").ap()
    bout_h = nc.dram_tensor("bout", [1, D], F32, kind="ExternalInput").ap()
    w1t_h = nc.dram_tensor("w1t", [D, DH], F32, kind="ExternalInput").ap()
    w2t_h = nc.dram_tensor("w2t", [D, DH], F32, kind="ExternalInput").ap()
    w1s_h = nc.dram_tensor("w1s_t", [128, 32], F32, kind="ExternalInput").ap()
    w2s_h = nc.dram_tensor("w2s_t", [128, 32], F32, kind="ExternalInput").ap()
    h1t_h = nc.dram_tensor("h1t", [32, DH, 128], F32, kind="ExternalInput").ap()
    h2t_h = nc.dram_tensor("h2t", [32, DH, 128], F32, kind="ExternalInput").ap()
    h1s_h = nc.dram_tensor("h1s_t", [128, 32], F32, kind="ExternalInput").ap()
    h2s_h = nc.dram_tensor("h2s_t", [128, 32], F32, kind="ExternalInput").ap()
    w3t_h = nc.dram_tensor("w3t", [DH, D], F32, kind="ExternalInput").ap()
    w3s_h = nc.dram_tensor("w3s_r", [1, D], F32, kind="ExternalInput").ap()
    maskT_h = nc.dram_tensor("maskT", [128, 128], F32, kind="ExternalInput").ap()
    negones_h = nc.dram_tensor("negones", [1, 128], F32, kind="ExternalInput").ap()
    ones64_h = nc.dram_tensor("ones64", [1, 64], F32, kind="ExternalInput").ap()
    sel2_h = nc.dram_tensor("sel2", [65, 128], F32, kind="ExternalInput").ap()
    ind8_h = nc.dram_tensor("ind8", [128, 128], BF16, kind="ExternalInput").ap()

    y_h = nc.dram_tensor("y", [n_seq, S, D], F32, kind="ExternalOutput").ap()

    xn_sc = nc.dram_tensor("xn_sc", [8, 128, T], BF16, kind="Internal").ap()
    xn_tm_sc = nc.dram_tensor("xn_tm_sc", [n_seq, S, D], BF16, kind="Internal").ap()
    a_sc = nc.dram_tensor("a_sc", [32, 128, T], BF16, kind="Internal").ap()
    g1_sc = nc.dram_tensor("g1_sc", [32, 128, T], BF16, kind="Internal").ap()
    g2_sc = nc.dram_tensor("g2_sc", [TT, 32, 128, 128], BF16, kind="Internal").ap()
    attn_sc = nc.dram_tensor("attn_sc", [n_seq, S, D], F32, kind="Internal").ap()
    csc_h = nc.dram_tensor("csc", [n_seq, 16], F32, kind="Internal").ap()

    dbg = {}
    if "xn" in debug:
        dbg["xn"] = nc.dram_tensor("dbg_xn", [8, 128, T], BF16, kind="ExternalOutput").ap()
    if "qkv" in debug:
        dbg["q"] = nc.dram_tensor("dbg_q", [8, 128, S], BF16, kind="ExternalOutput").ap()
        dbg["k"] = nc.dram_tensor("dbg_k", [8, 128, S], BF16, kind="ExternalOutput").ap()
        dbg["v"] = nc.dram_tensor("dbg_v", [16, 128, NH * 65], BF16, kind="ExternalOutput").ap()
        dbg["ssq"] = nc.dram_tensor("dbg_ssq", [16, 2 * S], F32, kind="ExternalOutput").ap()
    if "vals" in debug:
        dbg["vals"] = nc.dram_tensor("dbg_vals", [8, 128, S], BF16, kind="ExternalOutput").ap()
    if "attn" in debug:
        dbg["attn"] = nc.dram_tensor("dbg_attn", [n_seq, S, D], F32, kind="ExternalOutput").ap()
    if "a" in debug:
        dbg["a"] = nc.dram_tensor("dbg_a", [32, 128, T], BF16, kind="ExternalOutput").ap()
    if "g2" in debug:
        dbg["g2"] = nc.dram_tensor("dbg_g2", [TT, 32, 128, 128], BF16, kind="ExternalOutput").ap()

    es = ExitStack()
    with es:
        maskT = es.enter_context(nc.sbuf_tensor("maskT_s", [128, 128], F32))
        negones = es.enter_context(nc.sbuf_tensor("negones_s", [1, 128], F32))
        ones64 = es.enter_context(nc.sbuf_tensor("ones64_s", [1, 64], F32))
        sel2 = es.enter_context(nc.sbuf_tensor("sel2_s", [65, 128], F32))
        ind8 = es.enter_context(nc.sbuf_tensor("ind8_s", [128, 128], BF16))
        bq8 = es.enter_context(nc.sbuf_tensor("bq8", [128, 16], F32))
        bv_bc = es.enter_context(nc.sbuf_tensor("bv_bc", [128, D], F32))
        bout_bc = es.enter_context(nc.sbuf_tensor("bout_bc", [128, D], F32))
        w3s_bc = es.enter_context(nc.sbuf_tensor("w3s_bc", [128, D], F32))
        scl4 = es.enter_context(nc.sbuf_tensor("scl4", [128, 128], F32))
        epsv = es.enter_context(nc.sbuf_tensor("epsv", [128, 1], F32))

        s_cast = Sem(es, nc, "s_cast")
        s_mm = Sem(es, nc, "s_mm")
        s_ev = Sem(es, nc, "s_ev")
        s_act = Sem(es, nc, "s_act")
        s_dve = Sem(es, nc, "s_dve")
        s_pool = Sem(es, nc, "s_pool")
        s_prep = Sem(es, nc, "s_prep")
        s_xs = [Sem(es, nc, "s_x0"), Sem(es, nc, "s_x1")]
        s_ws = [Sem(es, nc, "s_w0"), Sem(es, nc, "s_w1")]
        s_sts = [Sem(es, nc, "s_st0"), Sem(es, nc, "s_st1"),
                 Sem(es, nc, "s_st2"), Sem(es, nc, "s_st3")]
        s_xts = [Sem(es, nc, "s_xt0"), Sem(es, nc, "s_xt1"),
                 Sem(es, nc, "s_xt2"), Sem(es, nc, "s_xt3")]
        s_x4 = s_xs + [Sem(es, nc, "s_x2"), Sem(es, nc, "s_x3")]
        s_xtp = [Sem(es, nc, "s_xtp0"), Sem(es, nc, "s_xtp1"),
                 Sem(es, nc, "s_xtp2"), Sem(es, nc, "s_xtp3")]
        s_crs = [Sem(es, nc, "s_cr0"), Sem(es, nc, "s_cr1")]
        s_ys = [Sem(es, nc, "s_y0"), Sem(es, nc, "s_y1")]

        def wait_stores_all(eng):
            for q in s_sts:
                eng.wait_ge(q.h, q.n)

        if "prep" in phases:
            with nc.Block() as blk, (
                nc.sbuf_tensor("prep_row", [1, 3 * D], F32)) as prow, (
                nc.psum_tensor("prep_ps", [128, 512], F32)) as pps:

                @blk.gpsimd
                def _(gpsimd):
                    gpsimd.memset(epsv[:, :], 1e-5)

                @blk.sync
                def _(sync):
                    for dst, src in [
                        (maskT[:, :], maskT_h[:, :]), (negones[:, :], negones_h[:, :]),
                        (ones64[:, :], ones64_h[:, :]), (ind8[:, :], ind8_h[:, :]),
                        (sel2[:, :], sel2_h[:, :]),
                        (bq8[:, 0:8], bqQ_h[:, :]), (bq8[:, 8:16], bqK_h[:, :]),
                        (scl4[:, 0:32], w1s_h[:, :]), (scl4[:, 32:64], w2s_h[:, :]),
                        (scl4[:, 64:96], h1s_h[:, :]), (scl4[:, 96:128], h2s_h[:, :]),
                        (prow[:, 0:D], bqV_h[:, :]), (prow[:, D:2 * D], bout_h[:, :]),
                        (prow[:, 2 * D:3 * D], w3s_h[:, :]),
                    ]:
                        s_prep.inc(sync.dma_start(out=dst, in_=src), 16)
                ld_all = s_prep.n

                @blk.scalar
                def _(scalar):
                    scalar.wait_ge(s_prep.h, ld_all)
                    s_act.inc(scalar.activation(out=bq8[:, 0:8], in_=bq8[:, 0:8],
                                                func=AF.Copy, bias=0.0, scale=0.125))

                bc_ev = []
                for j, dst in enumerate([bv_bc, bout_bc, w3s_bc]):
                    for half in range(2):
                        @blk.tensor
                        def _(tensor, j=j, half=half):
                            tensor.wait_ge(s_prep.h, ld_all)
                            if bc_ev:
                                tensor.wait_ge(s_dve.h, bc_ev[-1])
                            s_mm.inc(tensor.matmul(
                                pps[:, 0:512], negones[:, :],
                                prow[:, j * D + half * 512:j * D + (half + 1) * 512],
                                start=True, stop=True))
                        mmv = s_mm.n

                        @blk.vector
                        def _(vector, dst=dst, half=half, mmv=mmv):
                            vector.wait_ge(s_mm.h, mmv)
                            s_dve.inc(vector.tensor_scalar(
                                out=dst[:, half * 512:(half + 1) * 512],
                                in0=pps[:, 0:512], scalar1=-1.0, scalar2=None, op0=ALU.mult))
                        bc_ev.append(s_dve.n)

        p1 = ExitStack()
        big_act = p1.enter_context(nc.sbuf_tensor("big_act", [128, 8 * S], BF16))
        q_fm = p1.enter_context(nc.sbuf_tensor("q_fm", [128, 8 * S], BF16))
        k_fm = p1.enter_context(nc.sbuf_tensor("k_fm", [128, 8 * S], BF16))
        vplus = p1.enter_context(nc.sbuf_tensor("vplus", [128, 16 * NH * 65], BF16))
        sumsq = p1.enter_context(nc.sbuf_tensor("sumsq", [16, 2 * S], F32))

        for s in range(n_seq):
            if "ln" in phases:
                with nc.Block() as blk, (
                    nc.sbuf_tensor(f"xt_{s}", [128, 4 * D], F32)) as xt, (
                    nc.sbuf_tensor(f"xsq_{s}", [128, D], F32)) as xsq, (
                    nc.sbuf_tensor(f"lnv_{s}", [128, 32], F32)) as lnv, (
                    nc.sbuf_tensor(f"xn16_{s}", [128, 4 * D], BF16)) as xn16, (
                    nc.sbuf_tensor(f"wstg_{s}", [128, 2 * 1024], F32)) as wstg, (
                    nc.sbuf_tensor(f"wq16_{s}", [128, 2 * 1024], BF16)) as wq16, (
                    nc.sbuf_tensor(f"wv16_{s}", [128, 2 * 1024], BF16)) as wv16, (
                    nc.sbuf_tensor(f"qsq_{s}", [128, S], BF16)) as qsq, (
                    nc.psum_tensor(f"psqk_{s}", [128, 2 * 512], F32)) as psqk, (
                    nc.psum_tensor(f"pssq_{s}", [16, 4 * 512], F32)) as pssq, (
                    nc.psum_tensor(f"psv_{s}", [128, 2 * 512], F32)) as psv:
                    act0 = s_act.n
                    dve0 = s_dve.n
                    xs0 = [q.n for q in s_x4]
                    xts0 = [q.n for q in s_xts]
                    xtp0 = [q.n for q in s_xtp]
                    xl_cf = lambda tt: xs0[tt % 4] + 16 * (tt // 4 + 1)
                    sqd_cf = lambda tt: act0 + 3 * tt + 1
                    sdd_cf = lambda tt: act0 + 3 * tt + 2
                    xnd_cf = lambda tt: act0 + 3 * tt + 3
                    vard_cf = lambda tt: dve0 + 7 * tt + 5
                    nmd_cf = lambda tt: dve0 + 7 * tt + 7
                    trp_cf = lambda tt: xtp0[tt % 4] + 16 * (tt // 4 + 1)

                    def emit_xload(sync, tts):
                        for tt in tts:
                            b = tt % 4
                            if tt >= 4:
                                sync.wait_ge(s_act.h, xnd_cf(tt - 4))
                            s_x4[b].inc(sync.dma_start(
                                out=xt[:, b * D:(b + 1) * D],
                                in_=x_h[s, tt * 128:(tt + 1) * 128, :]), 16)
                            assert s_x4[b].n == xl_cf(tt)

                    def emit_transposes(sync, half):
                        for sl4 in range(4):
                            sync.wait_ge(s_xtp[sl4].h,
                                         xtp0[sl4] + 16 * (2 * half + 2))
                        for ich in range(8):
                            s_xts[half].inc(sync.dma_start_transpose(
                                out=big_act[:, ich * S + half * 1024:
                                            ich * S + (half + 1) * 1024],
                                in_=xn_tm_sc[s, half * 1024:(half + 1) * 1024,
                                             ich * 128:(ich + 1) * 128]), 16)

                    if "qkv" not in phases:
                        @blk.sync
                        def _(sync):
                            emit_xload(sync, range(16))

                    @blk.scalar
                    def _(scalar):
                        for tt in range(16):
                            b = tt % 4
                            b2 = tt % 2
                            lv = b * 8
                            scalar.wait_ge(s_x4[b].h, xl_cf(tt))
                            if tt >= 1:
                                scalar.wait_ge(s_dve.h, dve0 + 7 * (tt - 1) + 2)
                            s_act.inc(scalar.activation(out=xsq[:, :],
                                                        in_=xt[:, b * D:(b + 1) * D],
                                                        func=AF.Square, bias=0.0, scale=1.0))
                            scalar.wait_ge(s_dve.h, vard_cf(tt))
                            s_act.inc(scalar.activation(out=lnv[:, lv + 6:lv + 7],
                                                        in_=lnv[:, lv + 4:lv + 5],
                                                        func=AF.Sqrt, bias=epsv[:, 0:1],
                                                        scale=-1.0))
                            scalar.wait_ge(s_dve.h, nmd_cf(tt))
                            if tt >= 4:
                                scalar.wait_ge(s_xtp[b].h, trp_cf(tt - 4))
                            s_act.inc(scalar.activation(out=xn16[:, b * D:(b + 1) * D],
                                                        in_=xt[:, b * D:(b + 1) * D],
                                                        func=AF.Identity,
                                                        bias=lnv[:, lv + 4:lv + 5],
                                                        scale=lnv[:, lv + 7:lv + 8]))
                            assert s_act.n == xnd_cf(tt)

                    @blk.vector
                    def _(vector):
                        for tt in range(16):
                            b = tt % 4
                            b2 = tt % 2
                            lv = b * 8
                            vector.wait_ge(s_x4[b].h, xl_cf(tt))
                            s_dve.inc(vector.reduce_sum(out=lnv[:, lv:lv + 1],
                                                        in_=xt[:, b * D:(b + 1) * D], axis=AX.X))
                            vector.wait_ge(s_act.h, sqd_cf(tt))
                            s_dve.inc(vector.reduce_sum(out=lnv[:, lv + 1:lv + 2],
                                                        in_=xsq[:, :], axis=AX.X))
                            vector.wait_ge(s_dve.h, s_dve.n - 1)
                            s_dve.inc(vector.tensor_scalar(out=lnv[:, lv + 2:lv + 3],
                                                           in0=lnv[:, lv:lv + 1],
                                                           scalar1=1.0 / D, scalar2=None,
                                                           op0=ALU.mult))
                            vector.wait_ge(s_dve.h, s_dve.n)
                            s_dve.inc(vector.tensor_scalar(out=lnv[:, lv + 3:lv + 4],
                                                           in0=lnv[:, lv + 1:lv + 2],
                                                           scalar1=1.0 / D, scalar2=None,
                                                           op0=ALU.mult))
                            vector.wait_ge(s_dve.h, s_dve.n)
                            s_dve.inc(vector.scalar_tensor_tensor(
                                out=lnv[:, lv + 4:lv + 5], in0=lnv[:, lv + 2:lv + 3],
                                scalar=lnv[:, lv + 2:lv + 3],
                                in1=lnv[:, lv + 3:lv + 4], op0=ALU.mult, op1=ALU.subtract))
                            assert s_dve.n == vard_cf(tt)
                            vector.wait_ge(s_act.h, sdd_cf(tt))
                            s_dve.inc(vector.reciprocal(out=lnv[:, lv + 7:lv + 8],
                                                        in_=lnv[:, lv + 6:lv + 7]))
                            vector.wait_ge(s_dve.h, s_dve.n)
                            s_dve.inc(vector.scalar_tensor_tensor(
                                out=lnv[:, lv + 4:lv + 5], in0=lnv[:, lv + 2:lv + 3], scalar=-1.0,
                                in1=lnv[:, lv + 7:lv + 8], op0=ALU.mult, op1=ALU.mult))
                            assert s_dve.n == nmd_cf(tt)


                    if "qkv" in phases:
                        mm0 = s_mm.n
                        ev0 = s_ev.n
                        actB = s_act.n
                        dveB = s_dve.n
                        cast0 = s_cast.n
                        ws0 = [s_ws[0].n, s_ws[1].n]
                        wld_cf = lambda och: ws0[och % 2] + 16 * (och // 2 + 1)
                        wcst_cf = lambda och: cast0 + och + 1
                        mmg_cf = lambda g: mm0 + g + 1
                        qkmm_cf = lambda och: mm0 + 4 * (och + 1)
                        qkev_cf = lambda g: ev0 + g + 1
                        qkevt_cf = lambda och: ev0 + 4 * (och + 1)
                        sqmm_cf = lambda och: mm0 + 64 + och + 1
                        sqa_cf = lambda och: actB + och + 1
                        vmm_cf = lambda g2: mm0 + 80 + g2 + 1
                        vev_cf = lambda g2: dveB + 2 + g2 + 1

                        def emit_slab_loads(sync, ochs):
                            for och in ochs:
                                b = och % 2
                                if och >= 2:
                                    sync.wait_ge(s_cast.h, wcst_cf(och - 2))
                                s_ws[b].inc(sync.dma_start(
                                    out=wstg[:, b * 1024:(b + 1) * 1024].rearrange(
                                        "p (i o) -> p i o", o=128),
                                    in_=wqkv_h[och].rearrange("(i p) o -> p i o", p=128)), 16)
                                assert s_ws[b].n == wld_cf(och)

                        @blk.sync
                        def _(sync):
                            emit_xload(sync, range(0, 4))
                            emit_slab_loads(sync, range(0, 2))
                            emit_xload(sync, range(4, 12))
                            emit_transposes(sync, 0)
                            emit_slab_loads(sync, range(2, 4))
                            emit_xload(sync, range(12, 16))
                            emit_transposes(sync, 1)
                            emit_slab_loads(sync, range(4, 24))

                        @blk.gpsimd
                        def _(gpsimd):
                            def emit_store(tt):
                                b = tt % 4
                                gpsimd.wait_ge(s_act.h, xnd_cf(tt))
                                s_xtp[b].inc(gpsimd.dma_start(
                                    out=xn_tm_sc[s, tt * 128:(tt + 1) * 128, :],
                                    in_=xn16[:, b * D:(b + 1) * D]), 16)

                            def emit_cast(och):
                                b = och % 2
                                gpsimd.wait_ge(s_ws[b].h, wld_cf(och))
                                if och >= 2:
                                    gpsimd.wait_ge(s_mm.h, qkmm_cf(och - 2))
                                s_cast.inc(gpsimd.tensor_copy(
                                    out=wq16[:, b * 1024:(b + 1) * 1024],
                                    in_=wstg[:, b * 1024:(b + 1) * 1024]))
                                assert s_cast.n == wcst_cf(och)

                            for tt in range(8):
                                emit_store(tt)
                            for och in range(2):
                                emit_cast(och)
                            for tt in range(8, 16):
                                emit_store(tt)
                            for och in range(2, 16):
                                emit_cast(och)

                        @blk.gpsimd
                        def _(gpsimd):
                            for och in range(16, 24):
                                b = och % 2
                                ov = och - 16
                                gpsimd.wait_ge(s_ws[b].h, wld_cf(och))
                                if ov >= 2:
                                    gpsimd.wait_ge(s_mm.h, vmm_cf((ov - 1) * 16 - 1))
                                s_cast.inc(gpsimd.tensor_copy(
                                    out=wv16[:, (ov % 2) * 1024:(ov % 2 + 1) * 1024],
                                    in_=wstg[:, b * 1024:(b + 1) * 1024]))
                                assert s_cast.n == wcst_cf(och)

                        @blk.gpsimd
                        def _(gpsimd):
                            gpsimd.wait_ge(s_xts[0].h, s_xts[0].n)
                            gpsimd.wait_ge(s_xts[1].h, s_xts[1].n)
                            for ich in range(8):
                                s_sts[ich % 2].inc(gpsimd.dma_start(
                                    out=xn_sc[ich, :, s * S:(s + 1) * S],
                                    in_=big_act[:, ich * S:(ich + 1) * S]), 16)
                                if "xn" in debug:
                                    s_sts[ich % 2].inc(gpsimd.dma_start(
                                        out=dbg["xn"][ich, :, s * S:(s + 1) * S],
                                        in_=big_act[:, ich * S:(ich + 1) * S]), 16)


                        @blk.tensor
                        def _(tensor):
                            for och in range(16):
                                b = och % 2
                                for tb in range(4):
                                    g = och * 4 + tb
                                    pb = g % 2
                                    if och == 0:
                                        half = tb // 2
                                        tensor.wait_ge(s_xts[half].h,
                                                       xts0[half] + 128 * (half + 1) - 128 * half)
                                    if tb == 0:
                                        tensor.wait_ge(s_cast.h, wcst_cf(och))
                                    if g >= 2:
                                        tensor.wait_ge(s_ev.h, qkev_cf(g - 2))
                                    for ich in range(8):
                                        inst = tensor.matmul(
                                            psqk[:, pb * 512:(pb + 1) * 512],
                                            wq16[:, b * 1024 + ich * 128:b * 1024 + (ich + 1) * 128],
                                            big_act[:, ich * S + tb * 512:ich * S + (tb + 1) * 512],
                                            start=(ich == 0), stop=(ich == 7))
                                    s_mm.inc(inst)
                                    assert s_mm.n == mmg_cf(g)
                            for och in range(16):
                                c = och % 8
                                tensor.wait_ge(s_act.h, sqa_cf(och))
                                if och == 8:
                                    tensor.wait_ge(s_dve.h, dveB + 1)
                                for qb in range(4):
                                    inst = tensor.matmul(
                                        pssq[:, qb * 512:(qb + 1) * 512],
                                        ind8[:, c * 16:c * 16 + 16],
                                        qsq[:, qb * 512:(qb + 1) * 512],
                                        start=(c == 0), stop=(c == 7))
                                s_mm.inc(inst)
                                assert s_mm.n == sqmm_cf(och)
                            for ov in range(8):
                                tensor.wait_ge(s_cast.h, wcst_cf(16 + ov))
                                for tt in range(16):
                                    g2 = ov * 16 + tt
                                    pb = g2 % 2
                                    if g2 >= 2:
                                        tensor.wait_ge(s_dve.h, vev_cf(g2 - 2))
                                    for ich in range(8):
                                        inst = tensor.matmul(
                                            psv[:, pb * 512:pb * 512 + 128],
                                            big_act[:, ich * S + tt * 128:ich * S + (tt + 1) * 128],
                                            wv16[:, (ov % 2) * 1024 + ich * 128:
                                                 (ov % 2) * 1024 + (ich + 1) * 128],
                                            start=(ich == 0), stop=(ich == 7))
                                    s_mm.inc(inst)
                                    assert s_mm.n == vmm_cf(g2)

                        @blk.scalar
                        def _(scalar):
                            for och in range(16):
                                is_q = och < 8
                                c = och % 8
                                dst = q_fm if is_q else k_fm
                                for tb in range(4):
                                    g = och * 4 + tb
                                    pb = g % 2
                                    scalar.wait_ge(s_mm.h, mmg_cf(g))
                                    s_ev.inc(scalar.activation(
                                        out=dst[:, c * S + tb * 512:c * S + (tb + 1) * 512],
                                        in_=psqk[:, pb * 512:(pb + 1) * 512],
                                        func=AF.Identity, bias=bq8[:, och:och + 1],
                                        scale=0.125 if is_q else 1.0))
                                    assert s_ev.n == qkev_cf(g)
                            for och in range(16):
                                c = och % 8
                                src_t = q_fm if och < 8 else k_fm
                                scalar.wait_ge(s_ev.h, qkevt_cf(och))
                                if och >= 1:
                                    scalar.wait_ge(s_mm.h, sqmm_cf(och - 1))
                                s_act.inc(scalar.activation(out=qsq[:, :],
                                                            in_=src_t[:, c * S:(c + 1) * S],
                                                            func=AF.Square, bias=0.0, scale=1.0))
                                assert s_act.n == sqa_cf(och)

                        @blk.vector
                        def _(vector):
                            vector.wait_ge(s_mm.h, sqmm_cf(7))
                            for qb in range(4):
                                inst = vector.tensor_copy(
                                    out=sumsq[0:16, qb * 512:(qb + 1) * 512],
                                    in_=pssq[0:16, qb * 512:(qb + 1) * 512])
                            s_dve.inc(inst)
                            vector.wait_ge(s_mm.h, sqmm_cf(15))
                            for qb in range(4):
                                inst = vector.tensor_copy(
                                    out=sumsq[0:16, S + qb * 512:S + (qb + 1) * 512],
                                    in_=pssq[0:16, qb * 512:(qb + 1) * 512])
                            s_dve.inc(inst)
                            for ov in range(8):
                                for tt in range(16):
                                    g2 = ov * 16 + tt
                                    pb = g2 % 2
                                    vector.wait_ge(s_mm.h, vmm_cf(g2))
                                    for j in range(2):
                                        h = 2 * ov + j
                                        inst = vector.tensor_tensor(
                                            out=vplus[:, tt * NH * 65 + h * 65:
                                                      tt * NH * 65 + h * 65 + 64],
                                            in0=psv[:, pb * 512 + j * 64:pb * 512 + (j + 1) * 64],
                                            in1=bv_bc[:, ov * 128 + j * 64:ov * 128 + (j + 1) * 64],
                                            op=ALU.add)
                                    s_dve.inc(inst)
                                    assert s_dve.n == vev_cf(g2)

                        @blk.gpsimd
                        def _(gpsimd):
                            gpsimd.wait_ge(s_dve.h, s_dve.n)
                            for tt in range(16):
                                ap3 = vplus[:, tt * NH * 65:(tt + 1) * NH * 65].rearrange(
                                    "p (h c) -> p h c", c=65)[:, :, 64:65]
                                inst = gpsimd.memset(ap3, 1.0)
                            s_pool.inc(inst)

                        if "qkv" in debug:
                            @blk.sync
                            def _(sync):
                                sync.wait_ge(s_ev.h, s_ev.n)
                                sync.wait_ge(s_dve.h, s_dve.n)
                                sync.wait_ge(s_pool.h, s_pool.n)
                                for ich in range(8):
                                    s_prep.inc(sync.dma_start(out=dbg["q"][ich, :, :],
                                                              in_=q_fm[:, ich * S:(ich + 1) * S]), 16)
                                    s_prep.inc(sync.dma_start(out=dbg["k"][ich, :, :],
                                                              in_=k_fm[:, ich * S:(ich + 1) * S]), 16)
                                for tt in range(16):
                                    s_prep.inc(sync.dma_start(
                                        out=dbg["v"][tt, :, :],
                                        in_=vplus[:, tt * NH * 65:(tt + 1) * NH * 65]), 16)
                                s_prep.inc(sync.dma_start(out=dbg["ssq"][:, :], in_=sumsq[0:16, :]), 16)
                                sync.wait_ge(s_prep.h, s_prep.n)

            if "attn" in phases:
                with nc.Block() as blk, (
                    nc.sbuf_tensor(f"probsT_{s}", [128, 2 * 16 * 512], BF16)) as probsT, (
                    nc.sbuf_tensor(f"nch_{s}", [128, 16], F32)) as nch, (
                    nc.sbuf_tensor(f"nch2_{s}", [128, 8], F32)) as nch2, (
                    nc.sbuf_tensor(f"vraw_{s}", [128, 4 * 512], F32)) as vraw, (
                    nc.sbuf_tensor(f"sumr_{s}", [65, 512], F32)) as sumr, (
                    nc.sbuf_tensor(f"rsum_{s}", [65, 2 * 512], F32)) as rsum, (
                    nc.sbuf_tensor(f"cfl_{s}", [1, 16], F32)) as cfl, (
                    nc.sbuf_tensor(f"mk_{s}", [16, 4], F32)) as mkt, (
                    nc.psum_tensor(f"psT_{s}", [128, 4 * 512], F32)) as psT, (
                    nc.psum_tensor(f"psval_{s}", [128, 2 * 512], F32)) as psval, (
                    nc.psum_tensor(f"psibc_{s}", [128, 512], F32)) as psibc:

                    @blk.vector
                    def _(vector):
                        wait_stores_all(vector)
                        vector.wait_ge(s_pool.h, s_pool.n)
                        s_dve.inc(vector.reduce_max(out=mkt[:, 0:1], in_=sumsq[0:16, 0:S],
                                                    axis=AX.X))
                        s_dve.inc(vector.reduce_max(out=mkt[:, 1:2], in_=sumsq[0:16, S:2 * S],
                                                    axis=AX.X))
                        vector.wait_ge(s_dve.h, s_dve.n)
                        s_dve.inc(vector.tensor_tensor(out=mkt[:, 2:3], in0=mkt[:, 0:1],
                                                       in1=mkt[:, 1:2], op=ALU.mult))
                    cprod = s_dve.n

                    @blk.scalar
                    def _(scalar, cprod=cprod):
                        scalar.wait_ge(s_dve.h, cprod)
                        s_act.inc(scalar.activation(out=mkt[:, 3:4], in_=mkt[:, 2:3],
                                                    func=AF.Sqrt, bias=0.0, scale=1.0))
                    c16 = s_act.n

                    @blk.sync
                    def _(sync, c16=c16):
                        sync.wait_ge(s_act.h, c16)
                        s_crs[0].inc(sync.dma_start(out=csc_h[s, :], in_=mkt[0:16, 3:4]), 16)
                    c_st = s_crs[0].n

                    @blk.sync
                    def _(sync, c_st=c_st):
                        sync.wait_ge(s_crs[0].h, c_st)
                        s_crs[1].inc(sync.dma_start(out=cfl[:, :], in_=csc_h[s, :]), 16)
                    c_ld = s_crs[1].n

                    @blk.tensor
                    def _(tensor, c_ld=c_ld):
                        tensor.wait_ge(s_crs[1].h, c_ld)
                        s_mm.inc(tensor.matmul(psibc[:, 0:16], negones[:, :], cfl[:, :],
                                               start=True, stop=True))
                    nch_mm = s_mm.n

                    @blk.vector
                    def _(vector, nch_mm=nch_mm):
                        vector.wait_ge(s_mm.h, nch_mm)
                        s_dve.inc(vector.tensor_copy(out=nch[:, :], in_=psibc[:, 0:16]))
                        vector.wait_ge(s_dve.h, s_dve.n)
                        for p8 in range(8):
                            s_dve.inc(vector.tensor_tensor(
                                out=nch2[:, p8:p8 + 1], in0=nch[:, 2 * p8:2 * p8 + 1],
                                in1=nch[:, 2 * p8 + 1:2 * p8 + 2], op=ALU.min))
                    nch_ok = s_dve.n

                    actC = s_act.n
                    exp_done = {}
                    _cnt = actC
                    _kg = 0
                    for _qb in range(4):
                        for _p in range(8):
                            for _kb in range(4 * _qb + 4):
                                if _kb < 4 * _qb:
                                    _cnt += 1
                                    exp_done[_kg] = _cnt
                                    exp_done[_kg + 1] = _cnt
                                else:
                                    _cnt += 1
                                    exp_done[_kg] = _cnt
                                    _cnt += 1
                                    exp_done[_kg + 1] = _cnt
                                _kg += 2
                    kg0_of = {}
                    exp_k = {}
                    val_k = {}
                    mm_k = {}
                    msk_k = {}
                    val_ev = {}
                    vraw_d = {}
                    rec_d = {}
                    sumbc = {}
                    pzs = {}
                    kgg = 0
                    NKB = lambda i: 4 * (i // 16) + 4

                    def emit_vraw_pair(pr):
                        @blk.vector
                        def _(vector, pr=pr):
                            iB = 2 * pr + 1
                            vs = pr % 4
                            vector.wait_ge(s_mm.h, val_k[(iB, NKB(iB) - 1)])
                            if pr >= 1:
                                vector.wait_ge(s_dve.h, rec_d[pr - 1])
                            if pr >= 4:
                                vector.wait_ge(s_dve.h, val_ev[pr - 4])
                            if pr == 0:
                                vector.wait_ge(s_pool.h, pzs[0])
                            vector.tensor_copy(out=sumr[0:1, :],
                                               in_=psval[64:65, 0:512])
                            vector.tensor_copy(out=sumr[64:65, :],
                                               in_=psval[64:65, 512:1024])
                            vector.tensor_copy(
                                out=vraw[0:64, vs * 512:(vs + 1) * 512],
                                in_=psval[0:64, 0:512])
                            s_dve.inc(vector.tensor_copy(
                                out=vraw[64:128, vs * 512:(vs + 1) * 512],
                                in_=psval[0:64, 512:1024]))
                            vraw_d[pr] = s_dve.n
                            if pr >= 2:
                                vector.wait_ge(s_mm.h, sumbc[pr - 2])
                            vector.wait_ge(s_dve.h, vraw_d[pr])
                            s_dve.inc(vector.reciprocal(
                                out=rsum[0:65, (pr % 2) * 512:(pr % 2 + 1) * 512],
                                in_=sumr[0:65, :]))
                            rec_d[pr] = s_dve.n

                    def emit_tail_pair(pr):
                        @blk.tensor
                        def _(tensor, pr=pr):
                            tensor.wait_ge(s_dve.h, rec_d[pr])
                            tensor.wait_ge(s_dve.h,
                                           val_ev[pr - 1] if pr >= 1 else nch_ok)
                            s_mm.inc(tensor.matmul(
                                psibc[:, 0:512], sel2[:, :],
                                rsum[0:65, (pr % 2) * 512:(pr % 2 + 1) * 512],
                                start=True, stop=True))
                            sumbc[pr] = s_mm.n

                        @blk.vector
                        def _(vector, pr=pr):
                            vector.wait_ge(s_mm.h, sumbc[pr])
                            vs = pr % 4
                            qb_t = pr // 8
                            ph = pr % 8
                            s_dve.inc(vector.tensor_tensor(
                                out=big_act[:, ph * S + qb_t * 512:
                                            ph * S + (qb_t + 1) * 512],
                                in0=vraw[:, vs * 512:(vs + 1) * 512],
                                in1=psibc[:, 0:512], op=ALU.mult))
                            val_ev[pr] = s_dve.n

                    def emit_vals(tensor, idx_v, j):
                        vb = idx_v % 2
                        nkbv = NKB(idx_v)
                        if j == 0 and idx_v >= 2:
                            tensor.wait_ge(s_dve.h, vraw_d[idx_v // 2 - 1])
                        h_v = idx_v % 16
                        tensor.wait_ge(s_act.h, exp_done[kg0_of[idx_v] + 2 * j + (idx_v % 2)])
                        s_mm.inc(tensor.matmul(
                            psval[0:65, vb * 512:(vb + 1) * 512],
                            vplus[:, j * NH * 65 + h_v * 65:j * NH * 65 + (h_v + 1) * 65],
                            probsT[:, j * 1024 + vb * 512:j * 1024 + (vb + 1) * 512],
                            start=(j == 0), stop=(j == nkbv - 1)))
                        val_k[(idx_v, j)] = s_mm.n

                    for qb in range(4):
                        nkb = 4 * qb + 4

                        @blk.gpsimd
                        def _(gpsimd, qb=qb):
                            gpsimd.wait_ge(s_mm.h, s_mm.n)
                            inst = None
                            if qb == 0:
                                gpsimd.memset(sumr[:, :], 1.0)
                            for t in range(2):
                                for i in range(4):
                                    for j in range(i):
                                        kbz = 4 * qb + i
                                        inst = gpsimd.memset(
                                            probsT[:, kbz * 1024 + t * 512 + j * 128:
                                                   kbz * 1024 + t * 512 + (j + 1) * 128],
                                            0.0)
                            s_pool.inc(inst)
                        pzs[qb] = s_pool.n

                        for p in range(8):
                            iA = qb * 16 + 2 * p
                            iB = iA + 1
                            jA, jB = iA - 2, iB - 2
                            kg0 = kgg
                            kg0_of[iA] = kg0
                            kg0_of[iB] = kg0

                            @blk.tensor
                            def _(tensor, qb=qb, iA=iA, iB=iB, jA=jA, jB=jB,
                                  nkb=nkb, kg0=kg0):
                                for j in range(nkb):
                                    if jA >= 0 and j < NKB(jA):
                                        emit_vals(tensor, jA, j)
                                    if jB >= 0 and j < NKB(jB):
                                        emit_vals(tensor, jB, j)
                                    if kg0 + 2 * j + 1 >= 4:
                                        tensor.wait_ge(s_act.h, exp_done[kg0 + 2 * j - 3])
                                    for t, ii in ((0, iA), (1, iB)):
                                        kg = kg0 + 2 * j + t
                                        slot = kg % 4
                                        base = t * 64
                                        ch = (ii % 16) // 2
                                        s_mm.inc(tensor.matmul(
                                            psT[:, slot * 512:(slot + 1) * 512],
                                            k_fm[base:base + 64,
                                                 ch * S + j * 128:ch * S + (j + 1) * 128],
                                            q_fm[base:base + 64,
                                                 ch * S + qb * 512:ch * S + (qb + 1) * 512],
                                            start=True, stop=True))
                                        mm_k[(ii, j)] = s_mm.n

                            @blk.vector
                            def _(vector, qb=qb, iA=iA, iB=iB, nkb=nkb, kg0=kg0):
                                for kb in range(4 * qb, nkb):
                                    i_diag = kb - 4 * qb
                                    for t, ii in ((0, iA), (1, iB)):
                                        slot = (kg0 + 2 * kb + t) % 4
                                        vector.wait_ge(s_mm.h, mm_k[(ii, kb)])
                                        s_dve.inc(vector.tensor_tensor(
                                            out=psT[:, slot * 512 + i_diag * 128:
                                                    slot * 512 + (i_diag + 1) * 128],
                                            in0=psT[:, slot * 512 + i_diag * 128:
                                                    slot * 512 + (i_diag + 1) * 128],
                                            in1=maskT[:, :], op=ALU.add))
                                        msk_k[(ii, kb)] = s_dve.n

                            kgg += 2 * nkb
                            pr_cur = iA // 2
                            if pr_cur >= 1:
                                emit_vraw_pair(pr_cur - 1)
                            if pr_cur >= 2:
                                emit_tail_pair(pr_cur - 2)

                        @blk.scalar
                        def _(scalar, qb=qb, nkb=nkb):
                            scalar.wait_ge(s_pool.h, pzs[qb])
                            scalar.wait_ge(s_dve.h, nch_ok)
                            for p in range(8):
                                iA = qb * 16 + 2 * p
                                iB = iA + 1
                                kg0 = kg0_of[iA]
                                for kb in range(nkb):
                                    i_diag = kb - 4 * qb
                                    kgA = kg0 + 2 * kb
                                    if i_diag < 0:
                                        slotA = kgA % 4
                                        scalar.wait_ge(s_mm.h, mm_k[(iB, kb)])
                                        if iB >= 2 and kb < NKB(iB - 2):
                                            scalar.wait_ge(s_mm.h, val_k[(iB - 2, kb)])
                                        s_act.inc(scalar.activation(
                                            out=probsT[:, kb * 1024:(kb + 1) * 1024],
                                            in_=psT[:, slotA * 512:(slotA + 2) * 512],
                                            func=AF.Exp, bias=nch2[:, p:p + 1], scale=1.0))
                                        assert s_act.n == exp_done[kgA] == exp_done[kgA + 1]
                                    else:
                                        valid0 = i_diag * 128
                                        for t, ii in ((0, iA), (1, iB)):
                                            kg = kgA + t
                                            slot = kg % 4
                                            scalar.wait_ge(s_dve.h, msk_k[(ii, kb)])
                                            if ii >= 2 and kb < NKB(ii - 2):
                                                scalar.wait_ge(s_mm.h, val_k[(ii - 2, kb)])
                                            s_act.inc(scalar.activation(
                                                out=probsT[:, kb * 1024 + t * 512 + valid0:
                                                           kb * 1024 + (t + 1) * 512],
                                                in_=psT[:, slot * 512 + valid0:(slot + 1) * 512],
                                                func=AF.Exp, bias=nch2[:, p:p + 1], scale=1.0))
                                            assert s_act.n == exp_done[kg]

                    @blk.tensor
                    def _(tensor):
                        for j in range(NKB(62)):
                            emit_vals(tensor, 62, j)
                            emit_vals(tensor, 63, j)
                    emit_vraw_pair(31)
                    emit_tail_pair(30)
                    emit_tail_pair(31)

                    if "vals" in debug:
                        @blk.sync
                        def _(sync):
                            sync.wait_ge(s_dve.h, s_dve.n)
                            for ich in range(8):
                                s_prep.inc(sync.dma_start(out=dbg["vals"][ich, :, :],
                                                          in_=big_act[:, ich * S:(ich + 1) * S]), 16)
                            sync.wait_ge(s_prep.h, s_prep.n)

            if "proj" in phases:
                with nc.Block() as blk, (
                    nc.sbuf_tensor(f"wostg_{s}", [128, 2 * D], F32)) as wostg, (
                    nc.sbuf_tensor(f"wo16_{s}", [128, 8 * D], BF16)) as wo16, (
                    nc.sbuf_tensor(f"aout_{s}", [128, 2 * D], F32)) as aout, (
                    nc.psum_tensor(f"pspr_{s}", [128, 2 * 512], F32)) as pspr:

                    wol = []
                    woc = []
                    for ich in range(8):
                        b = ich % 2

                        @blk.sync
                        def _(sync, ich=ich, b=b):
                            if ich >= 2:
                                sync.wait_ge(s_cast.h, woc[ich - 2])
                            s_ws[b].inc(sync.dma_start(out=wostg[:, b * D:(b + 1) * D],
                                                       in_=wout_h[ich * 128:(ich + 1) * 128, :]), 16)
                        wol.append(s_ws[b].n)

                        @blk.gpsimd
                        def _(gpsimd, ich=ich, b=b):
                            gpsimd.wait_ge(s_ws[b].h, wol[ich])
                            s_cast.inc(gpsimd.tensor_copy(out=wo16[:, ich * D:(ich + 1) * D],
                                                          in_=wostg[:, b * D:(b + 1) * D]))
                        woc.append(s_cast.n)

                    pr_ev = []
                    st_done = []
                    for tt in range(16):
                        for oh in range(2):
                            g = tt * 2 + oh
                            pb = g % 2

                            @blk.tensor
                            def _(tensor, tt=tt, oh=oh, g=g, pb=pb):
                                if g == 0:
                                    tensor.wait_ge(s_cast.h, woc[7])
                                if g >= 2:
                                    tensor.wait_ge(s_dve.h, pr_ev[g - 2])
                                for ich in range(8):
                                    inst = tensor.matmul(
                                        pspr[:, pb * 512:(pb + 1) * 512],
                                        big_act[:, ich * S + tt * 128:ich * S + (tt + 1) * 128],
                                        wo16[:, ich * D + oh * 512:ich * D + (oh + 1) * 512],
                                        start=(ich == 0), stop=(ich == 7))
                                s_mm.inc(inst)
                            prm = s_mm.n

                            @blk.vector
                            def _(vector, tt=tt, oh=oh, pb=pb, prm=prm):
                                vector.wait_ge(s_mm.h, prm)
                                if tt >= 2:
                                    vector.wait_ge(s_sts[tt % 2].h, st_done[tt - 2])
                                s_dve.inc(vector.tensor_tensor(
                                    out=aout[:, (tt % 2) * D + oh * 512:(tt % 2) * D + (oh + 1) * 512],
                                    in0=pspr[:, pb * 512:(pb + 1) * 512],
                                    in1=bout_bc[:, oh * 512:(oh + 1) * 512], op=ALU.add))
                            pr_ev.append(s_dve.n)

                        pe = s_dve.n

                        @blk.gpsimd
                        def _(gpsimd, tt=tt, pe=pe):
                            gpsimd.wait_ge(s_dve.h, pe)
                            s_sts[tt % 2].inc(gpsimd.dma_start(
                                out=attn_sc[s, tt * 128:(tt + 1) * 128, :],
                                in_=aout[:, (tt % 2) * D:(tt % 2 + 1) * D]), 16)
                            if "attn" in debug:
                                s_sts[tt % 2].inc(gpsimd.dma_start(
                                    out=dbg["attn"][s, tt * 128:(tt + 1) * 128, :],
                                    in_=aout[:, (tt % 2) * D:(tt % 2 + 1) * D]), 16)
                        st_done.append(s_sts[tt % 2].n)

                    @blk.gpsimd
                    def _(gpsimd):
                        wait_stores_all(gpsimd)

        p1.close()

        if "w12" in phases:
            with nc.Block() as blk, (
                nc.sbuf_tensor("w12stg", [128, 2 * 2048], F32)) as wstg, (
                nc.sbuf_tensor("w1r", [128, 8 * DH], BF16)) as w1r, (
                nc.sbuf_tensor("w2r", [128, 8 * DH], BF16)) as w2r, (
                nc.sbuf_tensor("xnb", [128, 2 * 8 * 512], BF16)) as xnb, (
                nc.sbuf_tensor("t1b", [128, 4 * 512], F32)) as t1b, (
                nc.sbuf_tensor("ab", [128, 4 * 512], BF16)) as ab, (
                nc.psum_tensor("ps1", [128, 4 * 512], F32)) as ps1, (
                nc.psum_tensor("ps2", [128, 4 * 512], F32)) as ps2:

                ld = []
                cst = []
                cst0 = s_cast.n
                for hh in range(2):
                    for wi, (wh, wr) in enumerate([(w1t_h, w1r), (w2t_h, w2r)]):
                        for ich in range(8):
                            j = hh * 16 + wi * 8 + ich
                            b = j % 2

                            @blk.sync
                            def _(sync, wh=wh, ich=ich, hh=hh, b=b, j=j):
                                if j >= 2:
                                    sync.wait_ge(s_cast.h, cst[j - 2])
                                s_ws[b].inc(sync.dma_start(
                                    out=wstg[:, b * 2048:(b + 1) * 2048],
                                    in_=wh[ich * 128:(ich + 1) * 128,
                                           hh * 2048:(hh + 1) * 2048]), 16)
                            ld.append(s_ws[b].n)

                            @blk.gpsimd
                            def _(gpsimd, wr=wr, ich=ich, hh=hh, b=b, j=j):
                                gpsimd.wait_ge(s_ws[b].h, ld[j])
                                s_cast.inc(gpsimd.tensor_copy(
                                    out=wr[:, ich * DH + hh * 2048:ich * DH + (hh + 1) * 2048],
                                    in_=wstg[:, b * 2048:(b + 1) * 2048]))
                            cst.append(s_cast.n)
                w_ready = s_cast.n

                mmW = s_mm.n
                actW = s_act.n
                dveW = s_dve.n
                stW = [q.n for q in s_sts]
                mst = 2 if "a" in debug else 1
                mm_cf = lambda g: mmW + g + 1
                sln_cf = lambda g: actW + g + 1
                ev_cf = lambda g: dveW + g + 1
                st_cf = lambda g: stW[g % 4] + 16 * mst * (g // 4 + 1)

                w12_mm = []
                for tb in range(T // 512):
                    b = tb % 2

                    @blk.sync
                    def _(sync, tb=tb, b=b):
                        if tb == 0:
                            wait_stores_all(sync)
                        if tb >= 2:
                            sync.wait_ge(s_mm.h, w12_mm[tb - 2])
                        for ich in range(8):
                            s_xs[b].inc(sync.dma_start(
                                out=xnb[:, b * 8 * 512 + ich * 512:b * 8 * 512 + (ich + 1) * 512],
                                in_=xn_sc[ich, :, tb * 512:(tb + 1) * 512]), 16)
                    xl = s_xs[b].n

                    @blk.tensor
                    def _(tensor, tb=tb, b=b, xl=xl):
                        tensor.wait_ge(s_xs[b].h, xl)
                        for och in range(32):
                            g = tb * 32 + och
                            pb = g % 4
                            if tb == 0:
                                tensor.wait_ge(s_cast.h,
                                               cst0 + (16 if och < 16 else 32))
                            if g >= 4:
                                tensor.wait_ge(s_dve.h, ev_cf(g - 4))
                            for ich in range(8):
                                tensor.matmul(
                                    ps1[:, pb * 512:(pb + 1) * 512],
                                    w1r[:, ich * DH + och * 128:ich * DH + (och + 1) * 128],
                                    xnb[:, b * 8 * 512 + ich * 512:b * 8 * 512 + (ich + 1) * 512],
                                    start=(ich == 0), stop=(ich == 7))
                            for ich in range(8):
                                inst = tensor.matmul(
                                    ps2[:, pb * 512:(pb + 1) * 512],
                                    w2r[:, ich * DH + och * 128:ich * DH + (och + 1) * 128],
                                    xnb[:, b * 8 * 512 + ich * 512:b * 8 * 512 + (ich + 1) * 512],
                                    start=(ich == 0), stop=(ich == 7))
                            s_mm.inc(inst)
                            assert s_mm.n == mm_cf(g)
                    w12_mm.append(s_mm.n)

                    @blk.scalar
                    def _(scalar, tb=tb):
                        for och in range(32):
                            g = tb * 32 + och
                            pb = g % 4
                            scalar.wait_ge(s_mm.h, mm_cf(g))
                            if g >= 4:
                                scalar.wait_ge(s_dve.h, ev_cf(g - 4))
                            s_act.inc(scalar.activation(out=t1b[:, pb * 512:(pb + 1) * 512],
                                                        in_=ps1[:, pb * 512:(pb + 1) * 512],
                                                        func=AF.Silu, bias=0.0,
                                                        scale=scl4[:, och:och + 1]))
                            assert s_act.n == sln_cf(g)

                    @blk.vector
                    def _(vector, tb=tb):
                        for och in range(32):
                            g = tb * 32 + och
                            pb = g % 4
                            vector.wait_ge(s_act.h, sln_cf(g))
                            if g >= 4:
                                vector.wait_ge(s_sts[g % 4].h, st_cf(g - 4))
                            s_dve.inc(vector.scalar_tensor_tensor(
                                out=ab[:, pb * 512:(pb + 1) * 512],
                                in0=ps2[:, pb * 512:(pb + 1) * 512],
                                scalar=scl4[:, 32 + och:33 + och],
                                in1=t1b[:, pb * 512:(pb + 1) * 512],
                                op0=ALU.mult, op1=ALU.mult))
                            assert s_dve.n == ev_cf(g)

                    @blk.gpsimd
                    def _(gpsimd, tb=tb):
                        for och in range(32):
                            g = tb * 32 + och
                            pb = g % 4
                            sb = g % 4
                            gpsimd.wait_ge(s_dve.h, ev_cf(g))
                            s_sts[sb].inc(gpsimd.dma_start(
                                out=a_sc[och, :, tb * 512:(tb + 1) * 512],
                                in_=ab[:, pb * 512:(pb + 1) * 512]), 16)
                            if "a" in debug:
                                s_sts[sb].inc(gpsimd.dma_start(
                                    out=dbg["a"][och, :, tb * 512:(tb + 1) * 512],
                                    in_=ab[:, pb * 512:(pb + 1) * 512]), 16)
                            assert s_sts[sb].n == st_cf(g)

                @blk.gpsimd
                def _(gpsimd):
                    wait_stores_all(gpsimd)

        for li, (src_sc, dst_sc, wh, scol, name) in enumerate([
                (a_sc, g1_sc, h1t_h, 64, "h1"), (g1_sc, g2_sc, h2t_h, 96, "h2")]):
            if name not in phases:
                continue
            max_slabs = max(H_SLAB_PASSES)
            with nc.Block() as blk, (
                nc.sbuf_tensor(f"{name}_wstg", [128, 2 * 1024], F32)) as wstg, (
                nc.sbuf_tensor(f"{name}_wr", [128, 2 * max_slabs * DH], BF16)) as wr, (
                nc.sbuf_tensor(f"{name}_in", [128, 2 * 32 * 512], BF16)) as inb, (
                nc.sbuf_tensor(f"{name}_out", [128, 4 * 512], BF16)) as outb, (
                nc.psum_tensor(f"{name}_ps", [128, 4 * 512], F32)) as ps:

                actH = s_act.n
                stH = [q.n for q in s_sts]
                mst = 2 if (name == "h2" and "g2" in debug) else 1
                ev_cf = lambda g: actH + g + 1
                st_cf = lambda g: stH[g % 4] + 16 * mst * (g // 4 + 1)

                slab0 = 0
                pp_mm = []
                tb_mm = []
                gg = 0
                tbg = 0
                gj = 0
                ld_all = []
                cst_all = []
                cst_by_pass = {}
                pass_off = [0]
                for nslab in H_SLAB_PASSES:
                    pass_off.append(pass_off[-1] + nslab)

                def emit_wload(npass_t, och, sl_t, q4):
                    nonlocal gj
                    wslot_t = (npass_t % 2) * max_slabs * DH
                    b = gj % 2
                    j = gj

                    @blk.sync
                    def _(sync, och=och, q4=q4, b=b, j=j):
                        if j >= 2:
                            sync.wait_ge(s_cast.h, cst_all[j - 2])
                        s_ws[b].inc(sync.dma_start(
                            out=wstg[:, b * 1024:(b + 1) * 1024].rearrange(
                                "p (i o) -> p i o", o=128),
                            in_=wh[och].rearrange("(i p) o -> p i o", p=128)[
                                :, q4 * 8:(q4 + 1) * 8, :]), 16)
                    ld_all.append(s_ws[b].n)

                    @blk.gpsimd
                    def _(gpsimd, sl_t=sl_t, q4=q4, b=b, j=j, npass_t=npass_t,
                          wslot_t=wslot_t):
                        gpsimd.wait_ge(s_ws[b].h, ld_all[j])
                        if npass_t >= 2 and sl_t == 0 and q4 == 0:
                            gpsimd.wait_ge(s_mm.h, pp_mm[npass_t - 2])
                        s_cast.inc(gpsimd.tensor_copy(
                            out=wr[:, wslot_t + sl_t * DH + q4 * 1024:
                                    wslot_t + sl_t * DH + (q4 + 1) * 1024],
                            in_=wstg[:, b * 1024:(b + 1) * 1024]))
                    cst_all.append(s_cast.n)
                    cst_by_pass.setdefault(npass_t, []).append(s_cast.n)
                    gj += 1

                for npass, nslab in enumerate(H_SLAB_PASSES):
                    wslot = (npass % 2) * max_slabs * DH
                    if npass == 0:
                        for sl in range(nslab):
                            for q4 in range(4):
                                emit_wload(0, slab0 + sl, sl, q4)
                    nxt = []
                    if npass + 1 < len(H_SLAB_PASSES):
                        for sl in range(H_SLAB_PASSES[npass + 1]):
                            for q4 in range(4):
                                nxt.append((npass + 1, pass_off[npass + 1] + sl, sl, q4))
                    per_tb = -(-len(nxt) // (T // 512)) if nxt else 0

                    for tb in range(T // 512):
                        b = tb % 2

                        @blk.sync
                        def _(sync, tb=tb, b=b, npass=npass, tbg=tbg):
                            if tb == 0 and npass == 0:
                                wait_stores_all(sync)
                            if tbg >= 2:
                                sync.wait_ge(s_mm.h, tb_mm[tbg - 2])
                            for i8 in range(4):
                                s_xs[b].inc(sync.dma_start(
                                    out=inb[:, b * 32 * 512 + i8 * 8 * 512:
                                            b * 32 * 512 + (i8 + 1) * 8 * 512].rearrange(
                                                "p (c t) -> p c t", t=512),
                                    in_=src_sc.rearrange("c p t -> p c t")[
                                        :, i8 * 8:(i8 + 1) * 8, tb * 512:(tb + 1) * 512]), 16)
                        inl = s_xs[b].n

                        for _k in range(per_tb):
                            if nxt:
                                emit_wload(*nxt.pop(0))

                        @blk.tensor
                        def _(tensor, b=b, inl=inl, nslab=nslab, gg0=gg,
                              wslot=wslot, first=(tb == 0),
                              slab_cst=tuple(cst_by_pass.get(npass, []))):
                            tensor.wait_ge(s_xs[b].h, inl)
                            for sl in range(nslab):
                                g = gg0 + sl
                                pb = g % 4
                                if first:
                                    tensor.wait_ge(s_cast.h, slab_cst[sl * 4 + 3])
                                if g >= 4:
                                    tensor.wait_ge(s_act.h, ev_cf(g - 4))
                                for ich in range(32):
                                    inst = tensor.matmul(
                                        ps[:, pb * 512:(pb + 1) * 512],
                                        wr[:, wslot + sl * DH + ich * 128:
                                           wslot + sl * DH + (ich + 1) * 128],
                                        inb[:, b * 32 * 512 + ich * 512:
                                            b * 32 * 512 + (ich + 1) * 512],
                                        start=(ich == 0), stop=(ich == 31))
                                s_mm.inc(inst)
                        mm_hi = s_mm.n
                        tb_mm.append(mm_hi)

                        @blk.scalar
                        def _(scalar, nslab=nslab, gg0=gg, mm_hi=mm_hi, scol=scol, slab0=slab0):
                            for sl in range(nslab):
                                g = gg0 + sl
                                och = slab0 + sl
                                pb = g % 4
                                scalar.wait_ge(s_mm.h, mm_hi - nslab + sl + 1)
                                if g >= 4:
                                    scalar.wait_ge(s_sts[g % 4].h, st_cf(g - 4))
                                s_act.inc(scalar.activation(
                                    out=outb[:, pb * 512:(pb + 1) * 512],
                                    in_=ps[:, pb * 512:(pb + 1) * 512],
                                    func=AF.Gelu, bias=0.0,
                                    scale=scl4[:, scol + och:scol + och + 1]))
                                assert s_act.n == ev_cf(g)

                        @blk.gpsimd
                        def _(gpsimd, tb=tb, nslab=nslab, gg0=gg, slab0=slab0, name=name,
                              dst_sc=dst_sc):
                            for sl in range(nslab):
                                g = gg0 + sl
                                pb = g % 4
                                sb = g % 4
                                och = slab0 + sl
                                gpsimd.wait_ge(s_act.h, ev_cf(g))
                                if name == "h1":
                                    s_sts[sb].inc(gpsimd.dma_start(
                                        out=dst_sc[och, :, tb * 512:(tb + 1) * 512],
                                        in_=outb[:, pb * 512:(pb + 1) * 512]), 16)
                                else:
                                    s_sts[sb].inc(gpsimd.dma_start(
                                        out=dst_sc[tb * 4:(tb + 1) * 4, och, :, :].rearrange(
                                            "t p o -> p t o"),
                                        in_=outb[:, pb * 512:(pb + 1) * 512].rearrange(
                                            "p (t o) -> p t o", o=128)), 16)
                                    if "g2" in debug:
                                        s_sts[sb].inc(gpsimd.dma_start(
                                            out=dbg["g2"][tb * 4:(tb + 1) * 4, och, :, :].rearrange(
                                                "t p o -> p t o"),
                                            in_=outb[:, pb * 512:(pb + 1) * 512].rearrange(
                                                "p (t o) -> p t o", o=128)), 16)
                                assert s_sts[sb].n == st_cf(g)

                        gg += nslab
                        tbg += 1
                    assert not nxt
                    pp_mm.append(s_mm.n)
                    slab0 += nslab

                @blk.gpsimd
                def _(gpsimd):
                    wait_stores_all(gpsimd)

        if "w3" in phases:
            with nc.Block() as blk, (
                nc.sbuf_tensor("w3stg", [128, 2 * 1024], F32)) as wstg, (
                nc.sbuf_tensor("w3r", [128, 32 * D], BF16)) as w3r, (
                nc.sbuf_tensor("g2b", [128, 2 * 32 * 128], BF16)) as g2b, (
                nc.sbuf_tensor("resb", [128, 2 * 2 * D], F32)) as resb, (
                nc.sbuf_tensor("outb3", [128, 2 * D], F32)) as outb3, (
                nc.psum_tensor("ps3", [128, 4 * 512], F32)) as ps3:

                ld = []
                cst = []
                for ich in range(32):
                    b = ich % 2

                    @blk.sync
                    def _(sync, ich=ich, b=b):
                        if ich >= 2:
                            sync.wait_ge(s_cast.h, cst[ich - 2])
                        s_ws[b].inc(sync.dma_start(out=wstg[:, b * 1024:(b + 1) * 1024],
                                                   in_=w3t_h[ich * 128:(ich + 1) * 128, :]), 16)
                    ld.append(s_ws[b].n)

                    @blk.gpsimd
                    def _(gpsimd, ich=ich, b=b):
                        gpsimd.wait_ge(s_ws[b].h, ld[ich])
                        s_cast.inc(gpsimd.tensor_copy(out=w3r[:, ich * D:(ich + 1) * D],
                                                      in_=wstg[:, b * 1024:(b + 1) * 1024]))
                    cst.append(s_cast.n)
                w3rdy = s_cast.n

                w3tt_mm = []
                w3tt_ev = []
                y_st = []
                g_mm = []
                g_ev = []
                for tt in range(TT):
                    sq = tt // 16
                    ttl = tt % 16
                    b = tt % 2

                    @blk.sync
                    def _(sync, tt=tt, b=b):
                        if tt == 0:
                            wait_stores_all(sync)
                        if tt >= 2:
                            sync.wait_ge(s_mm.h, w3tt_mm[tt - 2])
                        s_xs[b].inc(sync.dma_start(
                            out=g2b[:, b * 32 * 128:(b + 1) * 32 * 128].rearrange(
                                "p (c t) -> p c t", t=128),
                            in_=g2_sc[tt].rearrange("c p t -> p c t")), 16)
                    gl = s_xs[b].n

                    @blk.scalar
                    def _(scalar, tt=tt, sq=sq, ttl=ttl, b=b):
                        if tt == 0:
                            wait_stores_all(scalar)
                        if tt >= 2:
                            scalar.wait_ge(s_dve.h, w3tt_ev[tt - 2])
                        s_crs[b].inc(scalar.dma_start(
                            out=resb[:, b * 2 * D:b * 2 * D + D],
                            in_=x_h[sq, ttl * 128:(ttl + 1) * 128, :]), 16)
                        s_crs[b].inc(scalar.dma_start(
                            out=resb[:, b * 2 * D + D:b * 2 * D + 2 * D],
                            in_=attn_sc[sq, ttl * 128:(ttl + 1) * 128, :]), 16)
                    rl = s_crs[b].n

                    for oh in range(2):
                        n = len(g_mm)
                        pb = b * 2 + oh

                        @blk.tensor
                        def _(tensor, tt=tt, oh=oh, b=b, pb=pb, gl=gl, n=n):
                            tensor.wait_ge(s_xs[b].h, gl)
                            if tt == 0 and oh == 0:
                                tensor.wait_ge(s_cast.h, w3rdy)
                            if n >= 4:
                                tensor.wait_ge(s_dve.h, g_ev[n - 4])
                            for ich in range(32):
                                inst = tensor.matmul(
                                    ps3[:, pb * 512:(pb + 1) * 512],
                                    g2b[:, b * 32 * 128 + ich * 128:b * 32 * 128 + (ich + 1) * 128],
                                    w3r[:, ich * D + oh * 512:ich * D + (oh + 1) * 512],
                                    start=(ich == 0), stop=(ich == 31))
                            s_mm.inc(inst)
                        g_mm.append(s_mm.n)
                        mmn = s_mm.n

                        @blk.vector
                        def _(vector, tt=tt, oh=oh, b=b, pb=pb, mmn=mmn, rl=rl):
                            vector.wait_ge(s_mm.h, mmn)
                            vector.wait_ge(s_crs[b].h, rl)
                            if tt >= 2:
                                vector.wait_ge(s_ys[b].h, y_st[tt - 2])
                            sl = slice(oh * 512, (oh + 1) * 512)
                            dsl = slice(b * D + oh * 512, b * D + (oh + 1) * 512)
                            s_dve.inc(vector.scalar_tensor_tensor(
                                out=outb3[:, dsl], in0=ps3[:, pb * 512:(pb + 1) * 512],
                                scalar=1.0, in1=w3s_bc[:, sl], op0=ALU.bypass, op1=ALU.mult))
                            vector.wait_ge(s_dve.h, s_dve.n)
                            s_dve.inc(vector.tensor_tensor(
                                out=outb3[:, dsl], in0=outb3[:, dsl],
                                in1=resb[:, b * 2 * D + D + oh * 512:b * 2 * D + D + (oh + 1) * 512],
                                op=ALU.add))
                            vector.wait_ge(s_dve.h, s_dve.n)
                            s_dve.inc(vector.tensor_tensor(
                                out=outb3[:, dsl], in0=outb3[:, dsl],
                                in1=resb[:, b * 2 * D + oh * 512:b * 2 * D + (oh + 1) * 512],
                                op=ALU.add))
                        g_ev.append(s_dve.n)
                    w3tt_mm.append(s_mm.n)
                    w3tt_ev.append(s_dve.n)
                    evn = s_dve.n

                    @blk.gpsimd
                    def _(gpsimd, tt=tt, sq=sq, ttl=ttl, b=b, evn=evn):
                        gpsimd.wait_ge(s_dve.h, evn)
                        s_ys[b].inc(gpsimd.dma_start(out=y_h[sq, ttl * 128:(ttl + 1) * 128, :],
                                                     in_=outb3[:, b * D:(b + 1) * D]), 16)
                    y_st.append(s_ys[b].n)

                @blk.gpsimd
                def _(gpsimd):
                    gpsimd.wait_ge(s_ys[0].h, s_ys[0].n)
                    gpsimd.wait_ge(s_ys[1].h, s_ys[1].n)

    return nc, dbg


def host_prep(inputs):
    f = np.ascontiguousarray
    W_qkv = inputs["L0_attn_W_qkv"]
    b_qkv = inputs["L0_attn_b_qkv"]
    d = {
        "wqkv_t": f(W_qkv.reshape(D, 24, 128).transpose(1, 0, 2)),
        "bq_q": f(b_qkv[0:D].reshape(8, 128).T),
        "bq_k": f(b_qkv[D:2 * D].reshape(8, 128).T),
        "bq_v": f(b_qkv[2 * D:3 * D].reshape(1, D)),
        "wout": f(inputs["L0_attn_W_out"]),
        "bout": f(inputs["L0_attn_b_out"].reshape(1, D)),
        "w1t": f(inputs["L0_ffn_w1_w"].T),
        "w2t": f(inputs["L0_ffn_w2_w"].T),
        "w1s_t": f(inputs["L0_ffn_w1_s"].reshape(32, 128).T),
        "w2s_t": f(inputs["L0_ffn_w2_s"].reshape(32, 128).T),
        "h1t": f(inputs["L0_ffn_h1_w"].T.reshape(DH, 32, 128).transpose(1, 0, 2)),
        "h2t": f(inputs["L0_ffn_h2_w"].T.reshape(DH, 32, 128).transpose(1, 0, 2)),
        "h1s_t": f(inputs["L0_ffn_h1_s"].reshape(32, 128).T),
        "h2s_t": f(inputs["L0_ffn_h2_s"].reshape(32, 128).T),
        "w3t": f(inputs["L0_ffn_w3_w"].T),
        "w3s_r": f(inputs["L0_ffn_w3_s"].reshape(1, D)),
        "maskT": np.tril(np.full((128, 128), -10000.0, dtype=np.float32), -1),
        "negones": np.full((1, 128), -1.0, dtype=np.float32),
        "ones64": np.ones((1, 64), dtype=np.float32),
        "sel2": np.concatenate([np.concatenate([np.ones((1, 64)), np.zeros((1, 64))], 1),
                                np.zeros((63, 128)),
                                np.concatenate([np.zeros((1, 64)), np.ones((1, 64))], 1)],
                               0).astype(np.float32),
    }
    ind = np.zeros((128, 128), dtype=ml_dtypes.bfloat16)
    for c in range(8):
        ind[0:64, c * 16 + 2 * c] = 1.0
        ind[64:128, c * 16 + 2 * c + 1] = 1.0
    d["ind8"] = ind
    return d


def kernel(**inputs):
    x = inputs["L0_attn_x"]
    n_seq = x.shape[0] // NCORES
    shared = host_prep(inputs)
    nc, _ = build_program(n_seq=n_seq)
    in_maps = []
    for c in range(NCORES):
        m = dict(shared)
        m["x"] = np.ascontiguousarray(x[c * n_seq:(c + 1) * n_seq])
        in_maps.append(m)
    res = run_bass_kernel_spmd(nc, in_maps, core_ids=list(range(NCORES)))
    return np.concatenate([r["y"] for r in res.results], axis=0)


# version 19
# speedup vs baseline: 1.2053x; 1.2053x over previous
import numpy as np
import ml_dtypes
from contextlib import ExitStack

import concourse.bass as bass
import concourse.mybir as mybir
from concourse.bass_utils import run_bass_kernel_spmd

F32 = mybir.dt.float32
BF16 = mybir.dt.bfloat16
AF = mybir.ActivationFunctionType
ALU = mybir.AluOpType
AX = mybir.AxisListType

S = 2048
D = 1024
NH = 16
HD = 64
DH = 4096
NCORES = 8
H_SLAB_PASSES = (7, 7, 7, 7, 4)


class Sem:
    def __init__(self, es, nc, name):
        self.h = es.enter_context(nc.semaphore(name))
        self.n = 0

    def inc(self, instr, k=1):
        instr.then_inc(self.h, k)
        self.n += k
        return self.n


def build_program(n_seq=4, phases=("prep", "ln", "qkv", "attn", "proj", "w12", "h1", "h2", "w3"),
                  debug=()):
    T = n_seq * S
    TT = T // 128
    nc = bass.Bass("TRN2", target_bir_lowering=False, debug=False)

    x_h = nc.dram_tensor("x", [n_seq, S, D], F32, kind="ExternalInput").ap()
    wqkv_h = nc.dram_tensor("wqkv_t", [24, D, 128], F32, kind="ExternalInput").ap()
    bqQ_h = nc.dram_tensor("bq_q", [128, 8], F32, kind="ExternalInput").ap()
    bqK_h = nc.dram_tensor("bq_k", [128, 8], F32, kind="ExternalInput").ap()
    bqV_h = nc.dram_tensor("bq_v", [1, D], F32, kind="ExternalInput").ap()
    wout_h = nc.dram_tensor("wout", [D, D], F32, kind="ExternalInput").ap()
    bout_h = nc.dram_tensor("bout", [1, D], F32, kind="ExternalInput").ap()
    w1t_h = nc.dram_tensor("w1t", [D, DH], F32, kind="ExternalInput").ap()
    w2t_h = nc.dram_tensor("w2t", [D, DH], F32, kind="ExternalInput").ap()
    w1s_h = nc.dram_tensor("w1s_t", [128, 32], F32, kind="ExternalInput").ap()
    w2s_h = nc.dram_tensor("w2s_t", [128, 32], F32, kind="ExternalInput").ap()
    h1t_h = nc.dram_tensor("h1t", [32, DH, 128], F32, kind="ExternalInput").ap()
    h2t_h = nc.dram_tensor("h2t", [32, DH, 128], F32, kind="ExternalInput").ap()
    h1s_h = nc.dram_tensor("h1s_t", [128, 32], F32, kind="ExternalInput").ap()
    h2s_h = nc.dram_tensor("h2s_t", [128, 32], F32, kind="ExternalInput").ap()
    w3t_h = nc.dram_tensor("w3t", [DH, D], F32, kind="ExternalInput").ap()
    w3s_h = nc.dram_tensor("w3s_r", [1, D], F32, kind="ExternalInput").ap()
    maskT_h = nc.dram_tensor("maskT", [128, 128], F32, kind="ExternalInput").ap()
    negones_h = nc.dram_tensor("negones", [1, 128], F32, kind="ExternalInput").ap()
    ones64_h = nc.dram_tensor("ones64", [1, 64], F32, kind="ExternalInput").ap()
    sel2_h = nc.dram_tensor("sel2", [65, 128], F32, kind="ExternalInput").ap()
    ind8_h = nc.dram_tensor("ind8", [128, 128], BF16, kind="ExternalInput").ap()

    y_h = nc.dram_tensor("y", [n_seq, S, D], F32, kind="ExternalOutput").ap()

    xn_sc = nc.dram_tensor("xn_sc", [8, 128, T], BF16, kind="Internal").ap()
    xn_tm_sc = nc.dram_tensor("xn_tm_sc", [n_seq, S, D], BF16, kind="Internal").ap()
    a_sc = nc.dram_tensor("a_sc", [32, 128, T], BF16, kind="Internal").ap()
    g1_sc = nc.dram_tensor("g1_sc", [32, 128, T], BF16, kind="Internal").ap()
    g2_sc = nc.dram_tensor("g2_sc", [TT, 32, 128, 128], BF16, kind="Internal").ap()
    attn_sc = nc.dram_tensor("attn_sc", [n_seq, S, D], F32, kind="Internal").ap()
    csc_h = nc.dram_tensor("csc", [n_seq, 16], F32, kind="Internal").ap()

    dbg = {}
    if "xn" in debug:
        dbg["xn"] = nc.dram_tensor("dbg_xn", [8, 128, T], BF16, kind="ExternalOutput").ap()
    if "qkv" in debug:
        dbg["q"] = nc.dram_tensor("dbg_q", [8, 128, S], BF16, kind="ExternalOutput").ap()
        dbg["k"] = nc.dram_tensor("dbg_k", [8, 128, S], BF16, kind="ExternalOutput").ap()
        dbg["v"] = nc.dram_tensor("dbg_v", [16, 128, NH * 65], BF16, kind="ExternalOutput").ap()
        dbg["ssq"] = nc.dram_tensor("dbg_ssq", [16, 2 * S], F32, kind="ExternalOutput").ap()
    if "vals" in debug:
        dbg["vals"] = nc.dram_tensor("dbg_vals", [8, 128, S], BF16, kind="ExternalOutput").ap()
    if "attn" in debug:
        dbg["attn"] = nc.dram_tensor("dbg_attn", [n_seq, S, D], F32, kind="ExternalOutput").ap()
    if "a" in debug:
        dbg["a"] = nc.dram_tensor("dbg_a", [32, 128, T], BF16, kind="ExternalOutput").ap()
    if "g2" in debug:
        dbg["g2"] = nc.dram_tensor("dbg_g2", [TT, 32, 128, 128], BF16, kind="ExternalOutput").ap()

    es = ExitStack()
    with es:
        maskT = es.enter_context(nc.sbuf_tensor("maskT_s", [128, 128], F32))
        negones = es.enter_context(nc.sbuf_tensor("negones_s", [1, 128], F32))
        ones64 = es.enter_context(nc.sbuf_tensor("ones64_s", [1, 64], F32))
        sel2 = es.enter_context(nc.sbuf_tensor("sel2_s", [65, 128], F32))
        ind8 = es.enter_context(nc.sbuf_tensor("ind8_s", [128, 128], BF16))
        bq8 = es.enter_context(nc.sbuf_tensor("bq8", [128, 16], F32))
        bv_bc = es.enter_context(nc.sbuf_tensor("bv_bc", [128, D], F32))
        bout_bc = es.enter_context(nc.sbuf_tensor("bout_bc", [128, D], F32))
        w3s_bc = es.enter_context(nc.sbuf_tensor("w3s_bc", [128, D], F32))
        scl4 = es.enter_context(nc.sbuf_tensor("scl4", [128, 128], F32))
        epsv = es.enter_context(nc.sbuf_tensor("epsv", [128, 1], F32))

        s_cast = Sem(es, nc, "s_cast")
        s_mm = Sem(es, nc, "s_mm")
        s_ev = Sem(es, nc, "s_ev")
        s_act = Sem(es, nc, "s_act")
        s_dve = Sem(es, nc, "s_dve")
        s_pool = Sem(es, nc, "s_pool")
        s_prep = Sem(es, nc, "s_prep")
        s_xs = [Sem(es, nc, "s_x0"), Sem(es, nc, "s_x1")]
        s_ws = [Sem(es, nc, "s_w0"), Sem(es, nc, "s_w1")]
        s_sts = [Sem(es, nc, "s_st0"), Sem(es, nc, "s_st1"),
                 Sem(es, nc, "s_st2"), Sem(es, nc, "s_st3")]
        s_xts = [Sem(es, nc, "s_xt0"), Sem(es, nc, "s_xt1"),
                 Sem(es, nc, "s_xt2"), Sem(es, nc, "s_xt3")]
        s_x4 = s_xs + [Sem(es, nc, "s_x2"), Sem(es, nc, "s_x3")]
        s_xtp = [Sem(es, nc, "s_xtp0"), Sem(es, nc, "s_xtp1"),
                 Sem(es, nc, "s_xtp2"), Sem(es, nc, "s_xtp3")]
        s_crs = [Sem(es, nc, "s_cr0"), Sem(es, nc, "s_cr1")]
        s_ys = [Sem(es, nc, "s_y0"), Sem(es, nc, "s_y1")]

        def wait_stores_all(eng):
            for q in s_sts:
                eng.wait_ge(q.h, q.n)

        if "prep" in phases:
            with nc.Block() as blk, (
                nc.sbuf_tensor("prep_row", [1, 3 * D], F32)) as prow, (
                nc.psum_tensor("prep_ps", [128, 512], F32)) as pps:

                @blk.gpsimd
                def _(gpsimd):
                    gpsimd.memset(epsv[:, :], 1e-5)

                @blk.sync
                def _(sync):
                    for dst, src in [
                        (maskT[:, :], maskT_h[:, :]), (negones[:, :], negones_h[:, :]),
                        (ones64[:, :], ones64_h[:, :]), (ind8[:, :], ind8_h[:, :]),
                        (sel2[:, :], sel2_h[:, :]),
                        (bq8[:, 0:8], bqQ_h[:, :]), (bq8[:, 8:16], bqK_h[:, :]),
                        (scl4[:, 0:32], w1s_h[:, :]), (scl4[:, 32:64], w2s_h[:, :]),
                        (scl4[:, 64:96], h1s_h[:, :]), (scl4[:, 96:128], h2s_h[:, :]),
                        (prow[:, 0:D], bqV_h[:, :]), (prow[:, D:2 * D], bout_h[:, :]),
                        (prow[:, 2 * D:3 * D], w3s_h[:, :]),
                    ]:
                        s_prep.inc(sync.dma_start(out=dst, in_=src), 16)
                ld_all = s_prep.n

                @blk.scalar
                def _(scalar):
                    scalar.wait_ge(s_prep.h, ld_all)
                    s_act.inc(scalar.activation(out=bq8[:, 0:8], in_=bq8[:, 0:8],
                                                func=AF.Copy, bias=0.0, scale=0.125))

                bc_ev = []
                for j, dst in enumerate([bv_bc, bout_bc, w3s_bc]):
                    for half in range(2):
                        @blk.tensor
                        def _(tensor, j=j, half=half):
                            tensor.wait_ge(s_prep.h, ld_all)
                            if bc_ev:
                                tensor.wait_ge(s_dve.h, bc_ev[-1])
                            s_mm.inc(tensor.matmul(
                                pps[:, 0:512], negones[:, :],
                                prow[:, j * D + half * 512:j * D + (half + 1) * 512],
                                start=True, stop=True))
                        mmv = s_mm.n

                        @blk.vector
                        def _(vector, dst=dst, half=half, mmv=mmv):
                            vector.wait_ge(s_mm.h, mmv)
                            s_dve.inc(vector.tensor_scalar(
                                out=dst[:, half * 512:(half + 1) * 512],
                                in0=pps[:, 0:512], scalar1=-1.0, scalar2=None, op0=ALU.mult))
                        bc_ev.append(s_dve.n)

        p1 = ExitStack()
        big_act = p1.enter_context(nc.sbuf_tensor("big_act", [128, 8 * S], BF16))
        q_fm = p1.enter_context(nc.sbuf_tensor("q_fm", [128, 8 * S], BF16))
        k_fm = p1.enter_context(nc.sbuf_tensor("k_fm", [128, 8 * S], BF16))
        vplus = p1.enter_context(nc.sbuf_tensor("vplus", [128, 16 * NH * 65], BF16))
        sumsq = p1.enter_context(nc.sbuf_tensor("sumsq", [16, 2 * S], BF16))

        for s in range(n_seq):
            if "ln" in phases:
                with nc.Block() as blk, (
                    nc.sbuf_tensor(f"xt_{s}", [128, 4 * D], F32)) as xt, (
                    nc.sbuf_tensor(f"xsq_{s}", [128, D], F32)) as xsq, (
                    nc.sbuf_tensor(f"lnv_{s}", [128, 32], F32)) as lnv, (
                    nc.sbuf_tensor(f"xn16_{s}", [128, 4 * D], BF16)) as xn16, (
                    nc.sbuf_tensor(f"wstg_{s}", [128, 2 * 1024], F32)) as wstg, (
                    nc.sbuf_tensor(f"wq16_{s}", [128, 2 * 1024], BF16)) as wq16, (
                    nc.sbuf_tensor(f"wv16_{s}", [128, 2 * 1024], BF16)) as wv16, (
                    nc.sbuf_tensor(f"qsq_{s}", [128, S], BF16)) as qsq, (
                    nc.psum_tensor(f"psqk_{s}", [128, 2 * 512], F32)) as psqk, (
                    nc.psum_tensor(f"pssq_{s}", [16, 4 * 512], F32)) as pssq, (
                    nc.psum_tensor(f"psv_{s}", [128, 2 * 512], F32)) as psv:
                    act0 = s_act.n
                    dve0 = s_dve.n
                    xs0 = [q.n for q in s_x4]
                    xts0 = [q.n for q in s_xts]
                    xtp0 = [q.n for q in s_xtp]
                    xl_cf = lambda tt: xs0[tt % 4] + 16 * (tt // 4 + 1)
                    sqd_cf = lambda tt: act0 + 3 * tt + 1
                    sdd_cf = lambda tt: act0 + 3 * tt + 2
                    xnd_cf = lambda tt: act0 + 3 * tt + 3
                    vard_cf = lambda tt: dve0 + 7 * tt + 5
                    nmd_cf = lambda tt: dve0 + 7 * tt + 7
                    trp_cf = lambda tt: xtp0[tt % 4] + 16 * (tt // 4 + 1)

                    def emit_xload(sync, tts):
                        for tt in tts:
                            b = tt % 4
                            if tt >= 4:
                                sync.wait_ge(s_act.h, xnd_cf(tt - 4))
                            s_x4[b].inc(sync.dma_start(
                                out=xt[:, b * D:(b + 1) * D],
                                in_=x_h[s, tt * 128:(tt + 1) * 128, :]), 16)
                            assert s_x4[b].n == xl_cf(tt)

                    def emit_transposes(sync, half):
                        for sl4 in range(4):
                            sync.wait_ge(s_xtp[sl4].h,
                                         xtp0[sl4] + 16 * (2 * half + 2))
                        for ich in range(8):
                            s_xts[half].inc(sync.dma_start_transpose(
                                out=big_act[:, ich * S + half * 1024:
                                            ich * S + (half + 1) * 1024],
                                in_=xn_tm_sc[s, half * 1024:(half + 1) * 1024,
                                             ich * 128:(ich + 1) * 128]), 16)

                    if "qkv" not in phases:
                        @blk.sync
                        def _(sync):
                            emit_xload(sync, range(16))

                    @blk.scalar
                    def _(scalar):
                        for tt in range(16):
                            b = tt % 4
                            b2 = tt % 2
                            lv = b * 8
                            scalar.wait_ge(s_x4[b].h, xl_cf(tt))
                            if tt >= 1:
                                scalar.wait_ge(s_dve.h, dve0 + 7 * (tt - 1) + 2)
                            s_act.inc(scalar.activation(out=xsq[:, :],
                                                        in_=xt[:, b * D:(b + 1) * D],
                                                        func=AF.Square, bias=0.0, scale=1.0))
                            scalar.wait_ge(s_dve.h, vard_cf(tt))
                            s_act.inc(scalar.activation(out=lnv[:, lv + 6:lv + 7],
                                                        in_=lnv[:, lv + 4:lv + 5],
                                                        func=AF.Sqrt, bias=epsv[:, 0:1],
                                                        scale=-1.0))
                            scalar.wait_ge(s_dve.h, nmd_cf(tt))
                            if tt >= 4:
                                scalar.wait_ge(s_xtp[b].h, trp_cf(tt - 4))
                            s_act.inc(scalar.activation(out=xn16[:, b * D:(b + 1) * D],
                                                        in_=xt[:, b * D:(b + 1) * D],
                                                        func=AF.Identity,
                                                        bias=lnv[:, lv + 4:lv + 5],
                                                        scale=lnv[:, lv + 7:lv + 8]))
                            assert s_act.n == xnd_cf(tt)

                    @blk.vector
                    def _(vector):
                        for tt in range(16):
                            b = tt % 4
                            b2 = tt % 2
                            lv = b * 8
                            vector.wait_ge(s_x4[b].h, xl_cf(tt))
                            s_dve.inc(vector.reduce_sum(out=lnv[:, lv:lv + 1],
                                                        in_=xt[:, b * D:(b + 1) * D], axis=AX.X))
                            vector.wait_ge(s_act.h, sqd_cf(tt))
                            s_dve.inc(vector.reduce_sum(out=lnv[:, lv + 1:lv + 2],
                                                        in_=xsq[:, :], axis=AX.X))
                            vector.wait_ge(s_dve.h, s_dve.n - 1)
                            s_dve.inc(vector.tensor_scalar(out=lnv[:, lv + 2:lv + 3],
                                                           in0=lnv[:, lv:lv + 1],
                                                           scalar1=1.0 / D, scalar2=None,
                                                           op0=ALU.mult))
                            vector.wait_ge(s_dve.h, s_dve.n)
                            s_dve.inc(vector.tensor_scalar(out=lnv[:, lv + 3:lv + 4],
                                                           in0=lnv[:, lv + 1:lv + 2],
                                                           scalar1=1.0 / D, scalar2=None,
                                                           op0=ALU.mult))
                            vector.wait_ge(s_dve.h, s_dve.n)
                            s_dve.inc(vector.scalar_tensor_tensor(
                                out=lnv[:, lv + 4:lv + 5], in0=lnv[:, lv + 2:lv + 3],
                                scalar=lnv[:, lv + 2:lv + 3],
                                in1=lnv[:, lv + 3:lv + 4], op0=ALU.mult, op1=ALU.subtract))
                            assert s_dve.n == vard_cf(tt)
                            vector.wait_ge(s_act.h, sdd_cf(tt))
                            s_dve.inc(vector.reciprocal(out=lnv[:, lv + 7:lv + 8],
                                                        in_=lnv[:, lv + 6:lv + 7]))
                            vector.wait_ge(s_dve.h, s_dve.n)
                            s_dve.inc(vector.scalar_tensor_tensor(
                                out=lnv[:, lv + 4:lv + 5], in0=lnv[:, lv + 2:lv + 3], scalar=-1.0,
                                in1=lnv[:, lv + 7:lv + 8], op0=ALU.mult, op1=ALU.mult))
                            assert s_dve.n == nmd_cf(tt)


                    if "qkv" in phases:
                        mm0 = s_mm.n
                        ev0 = s_ev.n
                        actB = s_act.n
                        dveB = s_dve.n
                        cast0 = s_cast.n
                        ws0 = [s_ws[0].n, s_ws[1].n]
                        wld_cf = lambda och: ws0[och % 2] + 16 * (och // 2 + 1)
                        wcst_cf = lambda och: cast0 + och + 1
                        mmg_cf = lambda g: mm0 + g + 1
                        qkmm_cf = lambda och: mm0 + 4 * (och + 1)
                        qkev_cf = lambda g: ev0 + g + 1
                        qkevt_cf = lambda och: ev0 + 4 * (och + 1)
                        sqmm_cf = lambda och: mm0 + 64 + och + 1
                        sqa_cf = lambda och: actB + och + 1
                        vmm_cf = lambda g2: mm0 + 80 + g2 + 1
                        vev_cf = lambda g2: dveB + 2 + g2 + 1

                        def emit_slab_loads(sync, ochs):
                            for och in ochs:
                                b = och % 2
                                if och >= 2:
                                    sync.wait_ge(s_cast.h, wcst_cf(och - 2))
                                s_ws[b].inc(sync.dma_start(
                                    out=wstg[:, b * 1024:(b + 1) * 1024].rearrange(
                                        "p (i o) -> p i o", o=128),
                                    in_=wqkv_h[och].rearrange("(i p) o -> p i o", p=128)), 16)
                                assert s_ws[b].n == wld_cf(och)

                        @blk.sync
                        def _(sync):
                            emit_xload(sync, range(0, 4))
                            emit_slab_loads(sync, range(0, 2))
                            emit_xload(sync, range(4, 12))
                            emit_transposes(sync, 0)
                            emit_slab_loads(sync, range(2, 4))
                            emit_xload(sync, range(12, 16))
                            emit_transposes(sync, 1)
                            emit_slab_loads(sync, range(4, 24))

                        @blk.gpsimd
                        def _(gpsimd):
                            def emit_store(tt):
                                b = tt % 4
                                gpsimd.wait_ge(s_act.h, xnd_cf(tt))
                                s_xtp[b].inc(gpsimd.dma_start(
                                    out=xn_tm_sc[s, tt * 128:(tt + 1) * 128, :],
                                    in_=xn16[:, b * D:(b + 1) * D]), 16)

                            def emit_cast(och):
                                b = och % 2
                                gpsimd.wait_ge(s_ws[b].h, wld_cf(och))
                                if och >= 2:
                                    gpsimd.wait_ge(s_mm.h, qkmm_cf(och - 2))
                                s_cast.inc(gpsimd.tensor_copy(
                                    out=wq16[:, b * 1024:(b + 1) * 1024],
                                    in_=wstg[:, b * 1024:(b + 1) * 1024]))
                                assert s_cast.n == wcst_cf(och)

                            for tt in range(8):
                                emit_store(tt)
                            for och in range(2):
                                emit_cast(och)
                            for tt in range(8, 16):
                                emit_store(tt)
                            for och in range(2, 16):
                                emit_cast(och)

                        @blk.gpsimd
                        def _(gpsimd):
                            for och in range(16, 24):
                                b = och % 2
                                ov = och - 16
                                gpsimd.wait_ge(s_ws[b].h, wld_cf(och))
                                if ov >= 2:
                                    gpsimd.wait_ge(s_mm.h, vmm_cf((ov - 1) * 16 - 1))
                                s_cast.inc(gpsimd.tensor_copy(
                                    out=wv16[:, (ov % 2) * 1024:(ov % 2 + 1) * 1024],
                                    in_=wstg[:, b * 1024:(b + 1) * 1024]))
                                assert s_cast.n == wcst_cf(och)

                        @blk.gpsimd
                        def _(gpsimd):
                            gpsimd.wait_ge(s_xts[0].h, s_xts[0].n)
                            gpsimd.wait_ge(s_xts[1].h, s_xts[1].n)
                            for ich in range(8):
                                s_sts[ich % 2].inc(gpsimd.dma_start(
                                    out=xn_sc[ich, :, s * S:(s + 1) * S],
                                    in_=big_act[:, ich * S:(ich + 1) * S]), 16)
                                if "xn" in debug:
                                    s_sts[ich % 2].inc(gpsimd.dma_start(
                                        out=dbg["xn"][ich, :, s * S:(s + 1) * S],
                                        in_=big_act[:, ich * S:(ich + 1) * S]), 16)


                        @blk.tensor
                        def _(tensor):
                            for och in range(16):
                                b = och % 2
                                for tb in range(4):
                                    g = och * 4 + tb
                                    pb = g % 2
                                    if och == 0:
                                        half = tb // 2
                                        tensor.wait_ge(s_xts[half].h,
                                                       xts0[half] + 128 * (half + 1) - 128 * half)
                                    if tb == 0:
                                        tensor.wait_ge(s_cast.h, wcst_cf(och))
                                    if g >= 2:
                                        tensor.wait_ge(s_ev.h, qkev_cf(g - 2))
                                    for ich in range(8):
                                        inst = tensor.matmul(
                                            psqk[:, pb * 512:(pb + 1) * 512],
                                            wq16[:, b * 1024 + ich * 128:b * 1024 + (ich + 1) * 128],
                                            big_act[:, ich * S + tb * 512:ich * S + (tb + 1) * 512],
                                            start=(ich == 0), stop=(ich == 7))
                                    s_mm.inc(inst)
                                    assert s_mm.n == mmg_cf(g)
                            for och in range(16):
                                c = och % 8
                                tensor.wait_ge(s_act.h, sqa_cf(och))
                                if och == 8:
                                    tensor.wait_ge(s_dve.h, dveB + 1)
                                for qb in range(4):
                                    inst = tensor.matmul(
                                        pssq[:, qb * 512:(qb + 1) * 512],
                                        ind8[:, c * 16:c * 16 + 16],
                                        qsq[:, qb * 512:(qb + 1) * 512],
                                        start=(c == 0), stop=(c == 7))
                                s_mm.inc(inst)
                                assert s_mm.n == sqmm_cf(och)
                            for ov in range(8):
                                tensor.wait_ge(s_cast.h, wcst_cf(16 + ov))
                                for tt in range(16):
                                    g2 = ov * 16 + tt
                                    pb = g2 % 2
                                    if g2 >= 2:
                                        tensor.wait_ge(s_dve.h, vev_cf(g2 - 2))
                                    for ich in range(8):
                                        inst = tensor.matmul(
                                            psv[:, pb * 512:pb * 512 + 128],
                                            big_act[:, ich * S + tt * 128:ich * S + (tt + 1) * 128],
                                            wv16[:, (ov % 2) * 1024 + ich * 128:
                                                 (ov % 2) * 1024 + (ich + 1) * 128],
                                            start=(ich == 0), stop=(ich == 7))
                                    s_mm.inc(inst)
                                    assert s_mm.n == vmm_cf(g2)

                        @blk.scalar
                        def _(scalar):
                            for och in range(16):
                                is_q = och < 8
                                c = och % 8
                                dst = q_fm if is_q else k_fm
                                for tb in range(4):
                                    g = och * 4 + tb
                                    pb = g % 2
                                    scalar.wait_ge(s_mm.h, mmg_cf(g))
                                    s_ev.inc(scalar.activation(
                                        out=dst[:, c * S + tb * 512:c * S + (tb + 1) * 512],
                                        in_=psqk[:, pb * 512:(pb + 1) * 512],
                                        func=AF.Identity, bias=bq8[:, och:och + 1],
                                        scale=0.125 if is_q else 1.0))
                                    assert s_ev.n == qkev_cf(g)
                            for och in range(16):
                                c = och % 8
                                src_t = q_fm if och < 8 else k_fm
                                scalar.wait_ge(s_ev.h, qkevt_cf(och))
                                if och >= 1:
                                    scalar.wait_ge(s_mm.h, sqmm_cf(och - 1))
                                s_act.inc(scalar.activation(out=qsq[:, :],
                                                            in_=src_t[:, c * S:(c + 1) * S],
                                                            func=AF.Square, bias=0.0, scale=1.0))
                                assert s_act.n == sqa_cf(och)

                        @blk.vector
                        def _(vector):
                            vector.wait_ge(s_mm.h, sqmm_cf(7))
                            for qb in range(4):
                                inst = vector.tensor_copy(
                                    out=sumsq[0:16, qb * 512:(qb + 1) * 512],
                                    in_=pssq[0:16, qb * 512:(qb + 1) * 512])
                            s_dve.inc(inst)
                            vector.wait_ge(s_mm.h, sqmm_cf(15))
                            for qb in range(4):
                                inst = vector.tensor_copy(
                                    out=sumsq[0:16, S + qb * 512:S + (qb + 1) * 512],
                                    in_=pssq[0:16, qb * 512:(qb + 1) * 512])
                            s_dve.inc(inst)
                            for ov in range(8):
                                for tt in range(16):
                                    g2 = ov * 16 + tt
                                    pb = g2 % 2
                                    vector.wait_ge(s_mm.h, vmm_cf(g2))
                                    for j in range(2):
                                        h = 2 * ov + j
                                        inst = vector.tensor_tensor(
                                            out=vplus[:, tt * NH * 65 + h * 65:
                                                      tt * NH * 65 + h * 65 + 64],
                                            in0=psv[:, pb * 512 + j * 64:pb * 512 + (j + 1) * 64],
                                            in1=bv_bc[:, ov * 128 + j * 64:ov * 128 + (j + 1) * 64],
                                            op=ALU.add)
                                    s_dve.inc(inst)
                                    assert s_dve.n == vev_cf(g2)

                        @blk.gpsimd
                        def _(gpsimd):
                            gpsimd.wait_ge(s_dve.h, s_dve.n)
                            for tt in range(16):
                                ap3 = vplus[:, tt * NH * 65:(tt + 1) * NH * 65].rearrange(
                                    "p (h c) -> p h c", c=65)[:, :, 64:65]
                                inst = gpsimd.memset(ap3, 1.0)
                            s_pool.inc(inst)

                        if "qkv" in debug:
                            @blk.sync
                            def _(sync):
                                sync.wait_ge(s_ev.h, s_ev.n)
                                sync.wait_ge(s_dve.h, s_dve.n)
                                sync.wait_ge(s_pool.h, s_pool.n)
                                for ich in range(8):
                                    s_prep.inc(sync.dma_start(out=dbg["q"][ich, :, :],
                                                              in_=q_fm[:, ich * S:(ich + 1) * S]), 16)
                                    s_prep.inc(sync.dma_start(out=dbg["k"][ich, :, :],
                                                              in_=k_fm[:, ich * S:(ich + 1) * S]), 16)
                                for tt in range(16):
                                    s_prep.inc(sync.dma_start(
                                        out=dbg["v"][tt, :, :],
                                        in_=vplus[:, tt * NH * 65:(tt + 1) * NH * 65]), 16)
                                s_prep.inc(sync.dma_start(out=dbg["ssq"][:, :], in_=sumsq[0:16, :]), 16)
                                sync.wait_ge(s_prep.h, s_prep.n)

            es_pr = ExitStack()
            if "proj" in phases:
                wostg = es_pr.enter_context(
                    nc.sbuf_tensor(f"wostg_{s}", [128, D], F32))
                wo16 = es_pr.enter_context(
                    nc.sbuf_tensor(f"wo16_{s}", [128, 7 * D], BF16))
                wo_c0 = s_cast.n
                woc = [wo_c0 + i + 1 for i in range(7)]
                wol = []
            if "attn" in phases:
                with nc.Block() as blk, (
                    nc.sbuf_tensor(f"probsT_{s}", [128, 2 * 16 * 512], BF16)) as probsT, (
                    nc.sbuf_tensor(f"nch_{s}", [128, 16], F32)) as nch, (
                    nc.sbuf_tensor(f"nch2_{s}", [128, 8], F32)) as nch2, (
                    nc.sbuf_tensor(f"vraw_{s}", [128, 2 * 512], BF16)) as vraw, (
                    nc.psum_tensor(f"sumr_{s}", [65, 512], F32)) as sumr, (
                    nc.sbuf_tensor(f"rsum_{s}", [65, 2 * 512], F32)) as rsum, (
                    nc.sbuf_tensor(f"cfl_{s}", [1, 16], F32)) as cfl, (
                    nc.sbuf_tensor(f"mk_{s}", [16, 4], F32)) as mkt, (
                    nc.psum_tensor(f"psT_{s}", [128, 4 * 512], F32)) as psT, (
                    nc.psum_tensor(f"psval_{s}", [128, 2 * 512], F32)) as psval, (
                    nc.psum_tensor(f"psibc_{s}", [128, 512], F32)) as psibc:

                    @blk.vector
                    def _(vector):
                        wait_stores_all(vector)
                        vector.wait_ge(s_pool.h, s_pool.n)
                        s_dve.inc(vector.memset(sumr[:, :], 1.0))
                        s_dve.inc(vector.reduce_max(out=mkt[:, 0:1], in_=sumsq[0:16, 0:S],
                                                    axis=AX.X))
                        s_dve.inc(vector.reduce_max(out=mkt[:, 1:2], in_=sumsq[0:16, S:2 * S],
                                                    axis=AX.X))
                        vector.wait_ge(s_dve.h, s_dve.n)
                        s_dve.inc(vector.tensor_tensor(out=mkt[:, 2:3], in0=mkt[:, 0:1],
                                                       in1=mkt[:, 1:2], op=ALU.mult))
                    cprod = s_dve.n

                    @blk.scalar
                    def _(scalar, cprod=cprod):
                        scalar.wait_ge(s_dve.h, cprod)
                        s_act.inc(scalar.activation(out=mkt[:, 3:4], in_=mkt[:, 2:3],
                                                    func=AF.Sqrt, bias=0.0, scale=1.0))
                    c16 = s_act.n

                    @blk.sync
                    def _(sync, c16=c16):
                        sync.wait_ge(s_act.h, c16)
                        s_crs[0].inc(sync.dma_start(out=csc_h[s, :], in_=mkt[0:16, 3:4]), 16)
                    c_st = s_crs[0].n

                    @blk.sync
                    def _(sync, c_st=c_st):
                        sync.wait_ge(s_crs[0].h, c_st)
                        s_crs[1].inc(sync.dma_start(out=cfl[:, :], in_=csc_h[s, :]), 16)
                    c_ld = s_crs[1].n

                    if "proj" in phases:
                        @blk.sync
                        def _(sync):
                            for ich in range(7):
                                if ich >= 1:
                                    sync.wait_ge(s_cast.h, woc[ich - 1])
                                s_ws[0].inc(sync.dma_start(
                                    out=wostg[:, :],
                                    in_=wout_h[ich * 128:(ich + 1) * 128, :]), 16)
                                wol.append(s_ws[0].n)

                    @blk.tensor
                    def _(tensor, c_ld=c_ld):
                        tensor.wait_ge(s_crs[1].h, c_ld)
                        s_mm.inc(tensor.matmul(psibc[:, 0:16], negones[:, :], cfl[:, :],
                                               start=True, stop=True))
                    nch_mm = s_mm.n

                    @blk.vector
                    def _(vector, nch_mm=nch_mm):
                        vector.wait_ge(s_mm.h, nch_mm)
                        s_dve.inc(vector.tensor_copy(out=nch[:, :], in_=psibc[:, 0:16]))
                        vector.wait_ge(s_dve.h, s_dve.n)
                        for p8 in range(8):
                            s_dve.inc(vector.tensor_tensor(
                                out=nch2[:, p8:p8 + 1], in0=nch[:, 2 * p8:2 * p8 + 1],
                                in1=nch[:, 2 * p8 + 1:2 * p8 + 2], op=ALU.min))
                    nch_ok = s_dve.n

                    actC = s_act.n
                    exp_done = {}
                    _cnt = actC
                    _kg = 0
                    for _qb in range(4):
                        for _p in range(8):
                            for _kb in range(4 * _qb + 4):
                                if _kb < 4 * _qb:
                                    _cnt += 1
                                    exp_done[_kg] = _cnt
                                    exp_done[_kg + 1] = _cnt
                                else:
                                    _cnt += 1
                                    exp_done[_kg] = _cnt
                                    _cnt += 1
                                    exp_done[_kg + 1] = _cnt
                                _kg += 2
                    kg0_of = {}
                    exp_k = {}
                    val_k = {}
                    mm_k = {}
                    msk_k = {}
                    val_ev = {}
                    vraw_d = {}
                    rec_d = {}
                    sumbc = {}
                    pzs = {}
                    kgg = 0
                    NKB = lambda i: 4 * (i // 16) + 4

                    def emit_vraw_pair(pr):
                        @blk.vector
                        def _(vector, pr=pr):
                            iB = 2 * pr + 1
                            vs = pr % 2
                            vector.wait_ge(s_mm.h, val_k[(iB, NKB(iB) - 1)])
                            if pr >= 1:
                                vector.wait_ge(s_dve.h, rec_d[pr - 1])
                            if pr >= 2:
                                vector.wait_ge(s_dve.h, val_ev[pr - 2])

                            vector.tensor_copy(out=sumr[0:1, :],
                                               in_=psval[64:65, 0:512])
                            vector.tensor_copy(out=sumr[64:65, :],
                                               in_=psval[64:65, 512:1024])
                            vector.tensor_copy(
                                out=vraw[0:64, vs * 512:(vs + 1) * 512],
                                in_=psval[0:64, 0:512])
                            s_dve.inc(vector.tensor_copy(
                                out=vraw[64:128, vs * 512:(vs + 1) * 512],
                                in_=psval[0:64, 512:1024]))
                            vraw_d[pr] = s_dve.n
                            if pr >= 2:
                                vector.wait_ge(s_mm.h, sumbc[pr - 2])
                            vector.wait_ge(s_dve.h, vraw_d[pr])
                            s_dve.inc(vector.reciprocal(
                                out=rsum[0:65, (pr % 2) * 512:(pr % 2 + 1) * 512],
                                in_=sumr[0:65, :]))
                            rec_d[pr] = s_dve.n

                    def emit_tail_pair(pr):
                        @blk.tensor
                        def _(tensor, pr=pr):
                            tensor.wait_ge(s_dve.h, rec_d[pr])
                            tensor.wait_ge(s_dve.h,
                                           val_ev[pr - 1] if pr >= 1 else nch_ok)
                            s_mm.inc(tensor.matmul(
                                psibc[:, 0:512], sel2[:, :],
                                rsum[0:65, (pr % 2) * 512:(pr % 2 + 1) * 512],
                                start=True, stop=True))
                            sumbc[pr] = s_mm.n

                        @blk.vector
                        def _(vector, pr=pr):
                            vector.wait_ge(s_mm.h, sumbc[pr])
                            vs = pr % 2
                            qb_t = pr // 8
                            ph = pr % 8
                            s_dve.inc(vector.tensor_tensor(
                                out=big_act[:, ph * S + qb_t * 512:
                                            ph * S + (qb_t + 1) * 512],
                                in0=vraw[:, vs * 512:(vs + 1) * 512],
                                in1=psibc[:, 0:512], op=ALU.mult))
                            val_ev[pr] = s_dve.n

                    def emit_vals(tensor, idx_v, j):
                        vb = idx_v % 2
                        nkbv = NKB(idx_v)
                        if j == 0 and idx_v >= 2:
                            tensor.wait_ge(s_dve.h, vraw_d[idx_v // 2 - 1])
                        h_v = idx_v % 16
                        tensor.wait_ge(s_act.h, exp_done[kg0_of[idx_v] + 2 * j + (idx_v % 2)])
                        s_mm.inc(tensor.matmul(
                            psval[0:65, vb * 512:(vb + 1) * 512],
                            vplus[:, j * NH * 65 + h_v * 65:j * NH * 65 + (h_v + 1) * 65],
                            probsT[:, j * 1024 + vb * 512:j * 1024 + (vb + 1) * 512],
                            start=(j == 0), stop=(j == nkbv - 1)))
                        val_k[(idx_v, j)] = s_mm.n

                    for qb in range(4):
                        nkb = 4 * qb + 4

                        @blk.gpsimd
                        def _(gpsimd, qb=qb):
                            gpsimd.wait_ge(s_mm.h, s_mm.n)
                            inst = None
                            for t in range(2):
                                for i in range(4):
                                    for j in range(i):
                                        kbz = 4 * qb + i
                                        inst = gpsimd.memset(
                                            probsT[:, kbz * 1024 + t * 512 + j * 128:
                                                   kbz * 1024 + t * 512 + (j + 1) * 128],
                                            0.0)
                            s_pool.inc(inst)
                        pzs[qb] = s_pool.n

                        if qb == 0 and "proj" in phases:
                            @blk.gpsimd
                            def _(gpsimd):
                                for ich in range(7):
                                    gpsimd.wait_ge(s_ws[0].h, wol[ich])
                                    s_cast.inc(gpsimd.tensor_copy(
                                        out=wo16[:, ich * D:(ich + 1) * D],
                                        in_=wostg[:, :]))
                                    assert s_cast.n == woc[ich]

                        for p in range(8):
                            iA = qb * 16 + 2 * p
                            iB = iA + 1
                            jA, jB = iA - 2, iB - 2
                            kg0 = kgg
                            kg0_of[iA] = kg0
                            kg0_of[iB] = kg0

                            @blk.tensor
                            def _(tensor, qb=qb, iA=iA, iB=iB, jA=jA, jB=jB,
                                  nkb=nkb, kg0=kg0):
                                for j in range(nkb):
                                    if jA >= 0 and j < NKB(jA):
                                        emit_vals(tensor, jA, j)
                                    if jB >= 0 and j < NKB(jB):
                                        emit_vals(tensor, jB, j)
                                    if kg0 + 2 * j + 1 >= 4:
                                        tensor.wait_ge(s_act.h, exp_done[kg0 + 2 * j - 3])
                                    for t, ii in ((0, iA), (1, iB)):
                                        kg = kg0 + 2 * j + t
                                        slot = kg % 4
                                        base = t * 64
                                        ch = (ii % 16) // 2
                                        s_mm.inc(tensor.matmul(
                                            psT[:, slot * 512:(slot + 1) * 512],
                                            k_fm[base:base + 64,
                                                 ch * S + j * 128:ch * S + (j + 1) * 128],
                                            q_fm[base:base + 64,
                                                 ch * S + qb * 512:ch * S + (qb + 1) * 512],
                                            start=True, stop=True))
                                        mm_k[(ii, j)] = s_mm.n

                            @blk.vector
                            def _(vector, qb=qb, iA=iA, iB=iB, nkb=nkb, kg0=kg0):
                                for kb in range(4 * qb, nkb):
                                    i_diag = kb - 4 * qb
                                    for t, ii in ((0, iA), (1, iB)):
                                        slot = (kg0 + 2 * kb + t) % 4
                                        vector.wait_ge(s_mm.h, mm_k[(ii, kb)])
                                        s_dve.inc(vector.tensor_tensor(
                                            out=psT[:, slot * 512 + i_diag * 128:
                                                    slot * 512 + (i_diag + 1) * 128],
                                            in0=psT[:, slot * 512 + i_diag * 128:
                                                    slot * 512 + (i_diag + 1) * 128],
                                            in1=maskT[:, :], op=ALU.add))
                                        msk_k[(ii, kb)] = s_dve.n

                            kgg += 2 * nkb
                            pr_cur = iA // 2
                            if pr_cur >= 1:
                                emit_vraw_pair(pr_cur - 1)
                            if pr_cur >= 2:
                                emit_tail_pair(pr_cur - 2)

                        @blk.scalar
                        def _(scalar, qb=qb, nkb=nkb):
                            scalar.wait_ge(s_pool.h, pzs[qb])
                            scalar.wait_ge(s_dve.h, nch_ok)
                            for p in range(8):
                                iA = qb * 16 + 2 * p
                                iB = iA + 1
                                kg0 = kg0_of[iA]
                                for kb in range(nkb):
                                    i_diag = kb - 4 * qb
                                    kgA = kg0 + 2 * kb
                                    if i_diag < 0:
                                        slotA = kgA % 4
                                        scalar.wait_ge(s_mm.h, mm_k[(iB, kb)])
                                        if iB >= 2 and kb < NKB(iB - 2):
                                            scalar.wait_ge(s_mm.h, val_k[(iB - 2, kb)])
                                        s_act.inc(scalar.activation(
                                            out=probsT[:, kb * 1024:(kb + 1) * 1024],
                                            in_=psT[:, slotA * 512:(slotA + 2) * 512],
                                            func=AF.Exp, bias=nch2[:, p:p + 1], scale=1.0))
                                        assert s_act.n == exp_done[kgA] == exp_done[kgA + 1]
                                    else:
                                        valid0 = i_diag * 128
                                        for t, ii in ((0, iA), (1, iB)):
                                            kg = kgA + t
                                            slot = kg % 4
                                            scalar.wait_ge(s_dve.h, msk_k[(ii, kb)])
                                            if ii >= 2 and kb < NKB(ii - 2):
                                                scalar.wait_ge(s_mm.h, val_k[(ii - 2, kb)])
                                            s_act.inc(scalar.activation(
                                                out=probsT[:, kb * 1024 + t * 512 + valid0:
                                                           kb * 1024 + (t + 1) * 512],
                                                in_=psT[:, slot * 512 + valid0:(slot + 1) * 512],
                                                func=AF.Exp, bias=nch2[:, p:p + 1], scale=1.0))
                                            assert s_act.n == exp_done[kg]

                    @blk.tensor
                    def _(tensor):
                        for j in range(NKB(62)):
                            emit_vals(tensor, 62, j)
                            emit_vals(tensor, 63, j)
                    emit_vraw_pair(31)
                    emit_tail_pair(30)
                    emit_tail_pair(31)

                    if "vals" in debug:
                        @blk.sync
                        def _(sync):
                            sync.wait_ge(s_dve.h, s_dve.n)
                            for ich in range(8):
                                s_prep.inc(sync.dma_start(out=dbg["vals"][ich, :, :],
                                                          in_=big_act[:, ich * S:(ich + 1) * S]), 16)
                            sync.wait_ge(s_prep.h, s_prep.n)

            if "proj" in phases:
                with nc.Block() as blk, (
                    nc.sbuf_tensor(f"aout_{s}", [128, 2 * D], F32)) as aout, (
                    nc.sbuf_tensor(f"wostgp_{s}", [128, D], F32)) as wostg_p, (
                    nc.sbuf_tensor(f"wo16b_{s}", [128, D], BF16)) as wo16b, (
                    nc.psum_tensor(f"pspr_{s}", [128, 2 * 512], F32)) as pspr:

                    @blk.sync
                    def _(sync):
                        s_ws[1].inc(sync.dma_start(
                            out=wostg_p[:, :],
                            in_=wout_h[7 * 128:8 * 128, :]), 16)
                    wp_l = s_ws[1].n

                    @blk.gpsimd
                    def _(gpsimd, wp_l=wp_l):
                        gpsimd.wait_ge(s_ws[1].h, wp_l)
                        s_cast.inc(gpsimd.tensor_copy(out=wo16b[:, :],
                                                      in_=wostg_p[:, :]))
                    wp_c = s_cast.n

                    pr_ev = []
                    st_done = []
                    for tt in range(16):
                        for oh in range(2):
                            g = tt * 2 + oh
                            pb = g % 2

                            @blk.tensor
                            def _(tensor, tt=tt, oh=oh, g=g, pb=pb):
                                if g == 0:
                                    tensor.wait_ge(s_cast.h, wp_c)
                                if g >= 2:
                                    tensor.wait_ge(s_dve.h, pr_ev[g - 2])
                                for ich in range(8):
                                    wsl = (wo16[:, ich * D + oh * 512:ich * D + (oh + 1) * 512]
                                           if ich < 7 else
                                           wo16b[:, oh * 512:(oh + 1) * 512])
                                    inst = tensor.matmul(
                                        pspr[:, pb * 512:(pb + 1) * 512],
                                        big_act[:, ich * S + tt * 128:ich * S + (tt + 1) * 128],
                                        wsl,
                                        start=(ich == 0), stop=(ich == 7))
                                s_mm.inc(inst)
                            prm = s_mm.n

                            @blk.vector
                            def _(vector, tt=tt, oh=oh, pb=pb, prm=prm):
                                vector.wait_ge(s_mm.h, prm)
                                if tt >= 2:
                                    vector.wait_ge(s_sts[tt % 2].h, st_done[tt - 2])
                                s_dve.inc(vector.tensor_tensor(
                                    out=aout[:, (tt % 2) * D + oh * 512:(tt % 2) * D + (oh + 1) * 512],
                                    in0=pspr[:, pb * 512:(pb + 1) * 512],
                                    in1=bout_bc[:, oh * 512:(oh + 1) * 512], op=ALU.add))
                            pr_ev.append(s_dve.n)

                        pe = s_dve.n

                        @blk.gpsimd
                        def _(gpsimd, tt=tt, pe=pe):
                            gpsimd.wait_ge(s_dve.h, pe)
                            s_sts[tt % 2].inc(gpsimd.dma_start(
                                out=attn_sc[s, tt * 128:(tt + 1) * 128, :],
                                in_=aout[:, (tt % 2) * D:(tt % 2 + 1) * D]), 16)
                            if "attn" in debug:
                                s_sts[tt % 2].inc(gpsimd.dma_start(
                                    out=dbg["attn"][s, tt * 128:(tt + 1) * 128, :],
                                    in_=aout[:, (tt % 2) * D:(tt % 2 + 1) * D]), 16)
                        st_done.append(s_sts[tt % 2].n)

                    @blk.gpsimd
                    def _(gpsimd):
                        wait_stores_all(gpsimd)

            es_pr.close()

        p1.close()

        if "w12" in phases:
            with nc.Block() as blk, (
                nc.sbuf_tensor("w12stg", [128, 2 * 2048], F32)) as wstg, (
                nc.sbuf_tensor("w1r", [128, 8 * DH], BF16)) as w1r, (
                nc.sbuf_tensor("w2r", [128, 8 * DH], BF16)) as w2r, (
                nc.sbuf_tensor("xnb", [128, 2 * 8 * 512], BF16)) as xnb, (
                nc.sbuf_tensor("t1b", [128, 4 * 512], F32)) as t1b, (
                nc.sbuf_tensor("ab", [128, 4 * 512], BF16)) as ab, (
                nc.psum_tensor("ps1", [128, 4 * 512], F32)) as ps1, (
                nc.psum_tensor("ps2", [128, 4 * 512], F32)) as ps2:

                ld = []
                cst = []
                cst0 = s_cast.n
                for hh in range(2):
                    for wi, (wh, wr) in enumerate([(w1t_h, w1r), (w2t_h, w2r)]):
                        for ich in range(8):
                            j = hh * 16 + wi * 8 + ich
                            b = j % 2

                            @blk.sync
                            def _(sync, wh=wh, ich=ich, hh=hh, b=b, j=j):
                                if j >= 2:
                                    sync.wait_ge(s_cast.h, cst[j - 2])
                                s_ws[b].inc(sync.dma_start(
                                    out=wstg[:, b * 2048:(b + 1) * 2048],
                                    in_=wh[ich * 128:(ich + 1) * 128,
                                           hh * 2048:(hh + 1) * 2048]), 16)
                            ld.append(s_ws[b].n)

                            @blk.gpsimd
                            def _(gpsimd, wr=wr, ich=ich, hh=hh, b=b, j=j):
                                gpsimd.wait_ge(s_ws[b].h, ld[j])
                                s_cast.inc(gpsimd.tensor_copy(
                                    out=wr[:, ich * DH + hh * 2048:ich * DH + (hh + 1) * 2048],
                                    in_=wstg[:, b * 2048:(b + 1) * 2048]))
                            cst.append(s_cast.n)
                w_ready = s_cast.n

                mmW = s_mm.n
                actW = s_act.n
                dveW = s_dve.n
                stW = [q.n for q in s_sts]
                mst = 2 if "a" in debug else 1
                mm_cf = lambda g: mmW + g + 1
                sln_cf = lambda g: actW + g + 1
                ev_cf = lambda g: dveW + g + 1
                st_cf = lambda g: stW[g % 4] + 16 * mst * (g // 4 + 1)

                w12_mm = []
                for tb in range(T // 512):
                    b = tb % 2

                    @blk.sync
                    def _(sync, tb=tb, b=b):
                        if tb == 0:
                            wait_stores_all(sync)
                        if tb >= 2:
                            sync.wait_ge(s_mm.h, w12_mm[tb - 2])
                        for ich in range(8):
                            s_xs[b].inc(sync.dma_start(
                                out=xnb[:, b * 8 * 512 + ich * 512:b * 8 * 512 + (ich + 1) * 512],
                                in_=xn_sc[ich, :, tb * 512:(tb + 1) * 512]), 16)
                    xl = s_xs[b].n

                    @blk.tensor
                    def _(tensor, tb=tb, b=b, xl=xl):
                        tensor.wait_ge(s_xs[b].h, xl)
                        for och in range(32):
                            g = tb * 32 + och
                            pb = g % 4
                            if tb == 0:
                                tensor.wait_ge(s_cast.h,
                                               cst0 + (16 if och < 16 else 32))
                            if g >= 4:
                                tensor.wait_ge(s_dve.h, ev_cf(g - 4))
                            for ich in range(8):
                                tensor.matmul(
                                    ps1[:, pb * 512:(pb + 1) * 512],
                                    w1r[:, ich * DH + och * 128:ich * DH + (och + 1) * 128],
                                    xnb[:, b * 8 * 512 + ich * 512:b * 8 * 512 + (ich + 1) * 512],
                                    start=(ich == 0), stop=(ich == 7))
                            for ich in range(8):
                                inst = tensor.matmul(
                                    ps2[:, pb * 512:(pb + 1) * 512],
                                    w2r[:, ich * DH + och * 128:ich * DH + (och + 1) * 128],
                                    xnb[:, b * 8 * 512 + ich * 512:b * 8 * 512 + (ich + 1) * 512],
                                    start=(ich == 0), stop=(ich == 7))
                            s_mm.inc(inst)
                            assert s_mm.n == mm_cf(g)
                    w12_mm.append(s_mm.n)

                    @blk.scalar
                    def _(scalar, tb=tb):
                        for och in range(32):
                            g = tb * 32 + och
                            pb = g % 4
                            scalar.wait_ge(s_mm.h, mm_cf(g))
                            if g >= 4:
                                scalar.wait_ge(s_dve.h, ev_cf(g - 4))
                            s_act.inc(scalar.activation(out=t1b[:, pb * 512:(pb + 1) * 512],
                                                        in_=ps1[:, pb * 512:(pb + 1) * 512],
                                                        func=AF.Silu, bias=0.0,
                                                        scale=scl4[:, och:och + 1]))
                            assert s_act.n == sln_cf(g)

                    @blk.vector
                    def _(vector, tb=tb):
                        for och in range(32):
                            g = tb * 32 + och
                            pb = g % 4
                            vector.wait_ge(s_act.h, sln_cf(g))
                            if g >= 4:
                                vector.wait_ge(s_sts[g % 4].h, st_cf(g - 4))
                            s_dve.inc(vector.scalar_tensor_tensor(
                                out=ab[:, pb * 512:(pb + 1) * 512],
                                in0=ps2[:, pb * 512:(pb + 1) * 512],
                                scalar=scl4[:, 32 + och:33 + och],
                                in1=t1b[:, pb * 512:(pb + 1) * 512],
                                op0=ALU.mult, op1=ALU.mult))
                            assert s_dve.n == ev_cf(g)

                    @blk.gpsimd
                    def _(gpsimd, tb=tb):
                        for och in range(32):
                            g = tb * 32 + och
                            pb = g % 4
                            sb = g % 4
                            gpsimd.wait_ge(s_dve.h, ev_cf(g))
                            s_sts[sb].inc(gpsimd.dma_start(
                                out=a_sc[och, :, tb * 512:(tb + 1) * 512],
                                in_=ab[:, pb * 512:(pb + 1) * 512]), 16)
                            if "a" in debug:
                                s_sts[sb].inc(gpsimd.dma_start(
                                    out=dbg["a"][och, :, tb * 512:(tb + 1) * 512],
                                    in_=ab[:, pb * 512:(pb + 1) * 512]), 16)
                            assert s_sts[sb].n == st_cf(g)

                @blk.gpsimd
                def _(gpsimd):
                    wait_stores_all(gpsimd)

        for li, (src_sc, dst_sc, wh, scol, name) in enumerate([
                (a_sc, g1_sc, h1t_h, 64, "h1"), (g1_sc, g2_sc, h2t_h, 96, "h2")]):
            if name not in phases:
                continue
            max_slabs = max(H_SLAB_PASSES)
            with nc.Block() as blk, (
                nc.sbuf_tensor(f"{name}_wstg", [128, 2 * 1024], F32)) as wstg, (
                nc.sbuf_tensor(f"{name}_wr", [128, 2 * max_slabs * DH], BF16)) as wr, (
                nc.sbuf_tensor(f"{name}_in", [128, 2 * 32 * 512], BF16)) as inb, (
                nc.sbuf_tensor(f"{name}_out", [128, 4 * 512], BF16)) as outb, (
                nc.psum_tensor(f"{name}_ps", [128, 4 * 512], F32)) as ps:

                actH = s_act.n
                stH = [q.n for q in s_sts]
                mst = 2 if (name == "h2" and "g2" in debug) else 1
                ev_cf = lambda g: actH + g + 1
                st_cf = lambda g: stH[g % 4] + 16 * mst * (g // 4 + 1)

                slab0 = 0
                pp_mm = []
                tb_mm = []
                gg = 0
                tbg = 0
                gj = 0
                ld_all = []
                cst_all = []
                cst_by_pass = {}
                pass_off = [0]
                for nslab in H_SLAB_PASSES:
                    pass_off.append(pass_off[-1] + nslab)

                def emit_wload(npass_t, och, sl_t, q4):
                    nonlocal gj
                    wslot_t = (npass_t % 2) * max_slabs * DH
                    b = gj % 2
                    j = gj

                    @blk.sync
                    def _(sync, och=och, q4=q4, b=b, j=j):
                        if j >= 2:
                            sync.wait_ge(s_cast.h, cst_all[j - 2])
                        s_ws[b].inc(sync.dma_start(
                            out=wstg[:, b * 1024:(b + 1) * 1024].rearrange(
                                "p (i o) -> p i o", o=128),
                            in_=wh[och].rearrange("(i p) o -> p i o", p=128)[
                                :, q4 * 8:(q4 + 1) * 8, :]), 16)
                    ld_all.append(s_ws[b].n)

                    @blk.gpsimd
                    def _(gpsimd, sl_t=sl_t, q4=q4, b=b, j=j, npass_t=npass_t,
                          wslot_t=wslot_t):
                        gpsimd.wait_ge(s_ws[b].h, ld_all[j])
                        if npass_t >= 2 and sl_t == 0 and q4 == 0:
                            gpsimd.wait_ge(s_mm.h, pp_mm[npass_t - 2])
                        s_cast.inc(gpsimd.tensor_copy(
                            out=wr[:, wslot_t + sl_t * DH + q4 * 1024:
                                    wslot_t + sl_t * DH + (q4 + 1) * 1024],
                            in_=wstg[:, b * 1024:(b + 1) * 1024]))
                    cst_all.append(s_cast.n)
                    cst_by_pass.setdefault(npass_t, []).append(s_cast.n)
                    gj += 1

                for npass, nslab in enumerate(H_SLAB_PASSES):
                    wslot = (npass % 2) * max_slabs * DH
                    if npass == 0:
                        for sl in range(nslab):
                            for q4 in range(4):
                                emit_wload(0, slab0 + sl, sl, q4)
                    nxt = []
                    if npass + 1 < len(H_SLAB_PASSES):
                        for sl in range(H_SLAB_PASSES[npass + 1]):
                            for q4 in range(4):
                                nxt.append((npass + 1, pass_off[npass + 1] + sl, sl, q4))
                    per_tb = -(-len(nxt) // (T // 512)) if nxt else 0

                    for tb in range(T // 512):
                        b = tb % 2

                        @blk.sync
                        def _(sync, tb=tb, b=b, npass=npass, tbg=tbg):
                            if tb == 0 and npass == 0:
                                wait_stores_all(sync)
                            if tbg >= 2:
                                sync.wait_ge(s_mm.h, tb_mm[tbg - 2])
                            for i8 in range(4):
                                s_xs[b].inc(sync.dma_start(
                                    out=inb[:, b * 32 * 512 + i8 * 8 * 512:
                                            b * 32 * 512 + (i8 + 1) * 8 * 512].rearrange(
                                                "p (c t) -> p c t", t=512),
                                    in_=src_sc.rearrange("c p t -> p c t")[
                                        :, i8 * 8:(i8 + 1) * 8, tb * 512:(tb + 1) * 512]), 16)
                        inl = s_xs[b].n

                        for _k in range(per_tb):
                            if nxt:
                                emit_wload(*nxt.pop(0))

                        @blk.tensor
                        def _(tensor, b=b, inl=inl, nslab=nslab, gg0=gg,
                              wslot=wslot, first=(tb == 0),
                              slab_cst=tuple(cst_by_pass.get(npass, []))):
                            tensor.wait_ge(s_xs[b].h, inl)
                            for sl in range(nslab):
                                g = gg0 + sl
                                pb = g % 4
                                if first:
                                    tensor.wait_ge(s_cast.h, slab_cst[sl * 4 + 3])
                                if g >= 4:
                                    tensor.wait_ge(s_act.h, ev_cf(g - 4))
                                for ich in range(32):
                                    inst = tensor.matmul(
                                        ps[:, pb * 512:(pb + 1) * 512],
                                        wr[:, wslot + sl * DH + ich * 128:
                                           wslot + sl * DH + (ich + 1) * 128],
                                        inb[:, b * 32 * 512 + ich * 512:
                                            b * 32 * 512 + (ich + 1) * 512],
                                        start=(ich == 0), stop=(ich == 31))
                                s_mm.inc(inst)
                        mm_hi = s_mm.n
                        tb_mm.append(mm_hi)

                        @blk.scalar
                        def _(scalar, nslab=nslab, gg0=gg, mm_hi=mm_hi, scol=scol, slab0=slab0):
                            for sl in range(nslab):
                                g = gg0 + sl
                                och = slab0 + sl
                                pb = g % 4
                                scalar.wait_ge(s_mm.h, mm_hi - nslab + sl + 1)
                                if g >= 4:
                                    scalar.wait_ge(s_sts[g % 4].h, st_cf(g - 4))
                                s_act.inc(scalar.activation(
                                    out=outb[:, pb * 512:(pb + 1) * 512],
                                    in_=ps[:, pb * 512:(pb + 1) * 512],
                                    func=AF.Gelu, bias=0.0,
                                    scale=scl4[:, scol + och:scol + och + 1]))
                                assert s_act.n == ev_cf(g)

                        @blk.gpsimd
                        def _(gpsimd, tb=tb, nslab=nslab, gg0=gg, slab0=slab0, name=name,
                              dst_sc=dst_sc):
                            for sl in range(nslab):
                                g = gg0 + sl
                                pb = g % 4
                                sb = g % 4
                                och = slab0 + sl
                                gpsimd.wait_ge(s_act.h, ev_cf(g))
                                if name == "h1":
                                    s_sts[sb].inc(gpsimd.dma_start(
                                        out=dst_sc[och, :, tb * 512:(tb + 1) * 512],
                                        in_=outb[:, pb * 512:(pb + 1) * 512]), 16)
                                else:
                                    s_sts[sb].inc(gpsimd.dma_start(
                                        out=dst_sc[tb * 4:(tb + 1) * 4, och, :, :].rearrange(
                                            "t p o -> p t o"),
                                        in_=outb[:, pb * 512:(pb + 1) * 512].rearrange(
                                            "p (t o) -> p t o", o=128)), 16)
                                    if "g2" in debug:
                                        s_sts[sb].inc(gpsimd.dma_start(
                                            out=dbg["g2"][tb * 4:(tb + 1) * 4, och, :, :].rearrange(
                                                "t p o -> p t o"),
                                            in_=outb[:, pb * 512:(pb + 1) * 512].rearrange(
                                                "p (t o) -> p t o", o=128)), 16)
                                assert s_sts[sb].n == st_cf(g)

                        gg += nslab
                        tbg += 1
                    assert not nxt
                    pp_mm.append(s_mm.n)
                    slab0 += nslab

                @blk.gpsimd
                def _(gpsimd):
                    wait_stores_all(gpsimd)

        if "w3" in phases:
            with nc.Block() as blk, (
                nc.sbuf_tensor("w3stg", [128, 2 * 1024], F32)) as wstg, (
                nc.sbuf_tensor("w3r", [128, 32 * D], BF16)) as w3r, (
                nc.sbuf_tensor("g2b", [128, 2 * 32 * 128], BF16)) as g2b, (
                nc.sbuf_tensor("resb", [128, 2 * 2 * D], F32)) as resb, (
                nc.sbuf_tensor("outb3", [128, 2 * D], F32)) as outb3, (
                nc.psum_tensor("ps3", [128, 4 * 512], F32)) as ps3:

                ld = []
                cst = []
                for ich in range(32):
                    b = ich % 2

                    @blk.sync
                    def _(sync, ich=ich, b=b):
                        if ich >= 2:
                            sync.wait_ge(s_cast.h, cst[ich - 2])
                        s_ws[b].inc(sync.dma_start(out=wstg[:, b * 1024:(b + 1) * 1024],
                                                   in_=w3t_h[ich * 128:(ich + 1) * 128, :]), 16)
                    ld.append(s_ws[b].n)

                    @blk.gpsimd
                    def _(gpsimd, ich=ich, b=b):
                        gpsimd.wait_ge(s_ws[b].h, ld[ich])
                        s_cast.inc(gpsimd.tensor_copy(out=w3r[:, ich * D:(ich + 1) * D],
                                                      in_=wstg[:, b * 1024:(b + 1) * 1024]))
                    cst.append(s_cast.n)
                w3rdy = s_cast.n

                w3tt_mm = []
                w3tt_ev = []
                y_st = []
                g_mm = []
                g_ev = []
                for tt in range(TT):
                    sq = tt // 16
                    ttl = tt % 16
                    b = tt % 2

                    @blk.sync
                    def _(sync, tt=tt, b=b):
                        if tt == 0:
                            wait_stores_all(sync)
                        if tt >= 2:
                            sync.wait_ge(s_mm.h, w3tt_mm[tt - 2])
                        s_xs[b].inc(sync.dma_start(
                            out=g2b[:, b * 32 * 128:(b + 1) * 32 * 128].rearrange(
                                "p (c t) -> p c t", t=128),
                            in_=g2_sc[tt].rearrange("c p t -> p c t")), 16)
                    gl = s_xs[b].n

                    @blk.scalar
                    def _(scalar, tt=tt, sq=sq, ttl=ttl, b=b):
                        if tt == 0:
                            wait_stores_all(scalar)
                        if tt >= 2:
                            scalar.wait_ge(s_dve.h, w3tt_ev[tt - 2])
                        s_crs[b].inc(scalar.dma_start(
                            out=resb[:, b * 2 * D:b * 2 * D + D],
                            in_=x_h[sq, ttl * 128:(ttl + 1) * 128, :]), 16)
                        s_crs[b].inc(scalar.dma_start(
                            out=resb[:, b * 2 * D + D:b * 2 * D + 2 * D],
                            in_=attn_sc[sq, ttl * 128:(ttl + 1) * 128, :]), 16)
                    rl = s_crs[b].n

                    for oh in range(2):
                        n = len(g_mm)
                        pb = b * 2 + oh

                        @blk.tensor
                        def _(tensor, tt=tt, oh=oh, b=b, pb=pb, gl=gl, n=n):
                            tensor.wait_ge(s_xs[b].h, gl)
                            if tt == 0 and oh == 0:
                                tensor.wait_ge(s_cast.h, w3rdy)
                            if n >= 4:
                                tensor.wait_ge(s_dve.h, g_ev[n - 4])
                            for ich in range(32):
                                inst = tensor.matmul(
                                    ps3[:, pb * 512:(pb + 1) * 512],
                                    g2b[:, b * 32 * 128 + ich * 128:b * 32 * 128 + (ich + 1) * 128],
                                    w3r[:, ich * D + oh * 512:ich * D + (oh + 1) * 512],
                                    start=(ich == 0), stop=(ich == 31))
                            s_mm.inc(inst)
                        g_mm.append(s_mm.n)
                        mmn = s_mm.n

                        @blk.vector
                        def _(vector, tt=tt, oh=oh, b=b, pb=pb, mmn=mmn, rl=rl):
                            vector.wait_ge(s_mm.h, mmn)
                            vector.wait_ge(s_crs[b].h, rl)
                            if tt >= 2:
                                vector.wait_ge(s_ys[b].h, y_st[tt - 2])
                            sl = slice(oh * 512, (oh + 1) * 512)
                            dsl = slice(b * D + oh * 512, b * D + (oh + 1) * 512)
                            s_dve.inc(vector.scalar_tensor_tensor(
                                out=outb3[:, dsl], in0=ps3[:, pb * 512:(pb + 1) * 512],
                                scalar=1.0, in1=w3s_bc[:, sl], op0=ALU.bypass, op1=ALU.mult))
                            vector.wait_ge(s_dve.h, s_dve.n)
                            s_dve.inc(vector.tensor_tensor(
                                out=outb3[:, dsl], in0=outb3[:, dsl],
                                in1=resb[:, b * 2 * D + D + oh * 512:b * 2 * D + D + (oh + 1) * 512],
                                op=ALU.add))
                            vector.wait_ge(s_dve.h, s_dve.n)
                            s_dve.inc(vector.tensor_tensor(
                                out=outb3[:, dsl], in0=outb3[:, dsl],
                                in1=resb[:, b * 2 * D + oh * 512:b * 2 * D + (oh + 1) * 512],
                                op=ALU.add))
                        g_ev.append(s_dve.n)
                    w3tt_mm.append(s_mm.n)
                    w3tt_ev.append(s_dve.n)
                    evn = s_dve.n

                    @blk.gpsimd
                    def _(gpsimd, tt=tt, sq=sq, ttl=ttl, b=b, evn=evn):
                        gpsimd.wait_ge(s_dve.h, evn)
                        s_ys[b].inc(gpsimd.dma_start(out=y_h[sq, ttl * 128:(ttl + 1) * 128, :],
                                                     in_=outb3[:, b * D:(b + 1) * D]), 16)
                    y_st.append(s_ys[b].n)

                @blk.gpsimd
                def _(gpsimd):
                    gpsimd.wait_ge(s_ys[0].h, s_ys[0].n)
                    gpsimd.wait_ge(s_ys[1].h, s_ys[1].n)

    return nc, dbg


def host_prep(inputs):
    f = np.ascontiguousarray
    W_qkv = inputs["L0_attn_W_qkv"]
    b_qkv = inputs["L0_attn_b_qkv"]
    d = {
        "wqkv_t": f(W_qkv.reshape(D, 24, 128).transpose(1, 0, 2)),
        "bq_q": f(b_qkv[0:D].reshape(8, 128).T),
        "bq_k": f(b_qkv[D:2 * D].reshape(8, 128).T),
        "bq_v": f(b_qkv[2 * D:3 * D].reshape(1, D)),
        "wout": f(inputs["L0_attn_W_out"]),
        "bout": f(inputs["L0_attn_b_out"].reshape(1, D)),
        "w1t": f(inputs["L0_ffn_w1_w"].T),
        "w2t": f(inputs["L0_ffn_w2_w"].T),
        "w1s_t": f(inputs["L0_ffn_w1_s"].reshape(32, 128).T),
        "w2s_t": f(inputs["L0_ffn_w2_s"].reshape(32, 128).T),
        "h1t": f(inputs["L0_ffn_h1_w"].T.reshape(DH, 32, 128).transpose(1, 0, 2)),
        "h2t": f(inputs["L0_ffn_h2_w"].T.reshape(DH, 32, 128).transpose(1, 0, 2)),
        "h1s_t": f(inputs["L0_ffn_h1_s"].reshape(32, 128).T),
        "h2s_t": f(inputs["L0_ffn_h2_s"].reshape(32, 128).T),
        "w3t": f(inputs["L0_ffn_w3_w"].T),
        "w3s_r": f(inputs["L0_ffn_w3_s"].reshape(1, D)),
        "maskT": np.tril(np.full((128, 128), -10000.0, dtype=np.float32), -1),
        "negones": np.full((1, 128), -1.0, dtype=np.float32),
        "ones64": np.ones((1, 64), dtype=np.float32),
        "sel2": np.concatenate([np.concatenate([np.ones((1, 64)), np.zeros((1, 64))], 1),
                                np.zeros((63, 128)),
                                np.concatenate([np.zeros((1, 64)), np.ones((1, 64))], 1)],
                               0).astype(np.float32),
    }
    ind = np.zeros((128, 128), dtype=ml_dtypes.bfloat16)
    for c in range(8):
        ind[0:64, c * 16 + 2 * c] = 1.0
        ind[64:128, c * 16 + 2 * c + 1] = 1.0
    d["ind8"] = ind
    return d


def kernel(**inputs):
    x = inputs["L0_attn_x"]
    n_seq = x.shape[0] // NCORES
    shared = host_prep(inputs)
    nc, _ = build_program(n_seq=n_seq)
    in_maps = []
    for c in range(NCORES):
        m = dict(shared)
        m["x"] = np.ascontiguousarray(x[c * n_seq:(c + 1) * n_seq])
        in_maps.append(m)
    res = run_bass_kernel_spmd(nc, in_maps, core_ids=list(range(NCORES)))
    return np.concatenate([r["y"] for r in res.results], axis=0)


# version 20 (final)
# speedup vs baseline: 1.2058x; 1.0004x over previous
import numpy as np
import ml_dtypes
from contextlib import ExitStack

import concourse.bass as bass
import concourse.mybir as mybir
from concourse.bass_utils import run_bass_kernel_spmd

F32 = mybir.dt.float32
BF16 = mybir.dt.bfloat16
AF = mybir.ActivationFunctionType
ALU = mybir.AluOpType
AX = mybir.AxisListType

S = 2048
D = 1024
NH = 16
HD = 64
DH = 4096
NCORES = 8
H_SLAB_PASSES = (7, 7, 7, 7, 4)


class Sem:
    def __init__(self, es, nc, name):
        self.h = es.enter_context(nc.semaphore(name))
        self.n = 0

    def inc(self, instr, k=1):
        instr.then_inc(self.h, k)
        self.n += k
        return self.n


def build_program(n_seq=4, phases=("prep", "ln", "qkv", "attn", "proj", "w12", "h1", "h2", "w3"),
                  debug=()):
    T = n_seq * S
    TT = T // 128
    nc = bass.Bass("TRN2", target_bir_lowering=False, debug=False)

    x_h = nc.dram_tensor("x", [n_seq, S, D], F32, kind="ExternalInput").ap()
    wqkv_h = nc.dram_tensor("wqkv_t", [24, D, 128], F32, kind="ExternalInput").ap()
    bqQ_h = nc.dram_tensor("bq_q", [128, 8], F32, kind="ExternalInput").ap()
    bqK_h = nc.dram_tensor("bq_k", [128, 8], F32, kind="ExternalInput").ap()
    bqV_h = nc.dram_tensor("bq_v", [1, D], F32, kind="ExternalInput").ap()
    wout_h = nc.dram_tensor("wout", [D, D], F32, kind="ExternalInput").ap()
    bout_h = nc.dram_tensor("bout", [1, D], F32, kind="ExternalInput").ap()
    w1t_h = nc.dram_tensor("w1t", [D, DH], F32, kind="ExternalInput").ap()
    w2t_h = nc.dram_tensor("w2t", [D, DH], F32, kind="ExternalInput").ap()
    w1s_h = nc.dram_tensor("w1s_t", [128, 32], F32, kind="ExternalInput").ap()
    w2s_h = nc.dram_tensor("w2s_t", [128, 32], F32, kind="ExternalInput").ap()
    h1t_h = nc.dram_tensor("h1t", [32, DH, 128], F32, kind="ExternalInput").ap()
    h2t_h = nc.dram_tensor("h2t", [32, DH, 128], F32, kind="ExternalInput").ap()
    h1s_h = nc.dram_tensor("h1s_t", [128, 32], F32, kind="ExternalInput").ap()
    h2s_h = nc.dram_tensor("h2s_t", [128, 32], F32, kind="ExternalInput").ap()
    w3t_h = nc.dram_tensor("w3t", [DH, D], F32, kind="ExternalInput").ap()
    w3s_h = nc.dram_tensor("w3s_r", [1, D], F32, kind="ExternalInput").ap()
    maskT_h = nc.dram_tensor("maskT", [128, 128], F32, kind="ExternalInput").ap()
    negones_h = nc.dram_tensor("negones", [1, 128], F32, kind="ExternalInput").ap()
    sel2_h = nc.dram_tensor("sel2", [65, 128], F32, kind="ExternalInput").ap()
    ind8_h = nc.dram_tensor("ind8", [128, 128], BF16, kind="ExternalInput").ap()

    y_h = nc.dram_tensor("y", [n_seq, S, D], F32, kind="ExternalOutput").ap()

    xn_sc = nc.dram_tensor("xn_sc", [8, 128, T], BF16, kind="Internal").ap()
    xn_tm_sc = nc.dram_tensor("xn_tm_sc", [n_seq, S, D], BF16, kind="Internal").ap()
    a_sc = nc.dram_tensor("a_sc", [32, 128, T], BF16, kind="Internal").ap()
    g1_sc = nc.dram_tensor("g1_sc", [32, 128, T], BF16, kind="Internal").ap()
    g2_sc = nc.dram_tensor("g2_sc", [TT, 32, 128, 128], BF16, kind="Internal").ap()
    attn_sc = nc.dram_tensor("attn_sc", [n_seq, S, D], F32, kind="Internal").ap()
    csc_h = nc.dram_tensor("csc", [n_seq, 16], F32, kind="Internal").ap()

    dbg = {}
    if "xn" in debug:
        dbg["xn"] = nc.dram_tensor("dbg_xn", [8, 128, T], BF16, kind="ExternalOutput").ap()
    if "qkv" in debug:
        dbg["q"] = nc.dram_tensor("dbg_q", [8, 128, S], BF16, kind="ExternalOutput").ap()
        dbg["k"] = nc.dram_tensor("dbg_k", [8, 128, S], BF16, kind="ExternalOutput").ap()
        dbg["v"] = nc.dram_tensor("dbg_v", [16, 128, NH * 65], BF16, kind="ExternalOutput").ap()
        dbg["ssq"] = nc.dram_tensor("dbg_ssq", [16, 2 * S], F32, kind="ExternalOutput").ap()
    if "vals" in debug:
        dbg["vals"] = nc.dram_tensor("dbg_vals", [8, 128, S], BF16, kind="ExternalOutput").ap()
    if "attn" in debug:
        dbg["attn"] = nc.dram_tensor("dbg_attn", [n_seq, S, D], F32, kind="ExternalOutput").ap()
    if "a" in debug:
        dbg["a"] = nc.dram_tensor("dbg_a", [32, 128, T], BF16, kind="ExternalOutput").ap()
    if "g2" in debug:
        dbg["g2"] = nc.dram_tensor("dbg_g2", [TT, 32, 128, 128], BF16, kind="ExternalOutput").ap()

    es = ExitStack()
    with es:
        maskT = es.enter_context(nc.sbuf_tensor("maskT_s", [128, 128], F32))
        negones = es.enter_context(nc.sbuf_tensor("negones_s", [1, 128], F32))
        sel2 = es.enter_context(nc.sbuf_tensor("sel2_s", [65, 128], F32))
        ind8 = es.enter_context(nc.sbuf_tensor("ind8_s", [128, 128], BF16))
        bq8 = es.enter_context(nc.sbuf_tensor("bq8", [128, 16], F32))
        bv_bc = es.enter_context(nc.sbuf_tensor("bv_bc", [128, D], F32))
        bout_bc = es.enter_context(nc.sbuf_tensor("bout_bc", [128, D], F32))
        w3s_bc = es.enter_context(nc.sbuf_tensor("w3s_bc", [128, D], F32))
        scl4 = es.enter_context(nc.sbuf_tensor("scl4", [128, 128], F32))
        epsv = es.enter_context(nc.sbuf_tensor("epsv", [128, 1], F32))

        s_cast = Sem(es, nc, "s_cast")
        s_mm = Sem(es, nc, "s_mm")
        s_ev = Sem(es, nc, "s_ev")
        s_act = Sem(es, nc, "s_act")
        s_dve = Sem(es, nc, "s_dve")
        s_pool = Sem(es, nc, "s_pool")
        s_prep = Sem(es, nc, "s_prep")
        s_xs = [Sem(es, nc, "s_x0"), Sem(es, nc, "s_x1")]
        s_ws = [Sem(es, nc, "s_w0"), Sem(es, nc, "s_w1")]
        s_sts = [Sem(es, nc, "s_st0"), Sem(es, nc, "s_st1"),
                 Sem(es, nc, "s_st2"), Sem(es, nc, "s_st3")]
        s_xts = [Sem(es, nc, "s_xt0"), Sem(es, nc, "s_xt1"),
                 Sem(es, nc, "s_xt2"), Sem(es, nc, "s_xt3")]
        s_x4 = s_xs + [Sem(es, nc, "s_x2"), Sem(es, nc, "s_x3")]
        s_xtp = [Sem(es, nc, "s_xtp0"), Sem(es, nc, "s_xtp1"),
                 Sem(es, nc, "s_xtp2"), Sem(es, nc, "s_xtp3")]
        s_crs = [Sem(es, nc, "s_cr0"), Sem(es, nc, "s_cr1")]
        s_ys = [Sem(es, nc, "s_y0"), Sem(es, nc, "s_y1")]

        def wait_stores_all(eng):
            for q in s_sts:
                eng.wait_ge(q.h, q.n)

        if "prep" in phases:
            with nc.Block() as blk, (
                nc.sbuf_tensor("prep_row", [1, 3 * D], F32)) as prow, (
                nc.psum_tensor("prep_ps", [128, 512], F32)) as pps:

                @blk.gpsimd
                def _(gpsimd):
                    gpsimd.memset(epsv[:, :], 1e-5)

                @blk.sync
                def _(sync):
                    for dst, src in [
                        (maskT[:, :], maskT_h[:, :]), (negones[:, :], negones_h[:, :]),
                        (ind8[:, :], ind8_h[:, :]),
                        (sel2[:, :], sel2_h[:, :]),
                        (bq8[:, 0:8], bqQ_h[:, :]), (bq8[:, 8:16], bqK_h[:, :]),
                        (scl4[:, 0:32], w1s_h[:, :]), (scl4[:, 32:64], w2s_h[:, :]),
                        (scl4[:, 64:96], h1s_h[:, :]), (scl4[:, 96:128], h2s_h[:, :]),
                        (prow[:, 0:D], bqV_h[:, :]), (prow[:, D:2 * D], bout_h[:, :]),
                        (prow[:, 2 * D:3 * D], w3s_h[:, :]),
                    ]:
                        s_prep.inc(sync.dma_start(out=dst, in_=src), 16)
                ld_all = s_prep.n

                @blk.scalar
                def _(scalar):
                    scalar.wait_ge(s_prep.h, ld_all)
                    s_act.inc(scalar.activation(out=bq8[:, 0:8], in_=bq8[:, 0:8],
                                                func=AF.Copy, bias=0.0, scale=0.125))

                bc_ev = []
                for j, dst in enumerate([bv_bc, bout_bc, w3s_bc]):
                    for half in range(2):
                        @blk.tensor
                        def _(tensor, j=j, half=half):
                            tensor.wait_ge(s_prep.h, ld_all)
                            if bc_ev:
                                tensor.wait_ge(s_dve.h, bc_ev[-1])
                            s_mm.inc(tensor.matmul(
                                pps[:, 0:512], negones[:, :],
                                prow[:, j * D + half * 512:j * D + (half + 1) * 512],
                                start=True, stop=True))
                        mmv = s_mm.n

                        @blk.vector
                        def _(vector, dst=dst, half=half, mmv=mmv):
                            vector.wait_ge(s_mm.h, mmv)
                            s_dve.inc(vector.tensor_scalar(
                                out=dst[:, half * 512:(half + 1) * 512],
                                in0=pps[:, 0:512], scalar1=-1.0, scalar2=None, op0=ALU.mult))
                        bc_ev.append(s_dve.n)

        p1 = ExitStack()
        big_act = p1.enter_context(nc.sbuf_tensor("big_act", [128, 8 * S], BF16))
        q_fm = p1.enter_context(nc.sbuf_tensor("q_fm", [128, 8 * S], BF16))
        k_fm = p1.enter_context(nc.sbuf_tensor("k_fm", [128, 8 * S], BF16))
        vplus = p1.enter_context(nc.sbuf_tensor("vplus", [128, 16 * NH * 65], BF16))
        sumsq = p1.enter_context(nc.sbuf_tensor("sumsq", [16, 2 * S], BF16))

        for s in range(n_seq):
            if "ln" in phases:
                with nc.Block() as blk, (
                    nc.sbuf_tensor(f"xt_{s}", [128, 4 * D], F32)) as xt, (
                    nc.sbuf_tensor(f"xsq_{s}", [128, D], F32)) as xsq, (
                    nc.sbuf_tensor(f"lnv_{s}", [128, 32], F32)) as lnv, (
                    nc.sbuf_tensor(f"xn16_{s}", [128, 4 * D], BF16)) as xn16, (
                    nc.sbuf_tensor(f"wstg_{s}", [128, 2 * 1024], F32)) as wstg, (
                    nc.sbuf_tensor(f"wq16_{s}", [128, 2 * 1024], BF16)) as wq16, (
                    nc.sbuf_tensor(f"wv16_{s}", [128, 2 * 1024], BF16)) as wv16, (
                    nc.sbuf_tensor(f"qsq_{s}", [128, S], BF16)) as qsq, (
                    nc.psum_tensor(f"psqk_{s}", [128, 2 * 512], F32)) as psqk, (
                    nc.psum_tensor(f"pssq_{s}", [16, 4 * 512], F32)) as pssq, (
                    nc.psum_tensor(f"psv_{s}", [128, 2 * 512], F32)) as psv:
                    act0 = s_act.n
                    dve0 = s_dve.n
                    xs0 = [q.n for q in s_x4]
                    xts0 = [q.n for q in s_xts]
                    xtp0 = [q.n for q in s_xtp]
                    xl_cf = lambda tt: xs0[tt % 4] + 16 * (tt // 4 + 1)
                    sqd_cf = lambda tt: act0 + 3 * tt + 1
                    sdd_cf = lambda tt: act0 + 3 * tt + 2
                    xnd_cf = lambda tt: act0 + 3 * tt + 3
                    vard_cf = lambda tt: dve0 + 7 * tt + 5
                    nmd_cf = lambda tt: dve0 + 7 * tt + 7
                    trp_cf = lambda tt: xtp0[tt % 4] + 16 * (tt // 4 + 1)

                    def emit_xload(sync, tts):
                        for tt in tts:
                            b = tt % 4
                            if tt >= 4:
                                sync.wait_ge(s_act.h, xnd_cf(tt - 4))
                            s_x4[b].inc(sync.dma_start(
                                out=xt[:, b * D:(b + 1) * D],
                                in_=x_h[s, tt * 128:(tt + 1) * 128, :]), 16)
                            assert s_x4[b].n == xl_cf(tt)

                    def emit_transposes(sync, half):
                        for sl4 in range(4):
                            sync.wait_ge(s_xtp[sl4].h,
                                         xtp0[sl4] + 16 * (2 * half + 2))
                        for ich in range(8):
                            s_xts[half].inc(sync.dma_start_transpose(
                                out=big_act[:, ich * S + half * 1024:
                                            ich * S + (half + 1) * 1024],
                                in_=xn_tm_sc[s, half * 1024:(half + 1) * 1024,
                                             ich * 128:(ich + 1) * 128]), 16)

                    if "qkv" not in phases:
                        @blk.sync
                        def _(sync):
                            emit_xload(sync, range(16))

                    @blk.scalar
                    def _(scalar):
                        for tt in range(16):
                            b = tt % 4
                            b2 = tt % 2
                            lv = b * 8
                            scalar.wait_ge(s_x4[b].h, xl_cf(tt))
                            if tt >= 1:
                                scalar.wait_ge(s_dve.h, dve0 + 7 * (tt - 1) + 2)
                            s_act.inc(scalar.activation(out=xsq[:, :],
                                                        in_=xt[:, b * D:(b + 1) * D],
                                                        func=AF.Square, bias=0.0, scale=1.0))
                            scalar.wait_ge(s_dve.h, vard_cf(tt))
                            s_act.inc(scalar.activation(out=lnv[:, lv + 6:lv + 7],
                                                        in_=lnv[:, lv + 4:lv + 5],
                                                        func=AF.Sqrt, bias=epsv[:, 0:1],
                                                        scale=-1.0))
                            scalar.wait_ge(s_dve.h, nmd_cf(tt))
                            if tt >= 4:
                                scalar.wait_ge(s_xtp[b].h, trp_cf(tt - 4))
                            s_act.inc(scalar.activation(out=xn16[:, b * D:(b + 1) * D],
                                                        in_=xt[:, b * D:(b + 1) * D],
                                                        func=AF.Identity,
                                                        bias=lnv[:, lv + 4:lv + 5],
                                                        scale=lnv[:, lv + 7:lv + 8]))
                            assert s_act.n == xnd_cf(tt)

                    @blk.vector
                    def _(vector):
                        for tt in range(16):
                            b = tt % 4
                            b2 = tt % 2
                            lv = b * 8
                            vector.wait_ge(s_x4[b].h, xl_cf(tt))
                            s_dve.inc(vector.reduce_sum(out=lnv[:, lv:lv + 1],
                                                        in_=xt[:, b * D:(b + 1) * D], axis=AX.X))
                            vector.wait_ge(s_act.h, sqd_cf(tt))
                            s_dve.inc(vector.reduce_sum(out=lnv[:, lv + 1:lv + 2],
                                                        in_=xsq[:, :], axis=AX.X))
                            vector.wait_ge(s_dve.h, s_dve.n - 1)
                            s_dve.inc(vector.tensor_scalar(out=lnv[:, lv + 2:lv + 3],
                                                           in0=lnv[:, lv:lv + 1],
                                                           scalar1=1.0 / D, scalar2=None,
                                                           op0=ALU.mult))
                            vector.wait_ge(s_dve.h, s_dve.n)
                            s_dve.inc(vector.tensor_scalar(out=lnv[:, lv + 3:lv + 4],
                                                           in0=lnv[:, lv + 1:lv + 2],
                                                           scalar1=1.0 / D, scalar2=None,
                                                           op0=ALU.mult))
                            vector.wait_ge(s_dve.h, s_dve.n)
                            s_dve.inc(vector.scalar_tensor_tensor(
                                out=lnv[:, lv + 4:lv + 5], in0=lnv[:, lv + 2:lv + 3],
                                scalar=lnv[:, lv + 2:lv + 3],
                                in1=lnv[:, lv + 3:lv + 4], op0=ALU.mult, op1=ALU.subtract))
                            assert s_dve.n == vard_cf(tt)
                            vector.wait_ge(s_act.h, sdd_cf(tt))
                            s_dve.inc(vector.reciprocal(out=lnv[:, lv + 7:lv + 8],
                                                        in_=lnv[:, lv + 6:lv + 7]))
                            vector.wait_ge(s_dve.h, s_dve.n)
                            s_dve.inc(vector.scalar_tensor_tensor(
                                out=lnv[:, lv + 4:lv + 5], in0=lnv[:, lv + 2:lv + 3], scalar=-1.0,
                                in1=lnv[:, lv + 7:lv + 8], op0=ALU.mult, op1=ALU.mult))
                            assert s_dve.n == nmd_cf(tt)


                    if "qkv" in phases:
                        mm0 = s_mm.n
                        ev0 = s_ev.n
                        actB = s_act.n
                        dveB = s_dve.n
                        cast0 = s_cast.n
                        ws0 = [s_ws[0].n, s_ws[1].n]
                        wld_cf = lambda och: ws0[och % 2] + 16 * (och // 2 + 1)
                        wcst_cf = lambda och: cast0 + och + 1
                        mmg_cf = lambda g: mm0 + g + 1
                        qkmm_cf = lambda och: mm0 + 4 * (och + 1)
                        qkev_cf = lambda g: ev0 + g + 1
                        qkevt_cf = lambda och: ev0 + 4 * (och + 1)
                        sqmm_cf = lambda och: mm0 + 64 + och + 1
                        sqa_cf = lambda och: actB + och + 1
                        vmm_cf = lambda g2: mm0 + 80 + g2 + 1
                        vev_cf = lambda g2: dveB + 2 + g2 + 1

                        def emit_slab_loads(sync, ochs):
                            for och in ochs:
                                b = och % 2
                                if och >= 2:
                                    sync.wait_ge(s_cast.h, wcst_cf(och - 2))
                                s_ws[b].inc(sync.dma_start(
                                    out=wstg[:, b * 1024:(b + 1) * 1024].rearrange(
                                        "p (i o) -> p i o", o=128),
                                    in_=wqkv_h[och].rearrange("(i p) o -> p i o", p=128)), 16)
                                assert s_ws[b].n == wld_cf(och)

                        @blk.sync
                        def _(sync):
                            emit_xload(sync, range(0, 4))
                            emit_slab_loads(sync, range(0, 2))
                            emit_xload(sync, range(4, 12))
                            emit_transposes(sync, 0)
                            emit_slab_loads(sync, range(2, 4))
                            emit_xload(sync, range(12, 16))
                            emit_transposes(sync, 1)
                            emit_slab_loads(sync, range(4, 24))

                        @blk.gpsimd
                        def _(gpsimd):
                            def emit_store(tt):
                                b = tt % 4
                                gpsimd.wait_ge(s_act.h, xnd_cf(tt))
                                s_xtp[b].inc(gpsimd.dma_start(
                                    out=xn_tm_sc[s, tt * 128:(tt + 1) * 128, :],
                                    in_=xn16[:, b * D:(b + 1) * D]), 16)

                            def emit_cast(och):
                                b = och % 2
                                gpsimd.wait_ge(s_ws[b].h, wld_cf(och))
                                if och >= 2:
                                    gpsimd.wait_ge(s_mm.h, qkmm_cf(och - 2))
                                s_cast.inc(gpsimd.tensor_copy(
                                    out=wq16[:, b * 1024:(b + 1) * 1024],
                                    in_=wstg[:, b * 1024:(b + 1) * 1024]))
                                assert s_cast.n == wcst_cf(och)

                            for tt in range(8):
                                emit_store(tt)
                            for och in range(2):
                                emit_cast(och)
                            for tt in range(8, 16):
                                emit_store(tt)
                            for och in range(2, 16):
                                emit_cast(och)

                        @blk.gpsimd
                        def _(gpsimd):
                            for och in range(16, 24):
                                b = och % 2
                                ov = och - 16
                                gpsimd.wait_ge(s_ws[b].h, wld_cf(och))
                                if ov >= 2:
                                    gpsimd.wait_ge(s_mm.h, vmm_cf((ov - 1) * 16 - 1))
                                s_cast.inc(gpsimd.tensor_copy(
                                    out=wv16[:, (ov % 2) * 1024:(ov % 2 + 1) * 1024],
                                    in_=wstg[:, b * 1024:(b + 1) * 1024]))
                                assert s_cast.n == wcst_cf(och)

                        @blk.gpsimd
                        def _(gpsimd):
                            gpsimd.wait_ge(s_xts[0].h, s_xts[0].n)
                            gpsimd.wait_ge(s_xts[1].h, s_xts[1].n)
                            for ich in range(8):
                                s_sts[ich % 2].inc(gpsimd.dma_start(
                                    out=xn_sc[ich, :, s * S:(s + 1) * S],
                                    in_=big_act[:, ich * S:(ich + 1) * S]), 16)
                                if "xn" in debug:
                                    s_sts[ich % 2].inc(gpsimd.dma_start(
                                        out=dbg["xn"][ich, :, s * S:(s + 1) * S],
                                        in_=big_act[:, ich * S:(ich + 1) * S]), 16)


                        @blk.tensor
                        def _(tensor):
                            for och in range(16):
                                b = och % 2
                                for tb in range(4):
                                    g = och * 4 + tb
                                    pb = g % 2
                                    if och == 0:
                                        half = tb // 2
                                        tensor.wait_ge(s_xts[half].h,
                                                       xts0[half] + 128 * (half + 1) - 128 * half)
                                    if tb == 0:
                                        tensor.wait_ge(s_cast.h, wcst_cf(och))
                                    if g >= 2:
                                        tensor.wait_ge(s_ev.h, qkev_cf(g - 2))
                                    for ich in range(8):
                                        inst = tensor.matmul(
                                            psqk[:, pb * 512:(pb + 1) * 512],
                                            wq16[:, b * 1024 + ich * 128:b * 1024 + (ich + 1) * 128],
                                            big_act[:, ich * S + tb * 512:ich * S + (tb + 1) * 512],
                                            start=(ich == 0), stop=(ich == 7))
                                    s_mm.inc(inst)
                                    assert s_mm.n == mmg_cf(g)
                            for och in range(16):
                                c = och % 8
                                tensor.wait_ge(s_act.h, sqa_cf(och))
                                if och == 8:
                                    tensor.wait_ge(s_dve.h, dveB + 1)
                                for qb in range(4):
                                    inst = tensor.matmul(
                                        pssq[:, qb * 512:(qb + 1) * 512],
                                        ind8[:, c * 16:c * 16 + 16],
                                        qsq[:, qb * 512:(qb + 1) * 512],
                                        start=(c == 0), stop=(c == 7))
                                s_mm.inc(inst)
                                assert s_mm.n == sqmm_cf(och)
                            for ov in range(8):
                                tensor.wait_ge(s_cast.h, wcst_cf(16 + ov))
                                for tt in range(16):
                                    g2 = ov * 16 + tt
                                    pb = g2 % 2
                                    if g2 >= 2:
                                        tensor.wait_ge(s_dve.h, vev_cf(g2 - 2))
                                    for ich in range(8):
                                        inst = tensor.matmul(
                                            psv[:, pb * 512:pb * 512 + 128],
                                            big_act[:, ich * S + tt * 128:ich * S + (tt + 1) * 128],
                                            wv16[:, (ov % 2) * 1024 + ich * 128:
                                                 (ov % 2) * 1024 + (ich + 1) * 128],
                                            start=(ich == 0), stop=(ich == 7))
                                    s_mm.inc(inst)
                                    assert s_mm.n == vmm_cf(g2)

                        @blk.scalar
                        def _(scalar):
                            for och in range(16):
                                is_q = och < 8
                                c = och % 8
                                dst = q_fm if is_q else k_fm
                                for tb in range(4):
                                    g = och * 4 + tb
                                    pb = g % 2
                                    scalar.wait_ge(s_mm.h, mmg_cf(g))
                                    s_ev.inc(scalar.activation(
                                        out=dst[:, c * S + tb * 512:c * S + (tb + 1) * 512],
                                        in_=psqk[:, pb * 512:(pb + 1) * 512],
                                        func=AF.Identity, bias=bq8[:, och:och + 1],
                                        scale=0.125 if is_q else 1.0))
                                    assert s_ev.n == qkev_cf(g)
                            for och in range(16):
                                c = och % 8
                                src_t = q_fm if och < 8 else k_fm
                                scalar.wait_ge(s_ev.h, qkevt_cf(och))
                                if och >= 1:
                                    scalar.wait_ge(s_mm.h, sqmm_cf(och - 1))
                                s_act.inc(scalar.activation(out=qsq[:, :],
                                                            in_=src_t[:, c * S:(c + 1) * S],
                                                            func=AF.Square, bias=0.0, scale=1.0))
                                assert s_act.n == sqa_cf(och)

                        @blk.vector
                        def _(vector):
                            vector.wait_ge(s_mm.h, sqmm_cf(7))
                            for qb in range(4):
                                inst = vector.tensor_copy(
                                    out=sumsq[0:16, qb * 512:(qb + 1) * 512],
                                    in_=pssq[0:16, qb * 512:(qb + 1) * 512])
                            s_dve.inc(inst)
                            vector.wait_ge(s_mm.h, sqmm_cf(15))
                            for qb in range(4):
                                inst = vector.tensor_copy(
                                    out=sumsq[0:16, S + qb * 512:S + (qb + 1) * 512],
                                    in_=pssq[0:16, qb * 512:(qb + 1) * 512])
                            s_dve.inc(inst)
                            for ov in range(8):
                                for tt in range(16):
                                    g2 = ov * 16 + tt
                                    pb = g2 % 2
                                    vector.wait_ge(s_mm.h, vmm_cf(g2))
                                    for j in range(2):
                                        h = 2 * ov + j
                                        inst = vector.tensor_tensor(
                                            out=vplus[:, tt * NH * 65 + h * 65:
                                                      tt * NH * 65 + h * 65 + 64],
                                            in0=psv[:, pb * 512 + j * 64:pb * 512 + (j + 1) * 64],
                                            in1=bv_bc[:, ov * 128 + j * 64:ov * 128 + (j + 1) * 64],
                                            op=ALU.add)
                                    s_dve.inc(inst)
                                    assert s_dve.n == vev_cf(g2)

                        @blk.gpsimd
                        def _(gpsimd):
                            gpsimd.wait_ge(s_dve.h, s_dve.n)
                            for tt in range(16):
                                ap3 = vplus[:, tt * NH * 65:(tt + 1) * NH * 65].rearrange(
                                    "p (h c) -> p h c", c=65)[:, :, 64:65]
                                inst = gpsimd.memset(ap3, 1.0)
                            s_pool.inc(inst)

                        if "qkv" in debug:
                            @blk.sync
                            def _(sync):
                                sync.wait_ge(s_ev.h, s_ev.n)
                                sync.wait_ge(s_dve.h, s_dve.n)
                                sync.wait_ge(s_pool.h, s_pool.n)
                                for ich in range(8):
                                    s_prep.inc(sync.dma_start(out=dbg["q"][ich, :, :],
                                                              in_=q_fm[:, ich * S:(ich + 1) * S]), 16)
                                    s_prep.inc(sync.dma_start(out=dbg["k"][ich, :, :],
                                                              in_=k_fm[:, ich * S:(ich + 1) * S]), 16)
                                for tt in range(16):
                                    s_prep.inc(sync.dma_start(
                                        out=dbg["v"][tt, :, :],
                                        in_=vplus[:, tt * NH * 65:(tt + 1) * NH * 65]), 16)
                                s_prep.inc(sync.dma_start(out=dbg["ssq"][:, :], in_=sumsq[0:16, :]), 16)
                                sync.wait_ge(s_prep.h, s_prep.n)

            es_pr = ExitStack()
            if "proj" in phases:
                wostg = es_pr.enter_context(
                    nc.sbuf_tensor(f"wostg_{s}", [128, D], F32))
                wo16 = es_pr.enter_context(
                    nc.sbuf_tensor(f"wo16_{s}", [128, 7 * D], BF16))
                wo_c0 = s_cast.n
                woc = [wo_c0 + i + 1 for i in range(7)]
                wol = []
            if "attn" in phases:
                with nc.Block() as blk, (
                    nc.sbuf_tensor(f"probsT_{s}", [128, 2 * 16 * 512], BF16)) as probsT, (
                    nc.sbuf_tensor(f"nch_{s}", [128, 16], F32)) as nch, (
                    nc.sbuf_tensor(f"nch2_{s}", [128, 8], F32)) as nch2, (
                    nc.sbuf_tensor(f"vraw_{s}", [128, 2 * 512], BF16)) as vraw, (
                    nc.psum_tensor(f"sumr_{s}", [65, 512], F32)) as sumr, (
                    nc.sbuf_tensor(f"rsum_{s}", [65, 2 * 512], F32)) as rsum, (
                    nc.sbuf_tensor(f"cfl_{s}", [1, 16], F32)) as cfl, (
                    nc.sbuf_tensor(f"mk_{s}", [16, 4], F32)) as mkt, (
                    nc.psum_tensor(f"psT_{s}", [128, 4 * 512], F32)) as psT, (
                    nc.psum_tensor(f"psval_{s}", [128, 2 * 512], F32)) as psval, (
                    nc.psum_tensor(f"psibc_{s}", [128, 512], F32)) as psibc:

                    @blk.vector
                    def _(vector):
                        wait_stores_all(vector)
                        vector.wait_ge(s_pool.h, s_pool.n)
                        s_dve.inc(vector.memset(sumr[:, :], 1.0))
                        s_dve.inc(vector.reduce_max(out=mkt[:, 0:1], in_=sumsq[0:16, 0:S],
                                                    axis=AX.X))
                        s_dve.inc(vector.reduce_max(out=mkt[:, 1:2], in_=sumsq[0:16, S:2 * S],
                                                    axis=AX.X))
                        vector.wait_ge(s_dve.h, s_dve.n)
                        s_dve.inc(vector.tensor_tensor(out=mkt[:, 2:3], in0=mkt[:, 0:1],
                                                       in1=mkt[:, 1:2], op=ALU.mult))
                    cprod = s_dve.n

                    @blk.scalar
                    def _(scalar, cprod=cprod):
                        scalar.wait_ge(s_dve.h, cprod)
                        s_act.inc(scalar.activation(out=mkt[:, 3:4], in_=mkt[:, 2:3],
                                                    func=AF.Sqrt, bias=0.0, scale=1.0))
                    c16 = s_act.n

                    @blk.sync
                    def _(sync, c16=c16):
                        sync.wait_ge(s_act.h, c16)
                        s_crs[0].inc(sync.dma_start(out=csc_h[s, :], in_=mkt[0:16, 3:4]), 16)
                    c_st = s_crs[0].n

                    @blk.sync
                    def _(sync, c_st=c_st):
                        sync.wait_ge(s_crs[0].h, c_st)
                        s_crs[1].inc(sync.dma_start(out=cfl[:, :], in_=csc_h[s, :]), 16)
                    c_ld = s_crs[1].n

                    if "proj" in phases:
                        @blk.sync
                        def _(sync):
                            for ich in range(7):
                                if ich >= 1:
                                    sync.wait_ge(s_cast.h, woc[ich - 1])
                                s_ws[0].inc(sync.dma_start(
                                    out=wostg[:, :],
                                    in_=wout_h[ich * 128:(ich + 1) * 128, :]), 16)
                                wol.append(s_ws[0].n)

                    @blk.tensor
                    def _(tensor, c_ld=c_ld):
                        tensor.wait_ge(s_crs[1].h, c_ld)
                        s_mm.inc(tensor.matmul(psibc[:, 0:16], negones[:, :], cfl[:, :],
                                               start=True, stop=True))
                    nch_mm = s_mm.n

                    @blk.vector
                    def _(vector, nch_mm=nch_mm):
                        vector.wait_ge(s_mm.h, nch_mm)
                        s_dve.inc(vector.tensor_copy(out=nch[:, :], in_=psibc[:, 0:16]))
                        vector.wait_ge(s_dve.h, s_dve.n)
                        for p8 in range(8):
                            s_dve.inc(vector.tensor_tensor(
                                out=nch2[:, p8:p8 + 1], in0=nch[:, 2 * p8:2 * p8 + 1],
                                in1=nch[:, 2 * p8 + 1:2 * p8 + 2], op=ALU.min))
                    nch_ok = s_dve.n

                    actC = s_act.n
                    exp_done = {}
                    _cnt = actC
                    _kg = 0
                    for _qb in range(4):
                        for _p in range(8):
                            for _kb in range(4 * _qb + 4):
                                if _kb < 4 * _qb:
                                    _cnt += 1
                                    exp_done[_kg] = _cnt
                                    exp_done[_kg + 1] = _cnt
                                else:
                                    _cnt += 1
                                    exp_done[_kg] = _cnt
                                    _cnt += 1
                                    exp_done[_kg + 1] = _cnt
                                _kg += 2
                    kg0_of = {}
                    exp_k = {}
                    val_k = {}
                    mm_k = {}
                    msk_k = {}
                    val_ev = {}
                    vraw_d = {}
                    rec_d = {}
                    sumbc = {}
                    pzs = {}
                    kgg = 0
                    NKB = lambda i: 4 * (i // 16) + 4

                    def emit_vraw_pair(pr):
                        @blk.vector
                        def _(vector, pr=pr):
                            iB = 2 * pr + 1
                            vs = pr % 2
                            vector.wait_ge(s_mm.h, val_k[(iB, NKB(iB) - 1)])
                            if pr >= 1:
                                vector.wait_ge(s_dve.h, rec_d[pr - 1])
                            if pr >= 2:
                                vector.wait_ge(s_dve.h, val_ev[pr - 2])

                            vector.tensor_copy(out=sumr[0:1, :],
                                               in_=psval[64:65, 0:512])
                            vector.tensor_copy(out=sumr[64:65, :],
                                               in_=psval[64:65, 512:1024])
                            vector.tensor_copy(
                                out=vraw[0:64, vs * 512:(vs + 1) * 512],
                                in_=psval[0:64, 0:512])
                            s_dve.inc(vector.tensor_copy(
                                out=vraw[64:128, vs * 512:(vs + 1) * 512],
                                in_=psval[0:64, 512:1024]))
                            vraw_d[pr] = s_dve.n
                            if pr >= 2:
                                vector.wait_ge(s_mm.h, sumbc[pr - 2])
                            vector.wait_ge(s_dve.h, vraw_d[pr])
                            s_dve.inc(vector.reciprocal(
                                out=rsum[0:65, (pr % 2) * 512:(pr % 2 + 1) * 512],
                                in_=sumr[0:65, :]))
                            rec_d[pr] = s_dve.n

                    def emit_tail_pair(pr):
                        @blk.tensor
                        def _(tensor, pr=pr):
                            tensor.wait_ge(s_dve.h, rec_d[pr])
                            tensor.wait_ge(s_dve.h,
                                           val_ev[pr - 1] if pr >= 1 else nch_ok)
                            s_mm.inc(tensor.matmul(
                                psibc[:, 0:512], sel2[:, :],
                                rsum[0:65, (pr % 2) * 512:(pr % 2 + 1) * 512],
                                start=True, stop=True))
                            sumbc[pr] = s_mm.n

                        @blk.vector
                        def _(vector, pr=pr):
                            vector.wait_ge(s_mm.h, sumbc[pr])
                            vs = pr % 2
                            qb_t = pr // 8
                            ph = pr % 8
                            s_dve.inc(vector.tensor_tensor(
                                out=big_act[:, ph * S + qb_t * 512:
                                            ph * S + (qb_t + 1) * 512],
                                in0=vraw[:, vs * 512:(vs + 1) * 512],
                                in1=psibc[:, 0:512], op=ALU.mult))
                            val_ev[pr] = s_dve.n

                    def emit_vals(tensor, idx_v, j):
                        vb = idx_v % 2
                        nkbv = NKB(idx_v)
                        if j == 0 and idx_v >= 2:
                            tensor.wait_ge(s_dve.h, vraw_d[idx_v // 2 - 1])
                        h_v = idx_v % 16
                        tensor.wait_ge(s_act.h, exp_done[kg0_of[idx_v] + 2 * j + (idx_v % 2)])
                        s_mm.inc(tensor.matmul(
                            psval[0:65, vb * 512:(vb + 1) * 512],
                            vplus[:, j * NH * 65 + h_v * 65:j * NH * 65 + (h_v + 1) * 65],
                            probsT[:, j * 1024 + vb * 512:j * 1024 + (vb + 1) * 512],
                            start=(j == 0), stop=(j == nkbv - 1)))
                        val_k[(idx_v, j)] = s_mm.n

                    for qb in range(4):
                        nkb = 4 * qb + 4

                        @blk.gpsimd
                        def _(gpsimd, qb=qb):
                            gpsimd.wait_ge(s_mm.h, s_mm.n)
                            inst = None
                            for t in range(2):
                                for i in range(4):
                                    for j in range(i):
                                        kbz = 4 * qb + i
                                        inst = gpsimd.memset(
                                            probsT[:, kbz * 1024 + t * 512 + j * 128:
                                                   kbz * 1024 + t * 512 + (j + 1) * 128],
                                            0.0)
                            s_pool.inc(inst)
                        pzs[qb] = s_pool.n

                        if qb == 0 and "proj" in phases:
                            @blk.gpsimd
                            def _(gpsimd):
                                for ich in range(7):
                                    gpsimd.wait_ge(s_ws[0].h, wol[ich])
                                    s_cast.inc(gpsimd.tensor_copy(
                                        out=wo16[:, ich * D:(ich + 1) * D],
                                        in_=wostg[:, :]))
                                    assert s_cast.n == woc[ich]

                        for p in range(8):
                            iA = qb * 16 + 2 * p
                            iB = iA + 1
                            jA, jB = iA - 2, iB - 2
                            kg0 = kgg
                            kg0_of[iA] = kg0
                            kg0_of[iB] = kg0

                            @blk.tensor
                            def _(tensor, qb=qb, iA=iA, iB=iB, jA=jA, jB=jB,
                                  nkb=nkb, kg0=kg0):
                                for j in range(nkb):
                                    if jA >= 0 and j < NKB(jA):
                                        emit_vals(tensor, jA, j)
                                    if jB >= 0 and j < NKB(jB):
                                        emit_vals(tensor, jB, j)
                                    if kg0 + 2 * j + 1 >= 4:
                                        tensor.wait_ge(s_act.h, exp_done[kg0 + 2 * j - 3])
                                    for t, ii in ((0, iA), (1, iB)):
                                        kg = kg0 + 2 * j + t
                                        slot = kg % 4
                                        base = t * 64
                                        ch = (ii % 16) // 2
                                        s_mm.inc(tensor.matmul(
                                            psT[:, slot * 512:(slot + 1) * 512],
                                            k_fm[base:base + 64,
                                                 ch * S + j * 128:ch * S + (j + 1) * 128],
                                            q_fm[base:base + 64,
                                                 ch * S + qb * 512:ch * S + (qb + 1) * 512],
                                            start=True, stop=True))
                                        mm_k[(ii, j)] = s_mm.n

                            @blk.vector
                            def _(vector, qb=qb, iA=iA, iB=iB, nkb=nkb, kg0=kg0):
                                for kb in range(4 * qb, nkb):
                                    i_diag = kb - 4 * qb
                                    for t, ii in ((0, iA), (1, iB)):
                                        slot = (kg0 + 2 * kb + t) % 4
                                        vector.wait_ge(s_mm.h, mm_k[(ii, kb)])
                                        s_dve.inc(vector.tensor_tensor(
                                            out=psT[:, slot * 512 + i_diag * 128:
                                                    slot * 512 + (i_diag + 1) * 128],
                                            in0=psT[:, slot * 512 + i_diag * 128:
                                                    slot * 512 + (i_diag + 1) * 128],
                                            in1=maskT[:, :], op=ALU.add))
                                        msk_k[(ii, kb)] = s_dve.n

                            kgg += 2 * nkb
                            pr_cur = iA // 2
                            if pr_cur >= 1:
                                emit_vraw_pair(pr_cur - 1)
                            if pr_cur >= 2:
                                emit_tail_pair(pr_cur - 2)

                        @blk.scalar
                        def _(scalar, qb=qb, nkb=nkb):
                            scalar.wait_ge(s_pool.h, pzs[qb])
                            scalar.wait_ge(s_dve.h, nch_ok)
                            for p in range(8):
                                iA = qb * 16 + 2 * p
                                iB = iA + 1
                                kg0 = kg0_of[iA]
                                for kb in range(nkb):
                                    i_diag = kb - 4 * qb
                                    kgA = kg0 + 2 * kb
                                    if i_diag < 0:
                                        slotA = kgA % 4
                                        scalar.wait_ge(s_mm.h, mm_k[(iB, kb)])
                                        if iB >= 2 and kb < NKB(iB - 2):
                                            scalar.wait_ge(s_mm.h, val_k[(iB - 2, kb)])
                                        s_act.inc(scalar.activation(
                                            out=probsT[:, kb * 1024:(kb + 1) * 1024],
                                            in_=psT[:, slotA * 512:(slotA + 2) * 512],
                                            func=AF.Exp, bias=nch2[:, p:p + 1], scale=1.0))
                                        assert s_act.n == exp_done[kgA] == exp_done[kgA + 1]
                                    else:
                                        valid0 = i_diag * 128
                                        for t, ii in ((0, iA), (1, iB)):
                                            kg = kgA + t
                                            slot = kg % 4
                                            scalar.wait_ge(s_dve.h, msk_k[(ii, kb)])
                                            if ii >= 2 and kb < NKB(ii - 2):
                                                scalar.wait_ge(s_mm.h, val_k[(ii - 2, kb)])
                                            s_act.inc(scalar.activation(
                                                out=probsT[:, kb * 1024 + t * 512 + valid0:
                                                           kb * 1024 + (t + 1) * 512],
                                                in_=psT[:, slot * 512 + valid0:(slot + 1) * 512],
                                                func=AF.Exp, bias=nch2[:, p:p + 1], scale=1.0))
                                            assert s_act.n == exp_done[kg]

                    @blk.tensor
                    def _(tensor):
                        for j in range(NKB(62)):
                            emit_vals(tensor, 62, j)
                            emit_vals(tensor, 63, j)
                    emit_vraw_pair(31)
                    emit_tail_pair(30)
                    emit_tail_pair(31)

                    if "vals" in debug:
                        @blk.sync
                        def _(sync):
                            sync.wait_ge(s_dve.h, s_dve.n)
                            for ich in range(8):
                                s_prep.inc(sync.dma_start(out=dbg["vals"][ich, :, :],
                                                          in_=big_act[:, ich * S:(ich + 1) * S]), 16)
                            sync.wait_ge(s_prep.h, s_prep.n)

            if "proj" in phases:
                with nc.Block() as blk, (
                    nc.sbuf_tensor(f"aout_{s}", [128, 2 * D], F32)) as aout, (
                    nc.sbuf_tensor(f"wostgp_{s}", [128, D], F32)) as wostg_p, (
                    nc.sbuf_tensor(f"wo16b_{s}", [128, D], BF16)) as wo16b, (
                    nc.psum_tensor(f"pspr_{s}", [128, 2 * 512], F32)) as pspr:

                    @blk.sync
                    def _(sync):
                        s_ws[1].inc(sync.dma_start(
                            out=wostg_p[:, :],
                            in_=wout_h[7 * 128:8 * 128, :]), 16)
                    wp_l = s_ws[1].n

                    @blk.gpsimd
                    def _(gpsimd, wp_l=wp_l):
                        gpsimd.wait_ge(s_ws[1].h, wp_l)
                        s_cast.inc(gpsimd.tensor_copy(out=wo16b[:, :],
                                                      in_=wostg_p[:, :]))
                    wp_c = s_cast.n

                    pr_ev = []
                    st_done = []
                    for tt in range(16):
                        for oh in range(2):
                            g = tt * 2 + oh
                            pb = g % 2

                            @blk.tensor
                            def _(tensor, tt=tt, oh=oh, g=g, pb=pb):
                                if g == 0:
                                    tensor.wait_ge(s_cast.h, wp_c)
                                if g >= 2:
                                    tensor.wait_ge(s_dve.h, pr_ev[g - 2])
                                for ich in range(8):
                                    wsl = (wo16[:, ich * D + oh * 512:ich * D + (oh + 1) * 512]
                                           if ich < 7 else
                                           wo16b[:, oh * 512:(oh + 1) * 512])
                                    inst = tensor.matmul(
                                        pspr[:, pb * 512:(pb + 1) * 512],
                                        big_act[:, ich * S + tt * 128:ich * S + (tt + 1) * 128],
                                        wsl,
                                        start=(ich == 0), stop=(ich == 7))
                                s_mm.inc(inst)
                            prm = s_mm.n

                            @blk.vector
                            def _(vector, tt=tt, oh=oh, pb=pb, prm=prm):
                                vector.wait_ge(s_mm.h, prm)
                                if tt >= 2:
                                    vector.wait_ge(s_sts[tt % 2].h, st_done[tt - 2])
                                s_dve.inc(vector.tensor_tensor(
                                    out=aout[:, (tt % 2) * D + oh * 512:(tt % 2) * D + (oh + 1) * 512],
                                    in0=pspr[:, pb * 512:(pb + 1) * 512],
                                    in1=bout_bc[:, oh * 512:(oh + 1) * 512], op=ALU.add))
                            pr_ev.append(s_dve.n)

                        pe = s_dve.n

                        @blk.gpsimd
                        def _(gpsimd, tt=tt, pe=pe):
                            gpsimd.wait_ge(s_dve.h, pe)
                            s_sts[tt % 2].inc(gpsimd.dma_start(
                                out=attn_sc[s, tt * 128:(tt + 1) * 128, :],
                                in_=aout[:, (tt % 2) * D:(tt % 2 + 1) * D]), 16)
                            if "attn" in debug:
                                s_sts[tt % 2].inc(gpsimd.dma_start(
                                    out=dbg["attn"][s, tt * 128:(tt + 1) * 128, :],
                                    in_=aout[:, (tt % 2) * D:(tt % 2 + 1) * D]), 16)
                        st_done.append(s_sts[tt % 2].n)

                    @blk.gpsimd
                    def _(gpsimd):
                        wait_stores_all(gpsimd)

            es_pr.close()

        p1.close()

        if "w12" in phases:
            with nc.Block() as blk, (
                nc.sbuf_tensor("w12stg", [128, 2 * 2048], F32)) as wstg, (
                nc.sbuf_tensor("w1r", [128, 8 * DH], BF16)) as w1r, (
                nc.sbuf_tensor("w2r", [128, 8 * DH], BF16)) as w2r, (
                nc.sbuf_tensor("xnb", [128, 2 * 8 * 512], BF16)) as xnb, (
                nc.sbuf_tensor("t1b", [128, 4 * 512], F32)) as t1b, (
                nc.sbuf_tensor("ab", [128, 4 * 512], BF16)) as ab, (
                nc.psum_tensor("ps1", [128, 4 * 512], F32)) as ps1, (
                nc.psum_tensor("ps2", [128, 4 * 512], F32)) as ps2:

                ld = []
                cst = []
                cst0 = s_cast.n
                for hh in range(2):
                    for wi, (wh, wr) in enumerate([(w1t_h, w1r), (w2t_h, w2r)]):
                        for ich in range(8):
                            j = hh * 16 + wi * 8 + ich
                            b = j % 2

                            @blk.sync
                            def _(sync, wh=wh, ich=ich, hh=hh, b=b, j=j):
                                if j >= 2:
                                    sync.wait_ge(s_cast.h, cst[j - 2])
                                s_ws[b].inc(sync.dma_start(
                                    out=wstg[:, b * 2048:(b + 1) * 2048],
                                    in_=wh[ich * 128:(ich + 1) * 128,
                                           hh * 2048:(hh + 1) * 2048]), 16)
                            ld.append(s_ws[b].n)

                            @blk.gpsimd
                            def _(gpsimd, wr=wr, ich=ich, hh=hh, b=b, j=j):
                                gpsimd.wait_ge(s_ws[b].h, ld[j])
                                s_cast.inc(gpsimd.tensor_copy(
                                    out=wr[:, ich * DH + hh * 2048:ich * DH + (hh + 1) * 2048],
                                    in_=wstg[:, b * 2048:(b + 1) * 2048]))
                            cst.append(s_cast.n)
                w_ready = s_cast.n

                mmW = s_mm.n
                actW = s_act.n
                dveW = s_dve.n
                stW = [q.n for q in s_sts]
                mst = 2 if "a" in debug else 1
                mm_cf = lambda g: mmW + g + 1
                sln_cf = lambda g: actW + g + 1
                ev_cf = lambda g: dveW + g + 1
                st_cf = lambda g: stW[g % 4] + 16 * mst * (g // 4 + 1)

                w12_mm = []
                for tb in range(T // 512):
                    b = tb % 2

                    @blk.sync
                    def _(sync, tb=tb, b=b):
                        if tb == 0:
                            wait_stores_all(sync)
                        if tb >= 2:
                            sync.wait_ge(s_mm.h, w12_mm[tb - 2])
                        for ich in range(8):
                            s_xs[b].inc(sync.dma_start(
                                out=xnb[:, b * 8 * 512 + ich * 512:b * 8 * 512 + (ich + 1) * 512],
                                in_=xn_sc[ich, :, tb * 512:(tb + 1) * 512]), 16)
                    xl = s_xs[b].n

                    @blk.tensor
                    def _(tensor, tb=tb, b=b, xl=xl):
                        tensor.wait_ge(s_xs[b].h, xl)
                        for och in range(32):
                            g = tb * 32 + och
                            pb = g % 4
                            if tb == 0:
                                tensor.wait_ge(s_cast.h,
                                               cst0 + (16 if och < 16 else 32))
                            if g >= 4:
                                tensor.wait_ge(s_dve.h, ev_cf(g - 4))
                            for ich in range(8):
                                tensor.matmul(
                                    ps1[:, pb * 512:(pb + 1) * 512],
                                    w1r[:, ich * DH + och * 128:ich * DH + (och + 1) * 128],
                                    xnb[:, b * 8 * 512 + ich * 512:b * 8 * 512 + (ich + 1) * 512],
                                    start=(ich == 0), stop=(ich == 7))
                            for ich in range(8):
                                inst = tensor.matmul(
                                    ps2[:, pb * 512:(pb + 1) * 512],
                                    w2r[:, ich * DH + och * 128:ich * DH + (och + 1) * 128],
                                    xnb[:, b * 8 * 512 + ich * 512:b * 8 * 512 + (ich + 1) * 512],
                                    start=(ich == 0), stop=(ich == 7))
                            s_mm.inc(inst)
                            assert s_mm.n == mm_cf(g)
                    w12_mm.append(s_mm.n)

                    @blk.scalar
                    def _(scalar, tb=tb):
                        for och in range(32):
                            g = tb * 32 + och
                            pb = g % 4
                            scalar.wait_ge(s_mm.h, mm_cf(g))
                            if g >= 4:
                                scalar.wait_ge(s_dve.h, ev_cf(g - 4))
                            s_act.inc(scalar.activation(out=t1b[:, pb * 512:(pb + 1) * 512],
                                                        in_=ps1[:, pb * 512:(pb + 1) * 512],
                                                        func=AF.Silu, bias=0.0,
                                                        scale=scl4[:, och:och + 1]))
                            assert s_act.n == sln_cf(g)

                    @blk.vector
                    def _(vector, tb=tb):
                        for och in range(32):
                            g = tb * 32 + och
                            pb = g % 4
                            vector.wait_ge(s_act.h, sln_cf(g))
                            if g >= 4:
                                vector.wait_ge(s_sts[g % 4].h, st_cf(g - 4))
                            s_dve.inc(vector.scalar_tensor_tensor(
                                out=ab[:, pb * 512:(pb + 1) * 512],
                                in0=ps2[:, pb * 512:(pb + 1) * 512],
                                scalar=scl4[:, 32 + och:33 + och],
                                in1=t1b[:, pb * 512:(pb + 1) * 512],
                                op0=ALU.mult, op1=ALU.mult))
                            assert s_dve.n == ev_cf(g)

                    @blk.gpsimd
                    def _(gpsimd, tb=tb):
                        for och in range(32):
                            g = tb * 32 + och
                            pb = g % 4
                            sb = g % 4
                            gpsimd.wait_ge(s_dve.h, ev_cf(g))
                            s_sts[sb].inc(gpsimd.dma_start(
                                out=a_sc[och, :, tb * 512:(tb + 1) * 512],
                                in_=ab[:, pb * 512:(pb + 1) * 512]), 16)
                            if "a" in debug:
                                s_sts[sb].inc(gpsimd.dma_start(
                                    out=dbg["a"][och, :, tb * 512:(tb + 1) * 512],
                                    in_=ab[:, pb * 512:(pb + 1) * 512]), 16)
                            assert s_sts[sb].n == st_cf(g)

                @blk.gpsimd
                def _(gpsimd):
                    wait_stores_all(gpsimd)

        for li, (src_sc, dst_sc, wh, scol, name) in enumerate([
                (a_sc, g1_sc, h1t_h, 64, "h1"), (g1_sc, g2_sc, h2t_h, 96, "h2")]):
            if name not in phases:
                continue
            max_slabs = max(H_SLAB_PASSES)
            with nc.Block() as blk, (
                nc.sbuf_tensor(f"{name}_wstg", [128, 2 * 1024], F32)) as wstg, (
                nc.sbuf_tensor(f"{name}_wr", [128, 2 * max_slabs * DH], BF16)) as wr, (
                nc.sbuf_tensor(f"{name}_in", [128, 2 * 32 * 512], BF16)) as inb, (
                nc.sbuf_tensor(f"{name}_out", [128, 4 * 512], BF16)) as outb, (
                nc.psum_tensor(f"{name}_ps", [128, 4 * 512], F32)) as ps:

                actH = s_act.n
                stH = [q.n for q in s_sts]
                mst = 2 if (name == "h2" and "g2" in debug) else 1
                ev_cf = lambda g: actH + g + 1
                st_cf = lambda g: stH[g % 4] + 16 * mst * (g // 4 + 1)

                slab0 = 0
                pp_mm = []
                tb_mm = []
                gg = 0
                tbg = 0
                gj = 0
                ld_all = []
                cst_all = []
                cst_by_pass = {}
                pass_off = [0]
                for nslab in H_SLAB_PASSES:
                    pass_off.append(pass_off[-1] + nslab)

                def emit_wload(npass_t, och, sl_t, q4):
                    nonlocal gj
                    wslot_t = (npass_t % 2) * max_slabs * DH
                    b = gj % 2
                    j = gj

                    @blk.sync
                    def _(sync, och=och, q4=q4, b=b, j=j):
                        if j >= 2:
                            sync.wait_ge(s_cast.h, cst_all[j - 2])
                        s_ws[b].inc(sync.dma_start(
                            out=wstg[:, b * 1024:(b + 1) * 1024].rearrange(
                                "p (i o) -> p i o", o=128),
                            in_=wh[och].rearrange("(i p) o -> p i o", p=128)[
                                :, q4 * 8:(q4 + 1) * 8, :]), 16)
                    ld_all.append(s_ws[b].n)

                    @blk.gpsimd
                    def _(gpsimd, sl_t=sl_t, q4=q4, b=b, j=j, npass_t=npass_t,
                          wslot_t=wslot_t):
                        gpsimd.wait_ge(s_ws[b].h, ld_all[j])
                        if npass_t >= 2 and sl_t == 0 and q4 == 0:
                            gpsimd.wait_ge(s_mm.h, pp_mm[npass_t - 2])
                        s_cast.inc(gpsimd.tensor_copy(
                            out=wr[:, wslot_t + sl_t * DH + q4 * 1024:
                                    wslot_t + sl_t * DH + (q4 + 1) * 1024],
                            in_=wstg[:, b * 1024:(b + 1) * 1024]))
                    cst_all.append(s_cast.n)
                    cst_by_pass.setdefault(npass_t, []).append(s_cast.n)
                    gj += 1

                for npass, nslab in enumerate(H_SLAB_PASSES):
                    wslot = (npass % 2) * max_slabs * DH
                    if npass == 0:
                        for sl in range(nslab):
                            for q4 in range(4):
                                emit_wload(0, slab0 + sl, sl, q4)
                    nxt = []
                    if npass + 1 < len(H_SLAB_PASSES):
                        for sl in range(H_SLAB_PASSES[npass + 1]):
                            for q4 in range(4):
                                nxt.append((npass + 1, pass_off[npass + 1] + sl, sl, q4))
                    per_tb = -(-len(nxt) // (T // 512)) if nxt else 0

                    for tb in range(T // 512):
                        b = tb % 2

                        @blk.sync
                        def _(sync, tb=tb, b=b, npass=npass, tbg=tbg):
                            if tb == 0 and npass == 0:
                                wait_stores_all(sync)
                            if tbg >= 2:
                                sync.wait_ge(s_mm.h, tb_mm[tbg - 2])
                            for i8 in range(4):
                                s_xs[b].inc(sync.dma_start(
                                    out=inb[:, b * 32 * 512 + i8 * 8 * 512:
                                            b * 32 * 512 + (i8 + 1) * 8 * 512].rearrange(
                                                "p (c t) -> p c t", t=512),
                                    in_=src_sc.rearrange("c p t -> p c t")[
                                        :, i8 * 8:(i8 + 1) * 8, tb * 512:(tb + 1) * 512]), 16)
                        inl = s_xs[b].n

                        for _k in range(per_tb):
                            if nxt:
                                emit_wload(*nxt.pop(0))

                        @blk.tensor
                        def _(tensor, b=b, inl=inl, nslab=nslab, gg0=gg,
                              wslot=wslot, first=(tb == 0),
                              slab_cst=tuple(cst_by_pass.get(npass, []))):
                            tensor.wait_ge(s_xs[b].h, inl)
                            for sl in range(nslab):
                                g = gg0 + sl
                                pb = g % 4
                                if first:
                                    tensor.wait_ge(s_cast.h, slab_cst[sl * 4 + 3])
                                if g >= 4:
                                    tensor.wait_ge(s_act.h, ev_cf(g - 4))
                                for ich in range(32):
                                    inst = tensor.matmul(
                                        ps[:, pb * 512:(pb + 1) * 512],
                                        wr[:, wslot + sl * DH + ich * 128:
                                           wslot + sl * DH + (ich + 1) * 128],
                                        inb[:, b * 32 * 512 + ich * 512:
                                            b * 32 * 512 + (ich + 1) * 512],
                                        start=(ich == 0), stop=(ich == 31))
                                s_mm.inc(inst)
                        mm_hi = s_mm.n
                        tb_mm.append(mm_hi)

                        @blk.scalar
                        def _(scalar, nslab=nslab, gg0=gg, mm_hi=mm_hi, scol=scol, slab0=slab0):
                            for sl in range(nslab):
                                g = gg0 + sl
                                och = slab0 + sl
                                pb = g % 4
                                scalar.wait_ge(s_mm.h, mm_hi - nslab + sl + 1)
                                if g >= 4:
                                    scalar.wait_ge(s_sts[g % 4].h, st_cf(g - 4))
                                s_act.inc(scalar.activation(
                                    out=outb[:, pb * 512:(pb + 1) * 512],
                                    in_=ps[:, pb * 512:(pb + 1) * 512],
                                    func=AF.Gelu, bias=0.0,
                                    scale=scl4[:, scol + och:scol + och + 1]))
                                assert s_act.n == ev_cf(g)

                        @blk.gpsimd
                        def _(gpsimd, tb=tb, nslab=nslab, gg0=gg, slab0=slab0, name=name,
                              dst_sc=dst_sc):
                            for sl in range(nslab):
                                g = gg0 + sl
                                pb = g % 4
                                sb = g % 4
                                och = slab0 + sl
                                gpsimd.wait_ge(s_act.h, ev_cf(g))
                                if name == "h1":
                                    s_sts[sb].inc(gpsimd.dma_start(
                                        out=dst_sc[och, :, tb * 512:(tb + 1) * 512],
                                        in_=outb[:, pb * 512:(pb + 1) * 512]), 16)
                                else:
                                    s_sts[sb].inc(gpsimd.dma_start(
                                        out=dst_sc[tb * 4:(tb + 1) * 4, och, :, :].rearrange(
                                            "t p o -> p t o"),
                                        in_=outb[:, pb * 512:(pb + 1) * 512].rearrange(
                                            "p (t o) -> p t o", o=128)), 16)
                                    if "g2" in debug:
                                        s_sts[sb].inc(gpsimd.dma_start(
                                            out=dbg["g2"][tb * 4:(tb + 1) * 4, och, :, :].rearrange(
                                                "t p o -> p t o"),
                                            in_=outb[:, pb * 512:(pb + 1) * 512].rearrange(
                                                "p (t o) -> p t o", o=128)), 16)
                                assert s_sts[sb].n == st_cf(g)

                        gg += nslab
                        tbg += 1
                    assert not nxt
                    pp_mm.append(s_mm.n)
                    slab0 += nslab

                @blk.gpsimd
                def _(gpsimd):
                    wait_stores_all(gpsimd)

        if "w3" in phases:
            with nc.Block() as blk, (
                nc.sbuf_tensor("w3stg", [128, 2 * 1024], F32)) as wstg, (
                nc.sbuf_tensor("w3r", [128, 32 * D], BF16)) as w3r, (
                nc.sbuf_tensor("g2b", [128, 2 * 32 * 128], BF16)) as g2b, (
                nc.sbuf_tensor("resb", [128, 2 * 2 * D], F32)) as resb, (
                nc.sbuf_tensor("outb3", [128, 2 * D], F32)) as outb3, (
                nc.psum_tensor("ps3", [128, 4 * 512], F32)) as ps3:

                ld = []
                cst = []
                for ich in range(32):
                    b = ich % 2

                    @blk.sync
                    def _(sync, ich=ich, b=b):
                        if ich >= 2:
                            sync.wait_ge(s_cast.h, cst[ich - 2])
                        s_ws[b].inc(sync.dma_start(out=wstg[:, b * 1024:(b + 1) * 1024],
                                                   in_=w3t_h[ich * 128:(ich + 1) * 128, :]), 16)
                    ld.append(s_ws[b].n)

                    @blk.gpsimd
                    def _(gpsimd, ich=ich, b=b):
                        gpsimd.wait_ge(s_ws[b].h, ld[ich])
                        s_cast.inc(gpsimd.tensor_copy(out=w3r[:, ich * D:(ich + 1) * D],
                                                      in_=wstg[:, b * 1024:(b + 1) * 1024]))
                    cst.append(s_cast.n)
                w3rdy = s_cast.n

                w3tt_mm = []
                w3tt_ev = []
                y_st = []
                g_mm = []
                g_ev = []
                for tt in range(TT):
                    sq = tt // 16
                    ttl = tt % 16
                    b = tt % 2

                    @blk.sync
                    def _(sync, tt=tt, b=b):
                        if tt == 0:
                            wait_stores_all(sync)
                        if tt >= 2:
                            sync.wait_ge(s_mm.h, w3tt_mm[tt - 2])
                        s_xs[b].inc(sync.dma_start(
                            out=g2b[:, b * 32 * 128:(b + 1) * 32 * 128].rearrange(
                                "p (c t) -> p c t", t=128),
                            in_=g2_sc[tt].rearrange("c p t -> p c t")), 16)
                    gl = s_xs[b].n

                    @blk.scalar
                    def _(scalar, tt=tt, sq=sq, ttl=ttl, b=b):
                        if tt == 0:
                            wait_stores_all(scalar)
                        if tt >= 2:
                            scalar.wait_ge(s_dve.h, w3tt_ev[tt - 2])
                        s_crs[b].inc(scalar.dma_start(
                            out=resb[:, b * 2 * D:b * 2 * D + D],
                            in_=x_h[sq, ttl * 128:(ttl + 1) * 128, :]), 16)
                        s_crs[b].inc(scalar.dma_start(
                            out=resb[:, b * 2 * D + D:b * 2 * D + 2 * D],
                            in_=attn_sc[sq, ttl * 128:(ttl + 1) * 128, :]), 16)
                    rl = s_crs[b].n

                    for oh in range(2):
                        n = len(g_mm)
                        pb = b * 2 + oh

                        @blk.tensor
                        def _(tensor, tt=tt, oh=oh, b=b, pb=pb, gl=gl, n=n):
                            tensor.wait_ge(s_xs[b].h, gl)
                            if tt == 0 and oh == 0:
                                tensor.wait_ge(s_cast.h, w3rdy)
                            if n >= 4:
                                tensor.wait_ge(s_dve.h, g_ev[n - 4])
                            for ich in range(32):
                                inst = tensor.matmul(
                                    ps3[:, pb * 512:(pb + 1) * 512],
                                    g2b[:, b * 32 * 128 + ich * 128:b * 32 * 128 + (ich + 1) * 128],
                                    w3r[:, ich * D + oh * 512:ich * D + (oh + 1) * 512],
                                    start=(ich == 0), stop=(ich == 31))
                            s_mm.inc(inst)
                        g_mm.append(s_mm.n)
                        mmn = s_mm.n

                        @blk.vector
                        def _(vector, tt=tt, oh=oh, b=b, pb=pb, mmn=mmn, rl=rl):
                            vector.wait_ge(s_mm.h, mmn)
                            vector.wait_ge(s_crs[b].h, rl)
                            if tt >= 2:
                                vector.wait_ge(s_ys[b].h, y_st[tt - 2])
                            sl = slice(oh * 512, (oh + 1) * 512)
                            dsl = slice(b * D + oh * 512, b * D + (oh + 1) * 512)
                            s_dve.inc(vector.scalar_tensor_tensor(
                                out=outb3[:, dsl], in0=ps3[:, pb * 512:(pb + 1) * 512],
                                scalar=1.0, in1=w3s_bc[:, sl], op0=ALU.bypass, op1=ALU.mult))
                            vector.wait_ge(s_dve.h, s_dve.n)
                            s_dve.inc(vector.tensor_tensor(
                                out=outb3[:, dsl], in0=outb3[:, dsl],
                                in1=resb[:, b * 2 * D + D + oh * 512:b * 2 * D + D + (oh + 1) * 512],
                                op=ALU.add))
                            vector.wait_ge(s_dve.h, s_dve.n)
                            s_dve.inc(vector.tensor_tensor(
                                out=outb3[:, dsl], in0=outb3[:, dsl],
                                in1=resb[:, b * 2 * D + oh * 512:b * 2 * D + (oh + 1) * 512],
                                op=ALU.add))
                        g_ev.append(s_dve.n)
                    w3tt_mm.append(s_mm.n)
                    w3tt_ev.append(s_dve.n)
                    evn = s_dve.n

                    @blk.gpsimd
                    def _(gpsimd, tt=tt, sq=sq, ttl=ttl, b=b, evn=evn):
                        gpsimd.wait_ge(s_dve.h, evn)
                        s_ys[b].inc(gpsimd.dma_start(out=y_h[sq, ttl * 128:(ttl + 1) * 128, :],
                                                     in_=outb3[:, b * D:(b + 1) * D]), 16)
                    y_st.append(s_ys[b].n)

                @blk.gpsimd
                def _(gpsimd):
                    gpsimd.wait_ge(s_ys[0].h, s_ys[0].n)
                    gpsimd.wait_ge(s_ys[1].h, s_ys[1].n)

    return nc, dbg


def host_prep(inputs):
    f = np.ascontiguousarray
    W_qkv = inputs["L0_attn_W_qkv"]
    b_qkv = inputs["L0_attn_b_qkv"]
    d = {
        "wqkv_t": f(W_qkv.reshape(D, 24, 128).transpose(1, 0, 2)),
        "bq_q": f(b_qkv[0:D].reshape(8, 128).T),
        "bq_k": f(b_qkv[D:2 * D].reshape(8, 128).T),
        "bq_v": f(b_qkv[2 * D:3 * D].reshape(1, D)),
        "wout": f(inputs["L0_attn_W_out"]),
        "bout": f(inputs["L0_attn_b_out"].reshape(1, D)),
        "w1t": f(inputs["L0_ffn_w1_w"].T),
        "w2t": f(inputs["L0_ffn_w2_w"].T),
        "w1s_t": f(inputs["L0_ffn_w1_s"].reshape(32, 128).T),
        "w2s_t": f(inputs["L0_ffn_w2_s"].reshape(32, 128).T),
        "h1t": f(inputs["L0_ffn_h1_w"].T.reshape(DH, 32, 128).transpose(1, 0, 2)),
        "h2t": f(inputs["L0_ffn_h2_w"].T.reshape(DH, 32, 128).transpose(1, 0, 2)),
        "h1s_t": f(inputs["L0_ffn_h1_s"].reshape(32, 128).T),
        "h2s_t": f(inputs["L0_ffn_h2_s"].reshape(32, 128).T),
        "w3t": f(inputs["L0_ffn_w3_w"].T),
        "w3s_r": f(inputs["L0_ffn_w3_s"].reshape(1, D)),
        "maskT": np.tril(np.full((128, 128), -10000.0, dtype=np.float32), -1),
        "negones": np.full((1, 128), -1.0, dtype=np.float32),
        "sel2": np.concatenate([np.concatenate([np.ones((1, 64)), np.zeros((1, 64))], 1),
                                np.zeros((63, 128)),
                                np.concatenate([np.zeros((1, 64)), np.ones((1, 64))], 1)],
                               0).astype(np.float32),
    }
    ind = np.zeros((128, 128), dtype=ml_dtypes.bfloat16)
    for c in range(8):
        ind[0:64, c * 16 + 2 * c] = 1.0
        ind[64:128, c * 16 + 2 * c + 1] = 1.0
    d["ind8"] = ind
    return d


def kernel(**inputs):
    x = inputs["L0_attn_x"]
    n_seq = x.shape[0] // NCORES
    shared = host_prep(inputs)
    nc, _ = build_program(n_seq=n_seq)
    in_maps = []
    for c in range(NCORES):
        m = dict(shared)
        m["x"] = np.ascontiguousarray(x[c * n_seq:(c + 1) * n_seq])
        in_maps.append(m)
    res = run_bass_kernel_spmd(nc, in_maps, core_ids=list(range(NCORES)))
    return np.concatenate([r["y"] for r in res.results], axis=0)
